# version 22
# baseline (speedup 1.0000x reference)
import numpy as np
import ml_dtypes
from contextlib import ExitStack

import concourse.bass as bass
import concourse.mybir as mybir
from concourse.bass_utils import run_bass_kernel_spmd

F32 = mybir.dt.float32
BF16 = mybir.dt.bfloat16
I32 = mybir.dt.int32
AF = mybir.ActivationFunctionType
ALU = mybir.AluOpType
AX = mybir.AxisListType

TOK, D, DS, NST, RHO, ER = 4096, 2048, 1024, 64, 128, 512
TC, NCC = 1024, 4
NSUP = 8
MYT = 2048
EPS = 1e-5
SCL = 0.5 * (1.0 / 64.0)
GP_NS = frozenset(n for n in range(NST) if n % 3 != 0)

ENGS = ("sync", "scalar", "vector", "tensor", "gpsimd")
SEMS = ("dL", "dR", "dS", "cV", "cS", "cP", "cG", "ccm",
        "dRB0", "dRB1", "dRC0", "dRC1", "dSy0", "dSy1", "dSa", "dSb",
        "dG0", "dG1", "dG2", "dG3", "dG4", "dG5", "dG6", "dG7", "dX0", "dX1")


class Prog:
    def __init__(self):
        self.ops = {e: [] for e in ENGS}
        self.cnt = {s: 0 for s in SEMS}

    def add(self, eng, fn, inc=None, k=1):
        self.ops[eng].append(("op", fn, inc, k))
        if inc is not None:
            self.cnt[inc] += k
            return self.cnt[inc]
        return None

    def wait(self, eng, sem, val):
        if val:
            self.ops[eng].append(("wait", sem, int(val)))

    def bar(self):
        return dict(self.cnt)

    def wait_bar(self, eng, bar):
        for s in SEMS:
            self.wait(eng, s, bar[s])


def build_nc(debug=False):
    nc = bass.Bass("TRN2", target_bir_lowering=False, debug=False)
    dt = nc.dram_tensor

    x_in = dt("x_in", [TOK, D], F32, kind="ExternalInput")
    xpwT = dt("xpwT", [D, 256], F32, kind="ExternalInput")
    dtwT = dt("dtwT", [RHO, D], F32, kind="ExternalInput")
    dtb_r = dt("dtb_r", [128, 16], F32, kind="ExternalInput")
    alog_r = dt("alog_r", [128, 512], F32, kind="ExternalInput")
    dpar_r = dt("dpar_r", [128, 16], F32, kind="ExternalInput")
    nw_r = dt("nw_r", [128, 16], F32, kind="ExternalInput")
    nb_r = dt("nb_r", [128, 16], F32, kind="ExternalInput")
    woutT = dt("woutT", [D, D], F32, kind="ExternalInput")
    ffnAT = dt("ffnAT", [D, ER], F32, kind="ExternalInput")
    ffnBT = dt("ffnBT", [ER, D], F32, kind="ExternalInput")
    wrT = dt("wrT", [D, 8], F32, kind="ExternalInput")
    br_in = dt("br_in", [1, 8], F32, kind="ExternalInput")
    blkid_in = dt("blkid_in", [8, ER], F32, kind="ExternalInput")
    idf_in = dt("idf_in", [128, 128], F32, kind="ExternalInput")
    idb_in = dt("idb_in", [128, 128], BF16, kind="ExternalInput")
    cst_in = dt("cst_in", [1, 4], F32, kind="ExternalInput")
    jv_in = dt("jv_in", [1, 1], I32, kind="ExternalInput")
    kv_in = dt("kv_in", [1, 1], I32, kind="ExternalInput")

    out_c = dt("out_c", [MYT, D], F32, kind="ExternalOutput")

    kd = "ExternalOutput" if debug else "Internal"
    x1_scr = dt("x1_scr", [MYT, D], F32, kind=kd)
    xlnbT_scr = dt("xlnbT_scr", [D, MYT], BF16, kind=kd)
    xln2T_scr = dt("xln2T_scr", [D, MYT], F32, kind=kd)
    Ph = dt("Ph", [4352, MYT], F32, kind=kd)
    Pgx = dt("Pgx", [8704, MYT], F32, kind="Internal")
    Pbc = dt("Pbc", [4352, MYT], F32, kind="Internal")
    y_scr = dt("y_scr", [DS, TOK], BF16, kind="Internal")
    y_dbg = dt("y_dbg", [DS, TOK], BF16, kind=kd) if debug else None
    ygx_scr = dt("ygx_scr", [2 * DS, TOK], BF16, kind="Internal")
    wo_b_scr = dt("wo_b_scr", [D, D], BF16, kind="Internal")
    fAT_b_scr = dt("fAT_b_scr", [D, ER], BF16, kind="Internal")
    fBT_b_scr = dt("fBT_b_scr", [ER, D], BF16, kind="Internal")
    xpwT_b_scr = dt("xpwT_b_scr", [D, 256], BF16, kind="Internal")
    dtwT_b_scr = dt("dtwT_b_scr", [RHO, D], BF16, kind="Internal")
    wrT_b_scr = dt("wrT_b_scr", [D, 8], BF16, kind="Internal")

    P = Prog()
    cell = {}

    with ExitStack() as st:
        sb = lambda name, shape, d=F32: st.enter_context(nc.sbuf_tensor(name, shape, d))
        s_jv = sb("s_jv", [1, 2], I32)
        idf = sb("idf", [128, 128], F32)
        idb = sb("idb", [128, 128], BF16)
        blkid = sb("blkid", [8, ER], F32)
        br_b = sb("br_b", [128, 8], F32)
        nw = sb("nw", [128, 16], F32)
        nb = sb("nb", [128, 16], F32)
        dtb = sb("dtb", [128, 16], F32)
        dpar = sb("dpar", [128, 16], F32)
        A_sb = sb("A_sb", [128, 512], F32)
        carry = sb("carry", [128, 512], F32)
        ST = sb("ST", [128, 192], F32)
        MT = sb("MT", [8, ER], F32)
        cst = sb("cst", [128, 4], F32)
        wr_sb = sb("wr_sb", [128, 128], BF16)
        dtw_sb = sb("dtw_sb", [128, D], BF16)
        AR1 = sb("AR1", [128, 16384], F32)
        AR2 = sb("AR2", [128, 16384], F32)
        AR3 = sb("AR3", [128, 8192], F32)
        SM0 = sb("SM0", [128, 2048], F32)
        SM1 = sb("SM1", [128, 2048], F32)
        SM2 = sb("SM2", [128, 2048], F32)
        SM3 = sb("SM3", [128, 2048], F32)
        AR1b = AR1[:, :].bitcast(BF16)
        AR2b = AR2[:, :].bitcast(BF16)
        AR3b = AR3[:, :].bitcast(BF16)
        SM0b = SM0[:, :].bitcast(BF16)
        SM2b = SM2[:, :].bitcast(BF16)
        SM3b = SM3[:, :].bitcast(BF16)
        PS = st.enter_context(nc.psum_tensor("PS", [128, 4096], F32))
        PSb = PS[:, :].bitcast(BF16)
        BANK = lambda k, w=512: PS[:, 512 * k:512 * k + w]
        sems = {s: st.enter_context(nc.semaphore(s)) for s in SEMS}
        block = st.enter_context(nc.Block())

        for dst, src in [
            (s_jv[0:1, 0:1], jv_in[:, :]), (s_jv[0:1, 1:2], kv_in[:, :]),
            (idf[:, :], idf_in[:, :]), (idb[:, :], idb_in[:, :]),
            (blkid[:, :], blkid_in[:, :]),
            (br_b[:, :], br_in[0:1, :].broadcast_to([128, 8])),
            (nw[:, :], nw_r[:, :]), (nb[:, :], nb_r[:, :]),
            (dtb[:, :], dtb_r[:, :]), (dpar[:, :], dpar_r[:, :]),
            (A_sb[:, :], alog_r[:, :]),
            (cst[:, :], cst_in[0:1, :].broadcast_to([128, 4])),
        ]:
            P.add("sync", (lambda d_, s_: (lambda q: q.dma_start(out=d_, in_=s_)))(dst, src), "dL", 16)
        v_glob = P.cnt["dL"]

        def _ld_regs(q):
            j = nc.values_load(s_jv[0:1, 0:1], engines=[q.engine], min_val=0,
                               max_val=1, skip_runtime_bounds_check=True)
            k = nc.values_load(s_jv[0:1, 1:2], engines=[q.engine], min_val=0,
                               max_val=1, skip_runtime_bounds_check=True)
            cell["jj"] = j * 1
            cell["kk2"] = k + 2
            cell["tsl"] = [(j * 4 + tp) * 512 for tp in range(4)]
            cell["k8"] = k * 8
            return None
        P.wait("sync", "dL", v_glob)
        P.add("sync", _ld_regs)

        P.wait("scalar", "dL", v_glob)
        vA1 = P.add("scalar", lambda q: q.activation(out=A_sb[:, :], in_=A_sb[:, :], func=AF.Exp), "cS")
        P.wait("vector", "cS", vA1)
        P.add("vector", lambda q: q.tensor_scalar(out=A_sb[:, :], in0=A_sb[:, :], scalar1=-1.0,
                                                  scalar2=None, op0=ALU.mult), "cV")

        jobs = []
        for r in range(16):
            jobs.append((wrT[r * 128:(r + 1) * 128, :], wrT_b_scr[r * 128:(r + 1) * 128, :], 8, None))
        for r in range(16):
            jobs.append((ffnAT[r * 128:(r + 1) * 128, :], fAT_b_scr[r * 128:(r + 1) * 128, :], ER, None))
        for r in range(4):
            jobs.append((ffnBT[r * 128:(r + 1) * 128, :], fBT_b_scr[r * 128:(r + 1) * 128, :], D, None))
        for r in range(16):
            jobs.append((xpwT[r * 128:(r + 1) * 128, :], xpwT_b_scr[r * 128:(r + 1) * 128, :], 256, None))
        jobs.append((dtwT[:, :], dtwT_b_scr[:, :], D, None))
        for r in range(16):
            jobs.append((woutT[r * 128:(r + 1) * 128, :], wo_b_scr[r * 128:(r + 1) * 128, :], D, None))
        stg_in = [SM0, SM1]
        stg_out = [SM2b, SM3b]
        v_st_last = {0: 0, 1: 0}
        for i, (jsrc, dst, cols, wchunk) in enumerate(jobs):
            par = i % 2
            sld = "dRB%d" % par
            sst = "dSa" if par == 0 else "dSb"
            P.wait("sync", sst, v_st_last[par])
            vl = P.add("sync", (lambda s_, c_, p_: (lambda q: q.dma_start(out=stg_in[p_][:, 0:c_], in_=s_)))(jsrc, cols, par), sld, 16)
            P.wait("vector", sld, vl)
            P.wait("vector", sst, v_st_last[par])
            vc = P.add("vector", (lambda c_, p_: (lambda q: q.tensor_copy(out=stg_out[p_][:, 0:c_], in_=stg_in[p_][:, 0:c_])))(cols, par), "cV")
            P.wait("sync", "cV", vc)
            vs = P.add("sync", (lambda d_, c_, p_: (lambda q: q.dma_start(out=d_, in_=stg_out[p_][:, 0:c_])))(dst, cols, par), sst, 16)
            v_st_last[par] = vs
        bar0 = P.bar()

        fA = AR3b[:, 0:8192]
        fB = AR3b[:, 8192:16384]
        pst = {"tr_evac": 0, "router_c": 0, "mscT_c": 0, "maskmm": 0, "msk_ev": 0,
               "hm": 0, "y_ev": 0, "trf": 0, "mt_rd": 0, "v_xlb_2a": [0, 0]}

        def emit_ffn_weights_load():
            vv = 0
            for r in range(16):
                vv = P.add("sync", (lambda r_: (lambda q: q.dma_start(
                    out=fA[:, r_ * 512:(r_ + 1) * 512], in_=fAT_b_scr[r_ * 128:(r_ + 1) * 128, :])))(r), "dL", 16)
            for r in range(4):
                vv = P.add("sync", (lambda r_: (lambda q: q.dma_start(
                    out=fB[:, r_ * 2048:(r_ + 1) * 2048], in_=fBT_b_scr[r_ * 128:(r_ + 1) * 128, :])))(r), "dL", 16)
            vv = P.add("sync", lambda q: q.dma_start(
                out=wr_sb[:, :].rearrange("p (c e) -> p c e", c=16),
                in_=wrT_b_scr[:, :].rearrange("(c p) e -> p c e", p=128)), "dL", 16)
            return vv

        BG = {2: {"cV": 0, "cS": 0}, 3: {"cV": 0, "cS": 0}}

        def bank_wait(k):
            for sem, val in BG[k].items():
                P.wait("tensor", sem, val)

        def emit_ffn_sup(sup, x_tiles, out_dst, fstate, tile_dma=None, v_tiles_ready=0):
            xlnT = AR2b[:, 0:8192]
            msk = AR2[:, 4096:6144]
            hm = AR2b[:, 12288:14336]
            ys = [AR2[:, 7168:7680], AR2[:, 7680:8192]]
            xlb = [SM0b[:, 0:2048], SM0b[:, 2048:4096]]
            sq = SM2
            v10_par = pst.setdefault("v10_par", [0, 0])
            v15_par = pst.setdefault("v15_par", [0, 0])
            vxh = [0, 0]
            if tile_dma is not None:
                for hf in range(2):
                    for sem, val in fstate.get("tile_free_h", [[], []])[hf]:
                        P.wait("sync", sem, val)
                    P.wait("sync", "dS", fstate.get("st_tile", 0))
                    vxh[hf] = P.add("sync", (lambda s_, hf_: (lambda q: tile_dma(q, s_, hf_)))(sup, hf), "dX%d" % hf, 16)
            for tt in range(4):
                pp = tt % 2
                base = 64 * pp
                col = lambda k: ST[:, base + k:base + k + 1]
                vx = vxh[tt // 2] if tile_dma is not None else 0
                sx = "dX%d" % (tt // 2)
                xt = x_tiles[tt]
                P.wait("vector", "cS", v10_par[pp])
                P.wait("vector", "cP", v15_par[pp])
                if v_tiles_ready and tile_dma is None:
                    P.wait("vector", "cV", v_tiles_ready)
                if vx:
                    P.wait("vector", sx, vx)
                v1 = P.add("vector", (lambda xt_, c_: (lambda q: q.reduce_sum(c_, xt_, axis=AX.X)))(xt, col(0)), "cV")
                if vx:
                    P.wait("scalar", sx, vx)
                if v_tiles_ready and tile_dma is None:
                    P.wait("scalar", "cV", v_tiles_ready)
                    P.wait("tensor", "cV", v_tiles_ready)
                v2 = P.add("scalar", (lambda xt_, c_: (lambda q: q.activation(out=sq[:, :], in_=xt_, func=AF.Square,
                                                                              accum_out=c_)))(xt, col(1)), "cS")
                P.wait("vector", "cS", v2)
                P.wait("vector", "cV", v1)
                vv = P.add("vector", (lambda a, b: (lambda q: q.tensor_scalar(out=a, in0=b, scalar1=1.0 / D, scalar2=None, op0=ALU.mult)))(col(2), col(0)), "cV")
                vv = P.add("vector", (lambda a, b: (lambda q: q.tensor_scalar(out=a, in0=b, scalar1=1.0 / D, scalar2=None, op0=ALU.mult)))(col(3), col(1)), "cV")
                P.wait("vector", "cV", vv)
                vv = P.add("vector", (lambda a, b: (lambda q: q.tensor_tensor(out=a, in0=b, in1=b, op=ALU.mult)))(col(4), col(2)), "cV")
                P.wait("vector", "cV", vv)
                v3 = P.add("vector", (lambda a, b, c: (lambda q: q.tensor_tensor(out=a, in0=b, in1=c, op=ALU.subtract)))(col(5), col(3), col(4)), "cV")
                P.wait("scalar", "cV", v3)
                vl = P.add("scalar", (lambda a, b: (lambda q: q.activation(out=a, in_=b, func=AF.Ln, bias=cst[:, 0:1])))(col(6), col(5)), "cS")
                P.wait("scalar", "cS", vl)
                v4 = P.add("scalar", (lambda a, b: (lambda q: q.activation(out=a, in_=b, func=AF.Exp, scale=cst[:, 1:2])))(col(7), col(6)), "cS")
                P.wait("vector", "cS", v4)
                v5 = P.add("vector", (lambda a, b, c: (lambda q: q.tensor_scalar(out=a, in0=b, scalar1=c,
                                                                                 scalar2=-1.0, op0=ALU.mult, op1=ALU.mult)))(col(8), col(2), col(7)), "cV")
                P.wait("scalar", "cV", v5)
                P.wait("scalar", "cP", pst["v_xlb"][pp])
                v6 = P.add("scalar", (lambda xt_, p_, c7, c8: (lambda q: q.activation(out=xlb[p_][:, :], in_=xt_, func=AF.Identity,
                                                                                      scale=c7, bias=c8)))(xt, pp, col(7), col(8)), "cS")
                for g in range(4):
                    P.wait("tensor", "cS", v6)
                    P.wait("tensor", "cV", pst["tr_evac"])
                    for c4 in range(4):
                        c = g * 4 + c4
                        P.add("tensor", (lambda c_, c4_, p_, g_: (lambda q: q.transpose(
                            PSb[:, (g_ % 2) * 1024 + c4_ * 128:(g_ % 2) * 1024 + (c4_ + 1) * 128],
                            xlb[p_][:, c_ * 128:(c_ + 1) * 128], idb[:, :])))(c, c4, pp, g), "cP")
                    vp = P.cnt["cP"]
                    P.wait("vector", "cP", vp)
                    pst["tr_evac"] = P.add("vector", (lambda g_, tt_: (lambda q: q.tensor_copy(
                        out=xlnT[:, :].rearrange("p (c t) -> p c t", c=16)[:, g_ * 4:(g_ + 1) * 4, tt_ * 128:(tt_ + 1) * 128],
                        in_=PSb[:, (g_ % 2) * 1024:(g_ % 2) * 1024 + 512].rearrange("p (c t) -> p c t", c=4))))(g, tt), "cV")
                pst["v_xlb"][pp] = P.cnt["cP"]
                P.wait("tensor", "cV", pst["tr_evac"])
                bank_wait(2)
                for c in range(16):
                    P.add("tensor", (lambda c_, tt_: (lambda q: q.matmul(
                        PS[:, 1024:1032], xlnT[:, c_ * 512 + tt_ * 128: c_ * 512 + (tt_ + 1) * 128],
                        wr_sb[:, c_ * 8:(c_ + 1) * 8], start=(c_ == 0), stop=(c_ == 15))))(c, tt), "cP")
                vr = P.cnt["cP"]
                P.wait("vector", "cP", vr)
                vrc = P.add("vector", (lambda c16: (lambda q: q.tensor_tensor(out=ST[:, c16:c16 + 8], in0=br_b[:, :],
                                                                              in1=PS[:, 1024:1032], op=ALU.add)))(base + 16), "cV")
                BG[2]["cV"] = vrc
                P.wait("vector", "cV", vrc)
                v8 = P.add("vector", (lambda a, c16: (lambda q: q.reduce_max(a, ST[:, c16:c16 + 8], axis=AX.X)))(col(9), base + 16), "cV")
                P.wait("vector", "cV", v8)
                v9 = P.add("vector", (lambda a, b: (lambda q: q.tensor_scalar(out=a, in0=b, scalar1=-1.0, scalar2=None, op0=ALU.mult)))(col(10), col(9)), "cV")
                P.wait("scalar", "cV", v9)
                v10 = P.add("scalar", (lambda c16, c24, c10: (lambda q: q.activation(out=ST[:, c24:c24 + 8], in_=ST[:, c16:c16 + 8],
                                                                                     func=AF.Exp, bias=c10)))(base + 16, base + 24, col(10)), "cS")
                v10_par[pp] = v10
                P.wait("vector", "cS", v10)
                v11 = P.add("vector", (lambda a, c24: (lambda q: q.reduce_sum(a, ST[:, c24:c24 + 8], axis=AX.X)))(col(11), base + 24), "cV")
                P.wait("vector", "cV", v11)
                vr2 = P.add("vector", (lambda a, b: (lambda q: q.reciprocal(a, b)))(col(12), col(11)), "cV")
                P.wait("vector", "cV", vr2)
                v12 = P.add("vector", (lambda a, b: (lambda q: q.tensor_scalar(out=a, in0=b, scalar1=SCL, scalar2=None, op0=ALU.mult)))(col(13), col(12)), "cV")
                P.wait("vector", "cV", v12)
                v13 = P.add("vector", (lambda c16, c32, c9: (lambda q: q.tensor_scalar(out=ST[:, c32:c32 + 8], in0=ST[:, c16:c16 + 8],
                                                                                       scalar1=c9, scalar2=None, op0=ALU.is_equal)))(base + 16, base + 32, col(9)), "cV")
                P.wait("vector", "cV", v13)
                v14 = P.add("vector", (lambda c32, c40, c13: (lambda q: q.tensor_scalar(out=ST[:, c40:c40 + 8], in0=ST[:, c32:c32 + 8],
                                                                                        scalar1=c13, scalar2=None, op0=ALU.mult)))(base + 32, base + 40, col(13)), "cV")
                P.wait("tensor", "cV", v14)
                P.wait("tensor", "cS", pst["mscT_c"])
                v15 = P.add("tensor", (lambda c40: (lambda q: q.transpose(PS[0:8, 3584:3712], ST[:, c40:c40 + 8], idf[:, :])))(base + 40), "cP")
                v15_par[pp] = v15
                P.wait("scalar", "cP", v15)
                P.wait("scalar", "cP", pst["mt_rd"])
                pst["mscT_c"] = P.add("scalar", (lambda tt_: (lambda q: q.copy(out=MT[0:8, tt_ * 128:(tt_ + 1) * 128],
                                                                               in_=PS[0:8, 3584:3712])))(tt), "cS")
            for c4 in range(4):
                P.wait("tensor", "cS", pst["mscT_c"])
                bank_wait(3)
                P.add("tensor", (lambda c4_: (lambda q: q.matmul(BANK(3), blkid[0:8, c4_ * 128:(c4_ + 1) * 128],
                                                                 MT[0:8, :], start=True, stop=True)))(c4), "cP")
                vp = P.cnt["cP"]
                P.wait("scalar", "cP", vp)
                P.wait("scalar", "cV", pst["hm"])
                pst["msk_ev"] = P.add("scalar", (lambda c4_: (lambda q: q.copy(out=msk[:, c4_ * 512:(c4_ + 1) * 512],
                                                                               in_=BANK(3))))(c4), "cS")
                BG[3]["cS"] = pst["msk_ev"]
            pst["mt_rd"] = P.cnt["cP"]
            v_hm_bank = pst.setdefault("v_hm_bank", [0, 0])
            for ec in range(4):
                bk = 4 if ec % 2 == 0 else 2
                P.wait("tensor", "cS", pst["msk_ev"])
                if bk == 2:
                    bank_wait(2)
                else:
                    P.wait("tensor", "cV", v_hm_bank[0])
                for c in range(16):
                    P.add("tensor", (lambda c_, ec_, bk_: (lambda q: q.matmul(
                        BANK(bk_), fA[:, c_ * 512 + ec_ * 128: c_ * 512 + (ec_ + 1) * 128],
                        xlnT[:, c_ * 512:(c_ + 1) * 512], start=(c_ == 0), stop=(c_ == 15))))(c, ec, bk), "cP")
                vp = P.cnt["cP"]
                P.wait("vector", "cP", vp)
                v_hm_bank[ec % 2] = P.add("vector", (lambda ec_, bk_: (lambda q: q.tensor_tensor(
                    out=hm[:, ec_ * 512:(ec_ + 1) * 512], in0=BANK(bk_), in1=msk[:, ec_ * 512:(ec_ + 1) * 512],
                    op=ALU.mult)))(ec, bk), "cV")
                if bk == 2:
                    BG[2]["cV"] = v_hm_bank[1]
            pst["hm"] = v_hm_bank[1]
            v_yev = {}
            v_tr_done = {}
            for oc in range(17):
                if oc < 16:
                    bk = 5 if oc % 2 == 0 else 3
                    P.wait("tensor", "cV", pst["hm"])
                    if bk == 3:
                        bank_wait(3)
                    else:
                        P.wait("tensor", "cS", v_yev.get(oc - 2, 0))
                    for ec in range(4):
                        P.add("tensor", (lambda oc_, ec_, bk_: (lambda q: q.matmul(
                            BANK(bk_), fB[:, ec_ * 2048 + oc_ * 128: ec_ * 2048 + (oc_ + 1) * 128],
                            hm[:, ec_ * 512:(ec_ + 1) * 512], start=(ec_ == 0), stop=(ec_ == 3))))(oc, ec, bk), "cP")
                    vmm = P.cnt["cP"]
                    P.wait("scalar", "cP", vmm)
                    v_yev[oc] = P.add("scalar", (lambda bk_, oc_: (lambda q: q.copy(out=ys[oc_ % 2][:, :], in_=BANK(bk_))))(bk, oc), "cS")
                    if bk == 3:
                        BG[3]["cS"] = v_yev[oc]
                ot = oc - 1
                if 0 <= ot < 16:
                    P.wait("tensor", "cS", v_yev[ot])
                    P.wait("tensor", "cV", pst["trf"])
                    for tt in range(4):
                        P.add("tensor", (lambda tt_, ot_: (lambda q: q.transpose(
                            BANK(6)[:, tt_ * 128:(tt_ + 1) * 128], ys[ot_ % 2][:, tt_ * 128:(tt_ + 1) * 128], idf[:, :])))(tt, ot), "cP")
                    v_tr_done[ot] = P.cnt["cP"]
                    P.wait("vector", "cP", v_tr_done[ot])
                    for tt in range(4):
                        pst["trf"] = P.add("vector", (lambda tt_, oc_: (lambda q: q.tensor_tensor(
                            out=x_tiles[tt_][:, oc_ * 128:(oc_ + 1) * 128], in0=x_tiles[tt_][:, oc_ * 128:(oc_ + 1) * 128],
                            in1=BANK(6)[:, tt_ * 128:(tt_ + 1) * 128], op=ALU.add)))(tt, ot), "cV")
            if out_dst is not None:
                P.wait("sync", "cV", pst["trf"])
                fstate["st_tile"] = P.add("sync", (lambda d_: (lambda q: q.dma_start(
                    out=d_, in_=AR1[:, 0:8192].rearrange("p (a d) -> p a d", a=4))))(out_dst), "dS", 16)
            return pst["trf"]

        P.wait_bar("sync", bar0)
        vw1 = emit_ffn_weights_load()
        for e in ("scalar", "vector", "tensor"):
            P.wait_bar(e, bar0)
            P.wait(e, "dL", vw1)
        x_tiles = [AR1[:, k * 2048:(k + 1) * 2048] for k in range(4)]
        pst["v_xlb"] = [0, 0]

        def x_tile_dma(q, sup, hf):
            return q.dma_start(
                out=AR1[:, hf * 4096:(hf + 1) * 4096].rearrange("p (j a d) -> p j a d", j=1, a=2),
                in_=x_in[:, :].rearrange("(j a p) d -> p j a d", j=2, p=128)[
                    :, bass.DynSlice(cell["jj"], 1), sup * 4 + hf * 2:sup * 4 + hf * 2 + 2, :])

        xf = SM1
        xf2 = AR2[:, 8192:10240]
        xb = SM3b[:, 0:2048]
        sqd = AR2[:, 10240:12288]
        xlnbT = AR2b[:, 24576:32768]
        xln2T_sb = AR1[:, 8192:16384]
        v13_par = [0, 0]
        v_x1_rd = [0] * 4
        v_xf_rd = 0
        v_evb = 0
        v_evf = 0
        v_2a_st = 0
        fstate = {"tile_free": [], "st_tile": 0}

        for sup in range(4):
            vtrf = emit_ffn_sup(sup, x_tiles,
                         x1_scr[:, :].rearrange("(a p) d -> p a d", p=128)[:, sup * 4:sup * 4 + 4, :],
                         fstate, tile_dma=x_tile_dma)
            P.wait("scalar", "cV", vtrf)
            P.wait("vector", "cV", vtrf)
            P.wait("vector", "dSy0", v_2a_st)
            P.wait("scalar", "dSy0", v_2a_st)
            for tt in range(4):
                ti = sup * 4 + tt
                pp = tt % 2
                base = 128 + 32 * pp
                col = lambda k: ST[:, base + k:base + k + 1]
                x1t = x_tiles[tt]
                P.wait("vector", "cS", v13_par[pp])
                v1a = P.add("vector", (lambda x_, c_: (lambda q: q.reduce_sum(c_, x_, axis=AX.X)))(x1t, col(0)), "cV")
                v2 = P.add("scalar", (lambda x_, c_: (lambda q: q.activation(out=sqd[:, :], in_=x_, func=AF.Square, accum_out=c_)))(x1t, col(1)), "cS")
                P.wait("vector", "cS", v2)
                P.wait("vector", "cV", v1a)
                vv = P.add("vector", (lambda a, b: (lambda q: q.tensor_scalar(out=a, in0=b, scalar1=1.0 / D, scalar2=None, op0=ALU.mult)))(col(2), col(0)), "cV")
                vv = P.add("vector", (lambda a, b: (lambda q: q.tensor_scalar(out=a, in0=b, scalar1=1.0 / D, scalar2=None, op0=ALU.mult)))(col(3), col(1)), "cV")
                P.wait("vector", "cV", vv)
                vv = P.add("vector", (lambda a, b: (lambda q: q.tensor_tensor(out=a, in0=b, in1=b, op=ALU.mult)))(col(4), col(2)), "cV")
                P.wait("vector", "cV", vv)
                v3 = P.add("vector", (lambda a, b, c: (lambda q: q.tensor_tensor(out=a, in0=b, in1=c, op=ALU.subtract)))(col(5), col(3), col(4)), "cV")
                P.wait("scalar", "cV", v3)
                vl = P.add("scalar", (lambda a, b: (lambda q: q.activation(out=a, in_=b, func=AF.Ln, bias=cst[:, 0:1])))(col(6), col(5)), "cS")
                P.wait("scalar", "cS", vl)
                v4 = P.add("scalar", (lambda a, b: (lambda q: q.activation(out=a, in_=b, func=AF.Exp, scale=cst[:, 1:2])))(col(7), col(6)), "cS")
                P.wait("vector", "cS", v4)
                v5 = P.add("vector", (lambda a, b, c: (lambda q: q.tensor_scalar(out=a, in0=b, scalar1=c, scalar2=-1.0,
                                                                                 op0=ALU.mult, op1=ALU.mult)))(col(8), col(2), col(7)), "cV")
                P.wait("scalar", "cV", v5)
                P.wait("scalar", "cS", v_xf_rd)
                v6 = P.add("scalar", (lambda x_, c7, c8: (lambda q: q.activation(out=xf[:, :], in_=x_, func=AF.Identity,
                                                                                 scale=c7, bias=c8)))(x1t, col(7), col(8)), "cS")
                v_x1_rd[tt] = v6
                P.wait("vector", "cS", v6)
                v7a = P.add("vector", (lambda c_: (lambda q: q.reduce_sum(c_, xf[:, :], axis=AX.X)))(col(9)), "cV")
                P.wait("scalar", "cS", v6)
                v8 = P.add("scalar", (lambda c_: (lambda q: q.activation(out=sqd[:, :], in_=xf[:, :], func=AF.Square, accum_out=c_)))(col(10)), "cS")
                P.wait("vector", "cS", v8)
                P.wait("vector", "cV", v7a)
                vv = P.add("vector", (lambda a, b: (lambda q: q.tensor_scalar(out=a, in0=b, scalar1=1.0 / D, scalar2=None, op0=ALU.mult)))(col(11), col(9)), "cV")
                vv = P.add("vector", (lambda a, b: (lambda q: q.tensor_scalar(out=a, in0=b, scalar1=1.0 / D, scalar2=None, op0=ALU.mult)))(col(12), col(10)), "cV")
                P.wait("vector", "cV", vv)
                vv = P.add("vector", (lambda a, b: (lambda q: q.tensor_tensor(out=a, in0=b, in1=b, op=ALU.mult)))(col(13), col(11)), "cV")
                P.wait("vector", "cV", vv)
                v9 = P.add("vector", (lambda a, b, c: (lambda q: q.tensor_tensor(out=a, in0=b, in1=c, op=ALU.subtract)))(col(14), col(12), col(13)), "cV")
                P.wait("scalar", "cV", v9)
                vl = P.add("scalar", (lambda a, b: (lambda q: q.activation(out=a, in_=b, func=AF.Ln, bias=cst[:, 0:1])))(col(15), col(14)), "cS")
                P.wait("scalar", "cS", vl)
                v10 = P.add("scalar", (lambda a, b: (lambda q: q.activation(out=a, in_=b, func=AF.Exp, scale=cst[:, 1:2])))(col(16), col(15)), "cS")
                P.wait("vector", "cS", v10)
                v11 = P.add("vector", (lambda a, b, c: (lambda q: q.tensor_scalar(out=a, in0=b, scalar1=c, scalar2=-1.0,
                                                                                  op0=ALU.mult, op1=ALU.mult)))(col(17), col(11), col(16)), "cV")
                P.wait("scalar", "cV", v11)
                P.wait("scalar", "cP", pst["v_xlb_2a"][0])
                v12 = P.add("scalar", lambda q: q.activation(out=xb[:, :], in_=xf[:, :], func=AF.Identity), "cS")
                P.wait("scalar", "cP", pst["v_xlb_2a"][1])
                v13 = P.add("scalar", (lambda c16, c17: (lambda q: q.activation(out=xf2[:, :], in_=xf[:, :], func=AF.Identity,
                                                                                scale=c16, bias=c17)))(col(16), col(17)), "cS")
                v13_par[pp] = v13
                v_xf_rd = v13
                for g in range(4):
                    P.wait("tensor", "cS", v12)
                    P.wait("tensor", "cV", pst["tr_evac"])
                    for c4 in range(4):
                        c = g * 4 + c4
                        P.add("tensor", (lambda c_, c4_, g_: (lambda q: q.transpose(
                            PSb[:, (g_ % 2) * 1024 + c4_ * 128:(g_ % 2) * 1024 + (c4_ + 1) * 128],
                            xb[:, c_ * 128:(c_ + 1) * 128], idb[:, :])))(c, c4, g), "cP")
                    vp = P.cnt["cP"]
                    P.wait("vector", "cP", vp)
                    pst["tr_evac"] = P.add("vector", (lambda g_, tt_: (lambda q: q.tensor_copy(
                        out=xlnbT[:, :].rearrange("p (c t) -> p c t", c=16)[:, g_ * 4:(g_ + 1) * 4, tt_ * 128:(tt_ + 1) * 128],
                        in_=PSb[:, (g_ % 2) * 1024:(g_ % 2) * 1024 + 512].rearrange("p (c t) -> p c t", c=4))))(g, tt), "cV")
                pst["v_xlb_2a"][0] = P.cnt["cP"]
                for g in range(8):
                    P.wait("tensor", "cS", v13)
                    bk = 2 + g % 2
                    bank_wait(bk)
                    for c2 in range(2):
                        c = g * 2 + c2
                        P.add("tensor", (lambda c_, c2_, bk_: (lambda q: q.transpose(
                            BANK(bk_)[:, c2_ * 128:(c2_ + 1) * 128],
                            xf2[:, c_ * 128:(c_ + 1) * 128], idf[:, :])))(c, c2, bk), "cP")
                    vp = P.cnt["cP"]
                    P.wait("scalar", "cP", vp)
                    vev = P.add("scalar", (lambda g_, tt_, bk_: (lambda q: q.copy(
                        out=xln2T_sb[:, :].rearrange("p (c t) -> p c t", c=16)[:, g_ * 2:(g_ + 1) * 2, tt_ * 128:(tt_ + 1) * 128],
                        in_=BANK(bk_, 256).rearrange("p (c t) -> p c t", c=2))))(g, tt, bk), "cS")
                    BG[bk]["cS"] = vev
                pst["v_xlb_2a"][1] = P.cnt["cP"]
            P.wait("sync", "cV", pst["tr_evac"])
            P.wait("sync", "cS", BG[2]["cS"])
            P.wait("sync", "cS", BG[3]["cS"])
            P.wait("sync", "dSy0", v_2a_st)
            P.add("sync", (lambda s_: (lambda q: q.dma_start(
                out=xlnbT_scr[:, :].rearrange("(c p) t -> p c t", p=128)[:, :, s_ * 512:(s_ + 1) * 512],
                in_=xlnbT[:, :].rearrange("p (c t) -> p c t", c=16))))(sup), "dSy0", 16)
            v_2a_st = P.add("sync", (lambda s_: (lambda q: q.dma_start(
                out=xln2T_scr[:, :].rearrange("(c p) t -> p c t", p=128)[:, :, s_ * 512:(s_ + 1) * 512],
                in_=xln2T_sb[:, :].rearrange("p (c t) -> p c t", c=16))))(sup), "dSy0", 16)
            fstate["tile_free_h"] = [
                [("cV", pst["trf"]), ("cS", max(v_x1_rd[0], v_x1_rd[1]))],
                [("cV", pst["trf"]), ("cS", max(v_x1_rd[2], v_x1_rd[3]))],
            ]
        P.wait("sync", "dSy0", P.cnt["dSy0"])
        bar2a = P.bar()

        for e in ENGS[:4]:
            P.wait_bar(e, bar2a)
        xnb = AR1b
        chunkbuf = AR3[:, 2048:4096]
        xpw_sb = AR3b[:, 12288:16384]
        projd = AR3b[:, 0:2048]
        proj_sb = AR2[:, 0:4096]
        v_pw = P.add("sync", lambda q: q.dma_start(
            out=xpw_sb[:, :].rearrange("p (c e) -> p c e", c=16),
            in_=xpwT_b_scr[:, :].rearrange("(c p) e -> p c e", p=128)), "dR", 16)
        v_dwl = P.add("sync", lambda q: q.dma_start(out=dtw_sb[:, :], in_=dtwT_b_scr[:, :]), "dR", 16)
        v_aff_last = 0
        v_mm_half = {0: 0, 1: 0}
        v_ev_last = 0
        for half in range(2):
            for i in range(8):
                c = half * 8 + i
                vx = P.add("sync", (lambda c_: (lambda q: q.dma_start(
                    out=chunkbuf[:, :], in_=xln2T_scr[c_ * 128:(c_ + 1) * 128, :])))(c), "dL", 16)
                P.wait("scalar", "dL", vx)
                if half == 1:
                    P.wait("scalar", "cP", v_mm_half[0])
                v_aff_last = P.add("scalar", (lambda c_, i_: (lambda q: q.activation(
                    out=xnb[:, i_ * 2048:(i_ + 1) * 2048], in_=chunkbuf[:, :], func=AF.Identity,
                    scale=nw[:, c_:c_ + 1], bias=nb[:, c_:c_ + 1])))(c, i), "cS")
                P.wait("sync", "cS", v_aff_last)
            for po in range(2):
                for tp2 in range(4):
                    P.wait("tensor", "cS", v_aff_last)
                    P.wait("tensor", "dR", max(v_pw, v_dwl))
                    P.wait("tensor", "cV", v_ev_last)
                    for i in range(8):
                        P.add("tensor", (lambda po_, tp_, i_, h_: (lambda q: q.matmul(
                            PS[:, 0:512],
                            xpw_sb[:, (h_ * 8 + i_) * 256 + po_ * 128:(h_ * 8 + i_) * 256 + (po_ + 1) * 128],
                            xnb[:, i_ * 2048 + tp_ * 512:i_ * 2048 + (tp_ + 1) * 512],
                            start=(i_ == 0), stop=(i_ == 7))))(po, tp2, i, half), "cP")
                    vp = P.cnt["cP"]
                    P.wait("vector", "cP", vp)
                    if half == 0:
                        v_ev_last = P.add("vector", (lambda po_, tp_: (lambda q: q.tensor_copy(
                            out=proj_sb[:, po_ * 2048 + tp_ * 512: po_ * 2048 + (tp_ + 1) * 512],
                            in_=PS[:, 0:512])))(po, tp2), "cV")
                    else:
                        v_ev_last = P.add("vector", (lambda po_, tp_: (lambda q: q.tensor_tensor(
                            out=proj_sb[:, po_ * 2048 + tp_ * 512: po_ * 2048 + (tp_ + 1) * 512],
                            in0=proj_sb[:, po_ * 2048 + tp_ * 512: po_ * 2048 + (tp_ + 1) * 512],
                            in1=PS[:, 0:512], op=ALU.add)))(po, tp2), "cV")
            v_mm_half[half] = P.cnt["cP"]
        P.wait("vector", "cV", v_ev_last)
        v_pd = P.add("vector", lambda q: q.tensor_copy(out=projd[:, :], in_=proj_sb[:, 0:2048]), "cV")
        P.wait("sync", "cV", v_ev_last)
        v_bc = P.add("sync", lambda q: q.dma_start(out=Ph[2048:2176, :], in_=proj_sb[:, 2048:4096]), "dS", 16)
        xnf = AR1[:, 0:2048]
        t1 = AR1[:, 4096:5120]
        t2 = AR1[:, 5120:6144]
        dpc = [SM2[:, 0:1024], SM2[:, 1024:2048]]
        wpc = [SM3[:, 0:1024], SM3[:, 1024:2048]]
        v_dst = {0: 0, 1: 0}
        v_sp = {0: 0, 1: 0}
        v_vd_prev = 0
        v_xnf_rd = 0
        for dblk in range(16):
            P.wait("sync", "cV", v_xnf_rd)
            vx = P.add("sync", (lambda d_: (lambda q: q.dma_start(
                out=chunkbuf[:, :], in_=xln2T_scr[d_ * 128:(d_ + 1) * 128, :])))(dblk), "dL", 16)
            P.wait("scalar", "dL", vx)
            P.wait("scalar", "cP", P.cnt["cP"] if dblk == 0 else 0)
            v_xnf = P.add("scalar", (lambda d_: (lambda q: q.activation(
                out=xnf[:, :], in_=chunkbuf[:, :], func=AF.Identity,
                scale=nw[:, d_:d_ + 1], bias=nb[:, d_:d_ + 1])))(dblk), "cS")
            P.wait("sync", "cS", v_xnf)
            for tp in range(2):
                par = tp % 2
                pslice = PS[:, 1024 + par * 1024: 2048 + par * 1024]
                P.wait("tensor", "cV", v_pd)
                P.wait("tensor", "cS", v_sp[par])
                for hh in range(2):
                    P.add("tensor", (lambda d_, tp_, ps_, hh_: (lambda q: q.matmul(
                        ps_[:, hh_ * 512:(hh_ + 1) * 512], dtw_sb[:, d_ * 128:(d_ + 1) * 128],
                        projd[:, tp_ * 1024 + hh_ * 512:tp_ * 1024 + (hh_ + 1) * 512],
                        start=True, stop=True)))(dblk, tp, pslice, hh), "cP")
                vp = P.cnt["cP"]
                P.wait("scalar", "cP", vp)
                P.wait("scalar", "cV", v_vd_prev)
                P.wait("scalar", "dS%s" % ("a" if par == 0 else "b"), v_dst[par])
                va = P.add("scalar", (lambda d_, ps_: (lambda q: q.activation(out=t1[:, :], in_=ps_, func=AF.Abs,
                                                                         bias=dtb[:, d_:d_ + 1])))(dblk, pslice), "cS")
                P.wait("scalar", "cS", va)
                va = P.add("scalar", lambda q: q.activation(out=t2[:, :], in_=t1[:, :], func=AF.Exp, scale=cst[:, 2:3]), "cS")
                P.wait("scalar", "cS", va)
                va = P.add("scalar", lambda q: q.activation(out=t1[:, :], in_=t2[:, :], func=AF.Ln, bias=cst[:, 3:4]), "cS")
                P.wait("scalar", "cS", va)
                v_sp[par] = P.add("scalar", (lambda d_, ps_, par_: (lambda q: q.activation(
                    out=dpc[par_][:, :], in_=ps_, func=AF.Relu, bias=dtb[:, d_:d_ + 1])))(dblk, pslice, par), "cS")
                P.wait("vector", "cS", v_sp[par])
                v_vd_prev = P.add("vector", (lambda par_: (lambda q: q.tensor_tensor(
                    out=dpc[par_][:, :], in0=dpc[par_][:, :], in1=t1[:, :], op=ALU.add)))(par), "cV")
                P.wait("vector", "dS%s" % ("a" if par == 0 else "b"), v_dst[par])
                P.wait("vector", "cV", v_vd_prev)
                vw = P.add("vector", (lambda par_, tp_: (lambda q: q.tensor_tensor(
                    out=wpc[par_][:, :], in0=dpc[par_][:, :], in1=xnf[:, tp_ * 1024:(tp_ + 1) * 1024],
                    op=ALU.mult)))(par, tp), "cV")
                if tp == 1:
                    v_xnf_rd = vw
                P.wait("sync", "cV", vw)
                spar = "dSa" if par == 0 else "dSb"
                gb = 2176 * (dblk // 8) + (dblk % 8) * 128
                P.add("sync", (lambda g_, tp_, par_: (lambda q: q.dma_start(
                    out=Ph[g_:g_ + 128, tp_ * 1024:(tp_ + 1) * 1024],
                    in_=dpc[par_][:, :])))(gb, tp, par), spar, 16)
                v_dst[par] = P.add("sync", (lambda g_, tp_, par_: (lambda q: q.dma_start(
                    out=Ph[1024 + g_:1024 + g_ + 128, tp_ * 1024:(tp_ + 1) * 1024],
                    in_=wpc[par_][:, :])))(gb, tp, par), spar, 16)
        P.wait("sync", "cV", v_ev_last)
        v_bc = P.add("sync", lambda q: q.dma_start(out=Ph[4224:4352, :], in_=proj_sb[:, 2048:4096]), "dS", 16)
        bar2b = P.bar()
        P.wait_bar("gpsimd", bar2b)
        v_cc2 = P.add("gpsimd", lambda q: q.collective_compute(
            "AllGather", ALU.bypass, replica_groups=[[0, 1], [2, 3], [4, 5], [6, 7]],
            ins=[Ph.ap().opt()], outs=[Pgx.ap().opt()]), "ccm")

        for e in ENGS:
            P.wait_bar(e, bar2b)
        delta_sb = AR1[:, 0:8192]
        w_sb = AR1[:, 8192:16384]
        y_acc = AR2[:, 0:8192]
        dA2 = [AR3[:, 0:2048], AR3[:, 2048:4096]]
        u2 = PS[:, 0:2048]
        tmp2 = [AR3[:, 6144:8192], AR2[:, 12288:14336]]
        h2 = [PS[:, 2048:4096], AR2[:, 10240:12288]]
        y_bf = [AR2[:, 14336:15360].bitcast(BF16), AR2[:, 15360:16384].bitcast(BF16)]
        bufB2 = [SM0[:, 0:2048], SM1[:, 0:2048]]
        bufC2 = [SM2[:, 0:2048], SM3[:, 0:2048]]
        v_dA = {}
        v_u2 = {}
        v_tts = {}
        v_tts2 = {}
        v_ym2 = {}
        v_acm2 = {}
        v_uB_par = {0: 0, 1: 0}
        v_ymC_par = {0: 0, 1: 0}
        v_store_y = {0: 0, 1: 0}
        v_cast_prev_cc = 0
        v_ext_last = 0
        v_pair_end = 0
        tmp_reader = [None, None]
        NPAIR = NST * 4
        q = 0
        P.wait("sync", "ccm", v_cc2)
        v_pbc = 0
        for hh_ in range(2):
            gkey0 = "jj" if hh_ == 0 else "kk2"
            v_pbc = P.add("sync", (lambda k_, h_: (lambda q_: q_.dma_start(
                out=Pbc[h_ * 2176:(h_ + 1) * 2176, :].rearrange("(g r) t -> g r t", g=1),
                in_=Pgx[:, :].rearrange("(g r) t -> g r t", g=4)[
                    bass.DynSlice(cell[k_], 1), :, :])))(gkey0, hh_), "dR", 16)
        P.wait("sync", "dR", v_pbc)
        for cc in range(NCC):
            jb, tloc = cc // 2, cc % 2
            if cc > 0:
                P.wait("sync", "cS", v_dA[2 * q - 1])
                P.wait("sync", "cV", v_u2[q - 1])
            vL1 = P.add("sync", (lambda jb_, tl_: (lambda q_: q_.dma_start(
                out=delta_sb[:, :].rearrange("p (g d t) -> p g d t", g=1, d=8),
                in_=Pbc[:, :].rearrange("(g d p) t -> p g d t", g=2, d=17)[
                    :, jb_:jb_ + 1, 0:8, tl_ * TC:(tl_ + 1) * TC])))(jb, tloc), "dL", 16)
            vL2 = P.add("sync", (lambda jb_, tl_: (lambda q_: q_.dma_start(
                out=w_sb[:, :].rearrange("p (g d t) -> p g d t", g=1, d=8),
                in_=Pbc[:, :].rearrange("(g d p) t -> p g d t", g=2, d=17)[
                    :, jb_:jb_ + 1, 8:16, tl_ * TC:(tl_ + 1) * TC])))(jb, tloc), "dL", 16)
            P.wait("vector", "cS", v_cast_prev_cc)
            P.wait("vector", "cV", v_ext_last)
            for n in range(NST):
                par_n = n % 2
                sB = "dRB%d" % par_n
                sC = "dRC%d" % par_n
                P.wait("sync", "cV", v_uB_par[par_n])
                P.wait("sync", "cV", v_ymC_par[par_n])
                for rr in range(2):
                    rB = P.add("sync", (lambda n_, p_, jb_, tl_, rr_: (lambda q_: q_.dma_start(
                        out=bufB2[p_][:, rr_ * TC:(rr_ + 1) * TC],
                        in_=Pbc[jb_ * 2176 + 2048 + n_:jb_ * 2176 + 2049 + n_,
                                tl_ * TC:(tl_ + 1) * TC].broadcast_to([128, TC]))))(
                        n, par_n, jb, tloc, rr), sB, 16)
                for rr in range(4):
                    rC = P.add("sync", (lambda n_, p_, rr_: (lambda q_: q_.dma_start(
                        out=bufC2[p_][:, rr_ * 512:(rr_ + 1) * 512],
                        in_=Pbc[2112 + n_:2113 + n_, 0:512].broadcast_to([128, 512]))))(n, par_n, rr), sC, 16)
                for dp in range(4):
                    qs = q % 2
                    if q >= 2:
                        P.wait("scalar", "cV", v_tts2[q - 2])
                    if n == 0 and dp == 0:
                        P.wait("scalar", "dL", vL2)
                    for hh in range(2):
                        dblk = 2 * dp + hh
                        v_dA[2 * q + hh] = P.add("scalar", (lambda d_, n_, s_, hh_: (lambda q_: q_.activation(
                            out=dA2[s_][:, hh_ * 1024:(hh_ + 1) * 1024],
                            in_=delta_sb[:, d_ * TC:(d_ + 1) * TC], func=AF.Exp,
                            scale=A_sb[:, d_ * 64 + n_: d_ * 64 + n_ + 1])))(dblk, n, qs, hh), "cS")
                    if dp == 0:
                        P.wait("vector", sB, rB)
                        P.wait("vector", sC, rC)
                        if n == 0:
                            P.wait("vector", "dL", vL2)
                    P.wait("vector", "cV", v_pair_end)
                    v_u2[q] = P.add("vector", (lambda dp_, p_: (lambda q_: q_.tensor_tensor(
                        out=u2[:, :], in0=w_sb[:, dp_ * 2048:(dp_ + 1) * 2048],
                        in1=bufB2[p_][:, :], op=ALU.mult)))(dp, par_n), "cV")
                    P.wait("vector", "cS", v_dA[2 * q + 1])
                    P.wait("vector", "cV", v_u2[q])
                    for hh in range(2):
                        dblk = 2 * dp + hh
                        ini = 0.0 if cc == 0 else carry[:, dblk * 64 + n: dblk * 64 + n + 1]
                        v_tts[2 * q + hh] = P.add("vector", (lambda s_, hh_, i_: (lambda q_: q_.tensor_tensor_scan(
                            out=h2[s_][:, hh_ * 1024:(hh_ + 1) * 1024],
                            data0=dA2[s_][:, hh_ * 1024:(hh_ + 1) * 1024],
                            data1=u2[:, hh_ * 1024:(hh_ + 1) * 1024], initial=i_,
                            op0=ALU.mult, op1=ALU.add)))(qs, hh, ini), "cV")
                    v_tts2[q] = v_tts[2 * q + 1]
                    if q >= 1 and cc < NCC - 1:
                        qp = q - 1
                        for hh in range(2):
                            mb = 2 * qp + hh
                            v_ext_last = P.add("vector", (lambda d_, n_, s_, hh_: (lambda q_: q_.tensor_copy(
                                out=carry[:, d_ * 64 + n_: d_ * 64 + n_ + 1],
                                in_=h2[s_][:, hh_ * 1024 + TC - 1: hh_ * 1024 + TC])))(
                                (mb // 1) % 8 if False else 2 * ((qp) % 4) + hh, (qp // 4) % NST, qp % 2, hh), "cV")
                    P.wait("vector", "cV", v_tts2[q])
                    if n == 0:
                        v_ym2[q] = P.add("vector", (lambda dp_, s_, p_: (lambda q_: q_.tensor_tensor(
                            out=y_acc[:, dp_ * 2048:(dp_ + 1) * 2048],
                            in0=h2[s_][:, :], in1=bufC2[p_][:, :],
                            op=ALU.mult)))(dp, qs, par_n), "cV")
                        v_acm2[q] = None
                    else:
                        tr = tmp_reader[qs]
                        if tr is not None:
                            P.wait("vector", tr[0], tr[1])
                        v_ym2[q] = P.add("vector", (lambda s_, p_: (lambda q_: q_.tensor_tensor(
                            out=tmp2[s_][:, :], in0=h2[s_][:, :],
                            in1=bufC2[p_][:, :], op=ALU.mult)))(qs, par_n), "cV")
                        sG = "dG%d" % dp
                        P.wait("gpsimd", "cV", v_ym2[q])
                        if v_acm2.get(q - 4) is not None:
                            P.wait("gpsimd", sG, v_acm2[q - 4][1])
                        va = P.add("gpsimd", (lambda dp_, s_: (lambda q_: q_.dma_start(
                            out=y_acc[:, dp_ * 2048:(dp_ + 1) * 2048], in_=tmp2[s_][:, :],
                            accum_op=ALU.add)))(dp, qs), sG, 16)
                        v_acm2[q] = (sG, va)
                        tmp_reader[qs] = (sG, va)
                    if dp == 3:
                        v_uB_par[par_n] = v_u2[q]
                        v_ymC_par[par_n] = v_ym2[q]
                    v_pair_end = P.cnt["cV"]
                    q += 1
            if cc < NCC - 1:
                qp = q - 1
                P.wait("vector", "cV", v_tts2[qp])
                for hh in range(2):
                    v_ext_last = P.add("vector", (lambda d_, n_, s_, hh_: (lambda q_: q_.tensor_copy(
                        out=carry[:, d_ * 64 + n_: d_ * 64 + n_ + 1],
                        in_=h2[s_][:, hh_ * 1024 + TC - 1: hh_ * 1024 + TC])))(
                        2 * (qp % 4) + hh, (qp // 4) % NST, qp % 2, hh), "cV")
            v_cast_prev_cc = 0
            for dp in range(4):
                pb = dp % 2
                last_q = (cc * NPAIR) + 63 * 4 + dp
                ac = v_acm2[last_q]
                P.wait("scalar", ac[0], ac[1])
                P.wait("scalar", "dSy%d" % pb, v_store_y[pb])
                vca = P.add("scalar", (lambda dp_, pb_: (lambda q_: q_.activation(
                    out=y_bf[pb_][:, :], in_=y_acc[:, dp_ * 2048:(dp_ + 1) * 2048], func=AF.Identity)))(dp, pb), "cS")
                v_cast_prev_cc = vca
                P.wait("sync", "cS", vca)
                v_store_y[pb] = P.add("sync", (lambda dp_, cc_, pb_: (lambda q_: q_.dma_start(
                    out=y_scr[:, :].rearrange("(d p) t -> p d t", p=128)[:, 2 * dp_:2 * dp_ + 2, cc_ * TC:(cc_ + 1) * TC],
                    in_=y_bf[pb_][:, :].rearrange("p (r t) -> p r t", r=2))))(dp, cc, pb), "dSy%d" % pb, 16)
        bar3 = P.bar()

        if debug:
            P.wait_bar("sync", bar3)
            P.add("sync", lambda q: q.dma_start(out=y_dbg[:, :], in_=y_scr[:, :]), "dS", 16)
        P.wait_bar("gpsimd", bar3)
        v_cc = P.add("gpsimd", lambda q: q.collective_compute(
            "AllGather", ALU.bypass, replica_groups=[[0, 1], [2, 3], [4, 5], [6, 7]],
            ins=[y_scr.ap().opt()], outs=[ygx_scr.ap().opt()]), "ccm")

        for e in ENGS[:4]:
            P.wait_bar(e, bar3)
        yrhs = AR1b[:, 16384:24576]
        xlr = AR2b[:, 16384:32768]
        wpair = [AR3b[:, 0:4096], AR3b[:, 4096:8192]]
        x2t = [AR1[:, k * 2048:(k + 1) * 2048] for k in range(4)]
        ys2 = AR2[:, 7680:8192]
        v_ffn3 = 0
        v_out_st = 0
        DynS = bass.DynSlice
        for tp in range(4):
            P.wait("sync", "cP", P.cnt["cP"] if tp else 0)
            P.wait("sync", "cV", v_ffn3)
            P.wait("sync", "dS", v_out_st)
            vx1 = P.add("sync", (lambda tp_: (lambda q: q.dma_start(
                out=AR1[:, 0:8192].rearrange("p (a d) -> p a d", a=4),
                in_=x1_scr[:, :].rearrange("(a p) d -> p a d", p=128)[:, tp_ * 4:(tp_ + 1) * 4, :])))(tp), "dL", 16)
            vy1 = P.add("sync", (lambda tp_: (lambda q: q.dma_start(
                out=yrhs[:, 0:4096].rearrange("p (c t) -> p c t", c=8),
                in_=y_scr[:, :].rearrange("(c p) t -> p c t", p=128)[:, :, DynS(cell["tsl"][tp_], 512)])))(tp), "dL", 16)
            v_x1 = [vy1] * 4
            vy3 = P.add("sync", (lambda tp_: (lambda q: q.dma_start(
                out=xlr[:, 0:8192].rearrange("p (c t) -> p c t", c=16),
                in_=xlnbT_scr[:, :].rearrange("(c p) t -> p c t", p=128)[:, :, tp_ * 512:(tp_ + 1) * 512])))(tp), "dRB0", 16)
            if tp == 0:
                P.wait("sync", "ccm", v_cc)
            vy2 = P.add("sync", (lambda tp_: (lambda q: q.dma_start(
                out=yrhs[:, 4096:8192].rearrange("p (c t) -> p c t", c=8),
                in_=ygx_scr[:, :].rearrange("(c p) t -> p c t", p=128)[:, DynS(cell["k8"], 8), DynS(cell["tsl"][tp_], 512)])))(tp), "dRC0", 16)
            P.wait("vector", "dL", vy1)
            P.wait("vector", "dRC0", vy2)
            P.wait("vector", "dRB0", vy3)
            skipb = [AR2b[:, 0:512], AR2b[:, 512:1024]]
            vv_sc = [0, 0]
            v_skip = 0
            for c in range(17):
                if c < 16:
                    vv_sc[c % 2] = P.add("vector", (lambda c_: (lambda q: q.tensor_scalar(
                        out=skipb[c_ % 2], in0=xlr[:, c_ * 512:(c_ + 1) * 512],
                        scalar1=dpar[:, c_:c_ + 1], scalar2=None, op0=ALU.mult)))(c), "cV")
                if c >= 1:
                    cm = c - 1
                    P.wait("vector", "cV", vv_sc[cm % 2])
                    v_skip = P.add("vector", (lambda c_: (lambda q: q.tensor_tensor(
                        out=yrhs[:, c_ * 512:(c_ + 1) * 512], in0=yrhs[:, c_ * 512:(c_ + 1) * 512],
                        in1=skipb[c_ % 2], op=ALU.add)))(cm), "cV")
            v_wl = {0: 0, 1: 0}
            v_add = 0
            v_oev = 0
            for oc in range(16):
                wp = oc % 2
                sw = "dX%d" % wp
                P.wait("sync", "cP", v_wl[wp])
                vwo = P.add("sync", (lambda oc_, wp_: (lambda q: q.dma_start(
                    out=wpair[wp_][:, 0:2048].rearrange("p (c o) -> p c o", c=16),
                    in_=wo_b_scr[:, oc_ * 128:(oc_ + 1) * 128].rearrange("(c p) o -> p c o", p=128))))(oc, wp), sw, 16)
                P.wait("tensor", sw, vwo)
                if v_add:
                    P.wait("tensor", "cV", v_add)
                if oc == 0:
                    P.wait("tensor", "cV", v_skip)
                P.wait("tensor", "dRC0", vy2)
                P.wait("tensor", "dL", vy1)
                for dc in range(16):
                    P.add("tensor", (lambda dc_, wp_: (lambda q: q.matmul(
                        BANK(0), wpair[wp_][:, dc_ * 128:(dc_ + 1) * 128],
                        yrhs[:, dc_ * 512:(dc_ + 1) * 512], start=(dc_ == 0), stop=(dc_ == 15))))(dc, wp), "cP")
                v_wl[wp] = P.cnt["cP"]
                P.wait("scalar", "cP", v_wl[wp])
                P.wait("scalar", "cV", v_add if oc else 0)
                v_oev = P.add("scalar", lambda q: q.copy(out=ys2[:, :], in_=BANK(0)), "cS")
                P.wait("tensor", "cS", v_oev)
                for tt in range(4):
                    P.add("tensor", (lambda tt_: (lambda q: q.transpose(
                        BANK(1)[:, tt_ * 128:(tt_ + 1) * 128], ys2[:, tt_ * 128:(tt_ + 1) * 128], idf[:, :])))(tt), "cP")
                vp = P.cnt["cP"]
                P.wait("vector", "cP", vp)
                for tt in range(4):
                    P.wait("vector", "dL", v_x1[tt])
                    v_add = P.add("vector", (lambda tt_, oc_: (lambda q: q.tensor_tensor(
                        out=x2t[tt_][:, oc_ * 128:(oc_ + 1) * 128], in0=x2t[tt_][:, oc_ * 128:(oc_ + 1) * 128],
                        in1=BANK(1)[:, tt_ * 128:(tt_ + 1) * 128], op=ALU.add)))(tt, oc), "cV")
            P.wait("sync", "cP", max(v_wl[0], v_wl[1]))
            vw5 = emit_ffn_weights_load()
            for e in ("scalar", "vector", "tensor"):
                P.wait(e, "dL", vw5)
            f5state = {"tile_free": [], "st_tile": 0}
            v_ffn3 = emit_ffn_sup(0, x2t,
                                  out_c[:, :].rearrange("(a p) d -> p a d", p=128)[:, tp * 4:tp * 4 + 4, :],
                                  f5state, tile_dma=None, v_tiles_ready=v_add)
            v_out_st = P.cnt["dS"]
        P.wait("sync", "dS", P.cnt["dS"])

        import os as _os
        _dbg_emit = bool(_os.environ.get("KRN_DBG_EMIT"))

        def emitter(name):
            def run(q):
                for idx, (kind, *rest) in enumerate(P.ops[name]):
                    if kind == "wait":
                        q.wait_ge(sems[rest[0]], rest[1])
                    else:
                        fn, inc, k = rest
                        try:
                            instr = fn(q)
                        except Exception:
                            if _dbg_emit:
                                import traceback as _tb
                                print(f"EMIT FAIL {name} op#{idx}")
                                _tb.print_exc()
                            raise
                        if inc is not None and instr is not None:
                            instr.then_inc(sems[inc], k)
            return run

        block.sync(emitter("sync"))
        block.scalar(emitter("scalar"))
        block.vector(emitter("vector"))
        block.tensor(emitter("tensor"))
        block.gpsimd(emitter("gpsimd"))
    return nc


def _prep_inputs(core, inputs):
    b, j = core // 2, core % 2
    pm = np.arange(D) if j == 0 else np.r_[1024:2048, 0:1024]
    g = lambda k: np.asarray(inputs[k])
    x = g("L0_attn_x")[b][:, pm]
    xpw = g("L0_attn_x_proj_w")[:, pm]
    dtw = g("L0_attn_dt_proj_w")[pm]
    dtbv = g("L0_attn_dt_proj_b")[pm]
    alog = g("L0_attn_A_log")[pm][:DS]
    dparf = g("L0_attn_D_param")[pm]
    nwf = g("L0_attn_norm_w")[pm]
    nbf = g("L0_attn_norm_b")[pm]
    wout = g("L0_attn_out_proj_w")[pm][:, pm]
    A_cat = g("L0_ffn_A").reshape(ER, D)[:, pm]
    B_cat = g("L0_ffn_B").transpose(1, 0, 2).reshape(D, ER)[pm]
    wr = g("L0_ffn_Wr")[:, pm]
    br = g("L0_ffn_br")
    C = np.ascontiguousarray
    return dict(
        x_in=C(x.astype(np.float32)),
        xpwT=C(xpw.T.astype(np.float32)),
        dtwT=C(dtw.T.astype(np.float32)),
        dtb_r=C(dtbv.reshape(16, 128).T.astype(np.float32)),
        alog_r=C(alog.reshape(8, 128, NST).transpose(1, 0, 2).reshape(128, 512).astype(np.float32)),
        dpar_r=C(dparf.reshape(16, 128).T.astype(np.float32)),
        nw_r=C(nwf.reshape(16, 128).T.astype(np.float32)),
        nb_r=C(nbf.reshape(16, 128).T.astype(np.float32)),
        woutT=C(wout.T.astype(np.float32)),
        ffnAT=C(A_cat.T.astype(np.float32)),
        ffnBT=C(B_cat.T.astype(np.float32)),
        wrT=C(wr.T.astype(np.float32)),
        br_in=C(br.reshape(1, 8).astype(np.float32)),
        blkid_in=C(np.repeat(np.eye(8, dtype=np.float32), 64, axis=1)),
        idf_in=np.eye(128, dtype=np.float32),
        idb_in=np.eye(128, dtype=np.float32).astype(ml_dtypes.bfloat16),
        cst_in=np.array([[EPS, -0.5, -1.0, 1.0]], np.float32),
        jv_in=np.array([[j]], np.int32),
        kv_in=np.array([[1 - j]], np.int32),
    )


def kernel(**inputs):
    nc = build_nc()
    in_maps = [_prep_inputs(c, inputs) for c in range(8)]
    res = run_bass_kernel_spmd(nc, in_maps, core_ids=list(range(8)))
    out = np.empty((4, TOK, D), np.float32)
    for c in range(8):
        b, j = c // 2, c % 2
        pm = np.arange(D) if j == 0 else np.r_[1024:2048, 0:1024]
        out[b, j * MYT:(j + 1) * MYT, :] = res.results[c]["out_c"][:, pm]
    return out


# version 23
# speedup vs baseline: 1.0410x; 1.0410x over previous
import numpy as np
import ml_dtypes
from contextlib import ExitStack

import concourse.bass as bass
import concourse.mybir as mybir
from concourse.bass_utils import run_bass_kernel_spmd

F32 = mybir.dt.float32
BF16 = mybir.dt.bfloat16
I32 = mybir.dt.int32
AF = mybir.ActivationFunctionType
ALU = mybir.AluOpType
AX = mybir.AxisListType

TOK, D, DS, NST, RHO, ER = 4096, 2048, 1024, 64, 128, 512
TC, NCC = 1024, 4
NSUP = 8
MYT = 2048
EPS = 1e-5
SCL = 0.5 * (1.0 / 64.0)
GP_NS = frozenset(n for n in range(NST) if n % 3 != 0)

ENGS = ("sync", "scalar", "vector", "tensor", "gpsimd")
SEMS = ("dL", "dR", "dS", "cV", "cS", "cP", "cG", "ccm",
        "dRB0", "dRB1", "dRC0", "dRC1", "dSy0", "dSy1", "dSa", "dSb",
        "dG0", "dG1", "dG2", "dG3", "dG4", "dG5", "dG6", "dG7", "dX0", "dX1")


class Prog:
    def __init__(self):
        self.ops = {e: [] for e in ENGS}
        self.cnt = {s: 0 for s in SEMS}

    def add(self, eng, fn, inc=None, k=1):
        self.ops[eng].append(("op", fn, inc, k))
        if inc is not None:
            self.cnt[inc] += k
            return self.cnt[inc]
        return None

    def wait(self, eng, sem, val):
        if val:
            self.ops[eng].append(("wait", sem, int(val)))

    def bar(self):
        return dict(self.cnt)

    def wait_bar(self, eng, bar):
        for s in SEMS:
            self.wait(eng, s, bar[s])


def build_nc(debug=False):
    nc = bass.Bass("TRN2", target_bir_lowering=False, debug=False)
    dt = nc.dram_tensor

    x_in = dt("x_in", [TOK, D], F32, kind="ExternalInput")
    xpwT = dt("xpwT", [D, 256], F32, kind="ExternalInput")
    dtwT = dt("dtwT", [RHO, D], F32, kind="ExternalInput")
    dtb_r = dt("dtb_r", [128, 16], F32, kind="ExternalInput")
    alog_r = dt("alog_r", [128, 512], F32, kind="ExternalInput")
    dpar_r = dt("dpar_r", [128, 16], F32, kind="ExternalInput")
    nw_r = dt("nw_r", [128, 16], F32, kind="ExternalInput")
    nb_r = dt("nb_r", [128, 16], F32, kind="ExternalInput")
    woutT = dt("woutT", [D, D], F32, kind="ExternalInput")
    ffnAT = dt("ffnAT", [D, ER], F32, kind="ExternalInput")
    ffnBT = dt("ffnBT", [ER, D], F32, kind="ExternalInput")
    wrT = dt("wrT", [D, 8], F32, kind="ExternalInput")
    br_in = dt("br_in", [1, 8], F32, kind="ExternalInput")
    blkid_in = dt("blkid_in", [8, ER], F32, kind="ExternalInput")
    idf_in = dt("idf_in", [128, 128], F32, kind="ExternalInput")
    idb_in = dt("idb_in", [128, 128], BF16, kind="ExternalInput")
    cst_in = dt("cst_in", [1, 4], F32, kind="ExternalInput")
    jv_in = dt("jv_in", [1, 1], I32, kind="ExternalInput")
    kv_in = dt("kv_in", [1, 1], I32, kind="ExternalInput")

    out_c = dt("out_c", [MYT, D], F32, kind="ExternalOutput")

    kd = "ExternalOutput" if debug else "Internal"
    x1_scr = dt("x1_scr", [MYT, D], F32, kind=kd)
    xlnbT_scr = dt("xlnbT_scr", [D, MYT], BF16, kind=kd)
    xln2T_scr = dt("xln2T_scr", [D, MYT], F32, kind=kd)
    PhA = dt("PhA", [2176, MYT], F32, kind=kd)
    PhB = dt("PhB", [2176, MYT], F32, kind=kd)
    PgxB = dt("PgxB", [4352, MYT], F32, kind="Internal")
    y_scr = dt("y_scr", [DS, TOK], BF16, kind="Internal")
    y_dbg = dt("y_dbg", [DS, TOK], BF16, kind=kd) if debug else None
    ygx_scr = dt("ygx_scr", [2 * DS, TOK], BF16, kind="Internal")
    wo_b_scr = dt("wo_b_scr", [D, D], BF16, kind="Internal")
    fAT_b_scr = dt("fAT_b_scr", [D, ER], BF16, kind="Internal")
    fBT_b_scr = dt("fBT_b_scr", [ER, D], BF16, kind="Internal")
    xpwT_b_scr = dt("xpwT_b_scr", [D, 256], BF16, kind="Internal")
    dtwT_b_scr = dt("dtwT_b_scr", [RHO, D], BF16, kind="Internal")
    wrT_b_scr = dt("wrT_b_scr", [D, 8], BF16, kind="Internal")

    P = Prog()
    cell = {}

    with ExitStack() as st:
        sb = lambda name, shape, d=F32: st.enter_context(nc.sbuf_tensor(name, shape, d))
        s_jv = sb("s_jv", [1, 2], I32)
        idf = sb("idf", [128, 128], F32)
        idb = sb("idb", [128, 128], BF16)
        blkid = sb("blkid", [8, ER], F32)
        br_b = sb("br_b", [128, 8], F32)
        nw = sb("nw", [128, 16], F32)
        nb = sb("nb", [128, 16], F32)
        dtb = sb("dtb", [128, 16], F32)
        dpar = sb("dpar", [128, 16], F32)
        A_sb = sb("A_sb", [128, 512], F32)
        carry = sb("carry", [128, 512], F32)
        ST = sb("ST", [128, 192], F32)
        MT = sb("MT", [8, ER], F32)
        cst = sb("cst", [128, 4], F32)
        wr_sb = sb("wr_sb", [128, 128], BF16)
        dtw_sb = sb("dtw_sb", [128, D], BF16)
        AR1 = sb("AR1", [128, 16384], F32)
        AR2 = sb("AR2", [128, 16384], F32)
        AR3 = sb("AR3", [128, 8192], F32)
        SM0 = sb("SM0", [128, 2048], F32)
        SM1 = sb("SM1", [128, 2048], F32)
        SM2 = sb("SM2", [128, 2048], F32)
        SM3 = sb("SM3", [128, 2048], F32)
        AR1b = AR1[:, :].bitcast(BF16)
        AR2b = AR2[:, :].bitcast(BF16)
        AR3b = AR3[:, :].bitcast(BF16)
        SM0b = SM0[:, :].bitcast(BF16)
        SM2b = SM2[:, :].bitcast(BF16)
        SM3b = SM3[:, :].bitcast(BF16)
        PS = st.enter_context(nc.psum_tensor("PS", [128, 4096], F32))
        PSb = PS[:, :].bitcast(BF16)
        BANK = lambda k, w=512: PS[:, 512 * k:512 * k + w]
        sems = {s: st.enter_context(nc.semaphore(s)) for s in SEMS}
        block = st.enter_context(nc.Block())

        for dst, src in [
            (s_jv[0:1, 0:1], jv_in[:, :]), (s_jv[0:1, 1:2], kv_in[:, :]),
            (idf[:, :], idf_in[:, :]), (idb[:, :], idb_in[:, :]),
            (blkid[:, :], blkid_in[:, :]),
            (br_b[:, :], br_in[0:1, :].broadcast_to([128, 8])),
            (nw[:, :], nw_r[:, :]), (nb[:, :], nb_r[:, :]),
            (dtb[:, :], dtb_r[:, :]), (dpar[:, :], dpar_r[:, :]),
            (A_sb[:, :], alog_r[:, :]),
            (cst[:, :], cst_in[0:1, :].broadcast_to([128, 4])),
        ]:
            P.add("sync", (lambda d_, s_: (lambda q: q.dma_start(out=d_, in_=s_)))(dst, src), "dL", 16)
        v_glob = P.cnt["dL"]

        def _ld_regs(q):
            j = nc.values_load(s_jv[0:1, 0:1], engines=[q.engine], min_val=0,
                               max_val=1, skip_runtime_bounds_check=True)
            k = nc.values_load(s_jv[0:1, 1:2], engines=[q.engine], min_val=0,
                               max_val=1, skip_runtime_bounds_check=True)
            cell["jj"] = j * 1
            cell["kk2"] = k + 2
            cell["tsl"] = [(j * 4 + tp) * 512 for tp in range(4)]
            cell["k8"] = k * 8
            return None
        P.wait("sync", "dL", v_glob)
        P.add("sync", _ld_regs)

        P.wait("scalar", "dL", v_glob)
        vA1 = P.add("scalar", lambda q: q.activation(out=A_sb[:, :], in_=A_sb[:, :], func=AF.Exp), "cS")
        P.wait("vector", "cS", vA1)
        P.add("vector", lambda q: q.tensor_scalar(out=A_sb[:, :], in0=A_sb[:, :], scalar1=-1.0,
                                                  scalar2=None, op0=ALU.mult), "cV")

        jobs = []
        for r in range(16):
            jobs.append((wrT[r * 128:(r + 1) * 128, :], wrT_b_scr[r * 128:(r + 1) * 128, :], 8, None))
        for r in range(16):
            jobs.append((ffnAT[r * 128:(r + 1) * 128, :], fAT_b_scr[r * 128:(r + 1) * 128, :], ER, None))
        for r in range(4):
            jobs.append((ffnBT[r * 128:(r + 1) * 128, :], fBT_b_scr[r * 128:(r + 1) * 128, :], D, None))
        for r in range(16):
            jobs.append((xpwT[r * 128:(r + 1) * 128, :], xpwT_b_scr[r * 128:(r + 1) * 128, :], 256, None))
        jobs.append((dtwT[:, :], dtwT_b_scr[:, :], D, None))
        for r in range(16):
            jobs.append((woutT[r * 128:(r + 1) * 128, :], wo_b_scr[r * 128:(r + 1) * 128, :], D, None))
        stg_in = [SM0, SM1]
        stg_out = [SM2b, SM3b]
        v_st_last = {0: 0, 1: 0}
        for i, (jsrc, dst, cols, wchunk) in enumerate(jobs):
            par = i % 2
            sld = "dRB%d" % par
            sst = "dSa" if par == 0 else "dSb"
            P.wait("sync", sst, v_st_last[par])
            vl = P.add("sync", (lambda s_, c_, p_: (lambda q: q.dma_start(out=stg_in[p_][:, 0:c_], in_=s_)))(jsrc, cols, par), sld, 16)
            P.wait("vector", sld, vl)
            P.wait("vector", sst, v_st_last[par])
            vc = P.add("vector", (lambda c_, p_: (lambda q: q.tensor_copy(out=stg_out[p_][:, 0:c_], in_=stg_in[p_][:, 0:c_])))(cols, par), "cV")
            P.wait("sync", "cV", vc)
            vs = P.add("sync", (lambda d_, c_, p_: (lambda q: q.dma_start(out=d_, in_=stg_out[p_][:, 0:c_])))(dst, cols, par), sst, 16)
            v_st_last[par] = vs
        bar0 = P.bar()

        fA = AR3b[:, 0:8192]
        fB = AR3b[:, 8192:16384]
        pst = {"tr_evac": 0, "router_c": 0, "mscT_c": 0, "maskmm": 0, "msk_ev": 0,
               "hm": 0, "y_ev": 0, "trf": 0, "mt_rd": 0, "v_xlb_2a": [0, 0]}

        def emit_ffn_weights_load():
            vv = 0
            for r in range(16):
                vv = P.add("sync", (lambda r_: (lambda q: q.dma_start(
                    out=fA[:, r_ * 512:(r_ + 1) * 512], in_=fAT_b_scr[r_ * 128:(r_ + 1) * 128, :])))(r), "dL", 16)
            for r in range(4):
                vv = P.add("sync", (lambda r_: (lambda q: q.dma_start(
                    out=fB[:, r_ * 2048:(r_ + 1) * 2048], in_=fBT_b_scr[r_ * 128:(r_ + 1) * 128, :])))(r), "dL", 16)
            vv = P.add("sync", lambda q: q.dma_start(
                out=wr_sb[:, :].rearrange("p (c e) -> p c e", c=16),
                in_=wrT_b_scr[:, :].rearrange("(c p) e -> p c e", p=128)), "dL", 16)
            return vv

        BG = {2: {"cV": 0, "cS": 0}, 3: {"cV": 0, "cS": 0}}

        def bank_wait(k):
            for sem, val in BG[k].items():
                P.wait("tensor", sem, val)

        def emit_ffn_sup(sup, x_tiles, out_dst, fstate, tile_dma=None, v_tiles_ready=0):
            xlnT = AR2b[:, 0:8192]
            msk = AR2[:, 4096:6144]
            hm = AR2b[:, 12288:14336]
            ys = [AR2[:, 7168:7680], AR2[:, 7680:8192]]
            xlb = [SM0b[:, 0:2048], SM0b[:, 2048:4096]]
            sq = SM2
            v10_par = pst.setdefault("v10_par", [0, 0])
            v15_par = pst.setdefault("v15_par", [0, 0])
            vxh = [0, 0]
            if tile_dma is not None:
                for hf in range(2):
                    for sem, val in fstate.get("tile_free_h", [[], []])[hf]:
                        P.wait("sync", sem, val)
                    P.wait("sync", "dS", fstate.get("st_tile", 0))
                    vxh[hf] = P.add("sync", (lambda s_, hf_: (lambda q: tile_dma(q, s_, hf_)))(sup, hf), "dX%d" % hf, 16)
            for tt in range(4):
                pp = tt % 2
                base = 64 * pp
                col = lambda k: ST[:, base + k:base + k + 1]
                vx = vxh[tt // 2] if tile_dma is not None else 0
                sx = "dX%d" % (tt // 2)
                xt = x_tiles[tt]
                P.wait("vector", "cS", v10_par[pp])
                P.wait("vector", "cP", v15_par[pp])
                if v_tiles_ready and tile_dma is None:
                    P.wait("vector", "cV", v_tiles_ready)
                if vx:
                    P.wait("vector", sx, vx)
                v1 = P.add("vector", (lambda xt_, c_: (lambda q: q.reduce_sum(c_, xt_, axis=AX.X)))(xt, col(0)), "cV")
                if vx:
                    P.wait("scalar", sx, vx)
                if v_tiles_ready and tile_dma is None:
                    P.wait("scalar", "cV", v_tiles_ready)
                    P.wait("tensor", "cV", v_tiles_ready)
                v2 = P.add("scalar", (lambda xt_, c_: (lambda q: q.activation(out=sq[:, :], in_=xt_, func=AF.Square,
                                                                              accum_out=c_)))(xt, col(1)), "cS")
                P.wait("vector", "cS", v2)
                P.wait("vector", "cV", v1)
                vv = P.add("vector", (lambda a, b: (lambda q: q.tensor_scalar(out=a, in0=b, scalar1=1.0 / D, scalar2=None, op0=ALU.mult)))(col(2), col(0)), "cV")
                vv = P.add("vector", (lambda a, b: (lambda q: q.tensor_scalar(out=a, in0=b, scalar1=1.0 / D, scalar2=None, op0=ALU.mult)))(col(3), col(1)), "cV")
                P.wait("vector", "cV", vv)
                vv = P.add("vector", (lambda a, b: (lambda q: q.tensor_tensor(out=a, in0=b, in1=b, op=ALU.mult)))(col(4), col(2)), "cV")
                P.wait("vector", "cV", vv)
                v3 = P.add("vector", (lambda a, b, c: (lambda q: q.tensor_tensor(out=a, in0=b, in1=c, op=ALU.subtract)))(col(5), col(3), col(4)), "cV")
                P.wait("scalar", "cV", v3)
                vl = P.add("scalar", (lambda a, b: (lambda q: q.activation(out=a, in_=b, func=AF.Ln, bias=cst[:, 0:1])))(col(6), col(5)), "cS")
                P.wait("scalar", "cS", vl)
                v4 = P.add("scalar", (lambda a, b: (lambda q: q.activation(out=a, in_=b, func=AF.Exp, scale=cst[:, 1:2])))(col(7), col(6)), "cS")
                P.wait("vector", "cS", v4)
                v5 = P.add("vector", (lambda a, b, c: (lambda q: q.tensor_scalar(out=a, in0=b, scalar1=c,
                                                                                 scalar2=-1.0, op0=ALU.mult, op1=ALU.mult)))(col(8), col(2), col(7)), "cV")
                P.wait("scalar", "cV", v5)
                P.wait("scalar", "cP", pst["v_xlb"][pp])
                v6 = P.add("scalar", (lambda xt_, p_, c7, c8: (lambda q: q.activation(out=xlb[p_][:, :], in_=xt_, func=AF.Identity,
                                                                                      scale=c7, bias=c8)))(xt, pp, col(7), col(8)), "cS")
                for g in range(4):
                    P.wait("tensor", "cS", v6)
                    P.wait("tensor", "cV", pst["tr_evac"])
                    for c4 in range(4):
                        c = g * 4 + c4
                        P.add("tensor", (lambda c_, c4_, p_, g_: (lambda q: q.transpose(
                            PSb[:, (g_ % 2) * 1024 + c4_ * 128:(g_ % 2) * 1024 + (c4_ + 1) * 128],
                            xlb[p_][:, c_ * 128:(c_ + 1) * 128], idb[:, :])))(c, c4, pp, g), "cP")
                    vp = P.cnt["cP"]
                    P.wait("vector", "cP", vp)
                    pst["tr_evac"] = P.add("vector", (lambda g_, tt_: (lambda q: q.tensor_copy(
                        out=xlnT[:, :].rearrange("p (c t) -> p c t", c=16)[:, g_ * 4:(g_ + 1) * 4, tt_ * 128:(tt_ + 1) * 128],
                        in_=PSb[:, (g_ % 2) * 1024:(g_ % 2) * 1024 + 512].rearrange("p (c t) -> p c t", c=4))))(g, tt), "cV")
                pst["v_xlb"][pp] = P.cnt["cP"]
                P.wait("tensor", "cV", pst["tr_evac"])
                bank_wait(2)
                for c in range(16):
                    P.add("tensor", (lambda c_, tt_: (lambda q: q.matmul(
                        PS[:, 1024:1032], xlnT[:, c_ * 512 + tt_ * 128: c_ * 512 + (tt_ + 1) * 128],
                        wr_sb[:, c_ * 8:(c_ + 1) * 8], start=(c_ == 0), stop=(c_ == 15))))(c, tt), "cP")
                vr = P.cnt["cP"]
                P.wait("vector", "cP", vr)
                vrc = P.add("vector", (lambda c16: (lambda q: q.tensor_tensor(out=ST[:, c16:c16 + 8], in0=br_b[:, :],
                                                                              in1=PS[:, 1024:1032], op=ALU.add)))(base + 16), "cV")
                BG[2]["cV"] = vrc
                P.wait("vector", "cV", vrc)
                v8 = P.add("vector", (lambda a, c16: (lambda q: q.reduce_max(a, ST[:, c16:c16 + 8], axis=AX.X)))(col(9), base + 16), "cV")
                P.wait("vector", "cV", v8)
                v9 = P.add("vector", (lambda a, b: (lambda q: q.tensor_scalar(out=a, in0=b, scalar1=-1.0, scalar2=None, op0=ALU.mult)))(col(10), col(9)), "cV")
                P.wait("scalar", "cV", v9)
                v10 = P.add("scalar", (lambda c16, c24, c10: (lambda q: q.activation(out=ST[:, c24:c24 + 8], in_=ST[:, c16:c16 + 8],
                                                                                     func=AF.Exp, bias=c10)))(base + 16, base + 24, col(10)), "cS")
                v10_par[pp] = v10
                P.wait("vector", "cS", v10)
                v11 = P.add("vector", (lambda a, c24: (lambda q: q.reduce_sum(a, ST[:, c24:c24 + 8], axis=AX.X)))(col(11), base + 24), "cV")
                P.wait("vector", "cV", v11)
                vr2 = P.add("vector", (lambda a, b: (lambda q: q.reciprocal(a, b)))(col(12), col(11)), "cV")
                P.wait("vector", "cV", vr2)
                v12 = P.add("vector", (lambda a, b: (lambda q: q.tensor_scalar(out=a, in0=b, scalar1=SCL, scalar2=None, op0=ALU.mult)))(col(13), col(12)), "cV")
                P.wait("vector", "cV", v12)
                v13 = P.add("vector", (lambda c16, c32, c9: (lambda q: q.tensor_scalar(out=ST[:, c32:c32 + 8], in0=ST[:, c16:c16 + 8],
                                                                                       scalar1=c9, scalar2=None, op0=ALU.is_equal)))(base + 16, base + 32, col(9)), "cV")
                P.wait("vector", "cV", v13)
                v14 = P.add("vector", (lambda c32, c40, c13: (lambda q: q.tensor_scalar(out=ST[:, c40:c40 + 8], in0=ST[:, c32:c32 + 8],
                                                                                        scalar1=c13, scalar2=None, op0=ALU.mult)))(base + 32, base + 40, col(13)), "cV")
                P.wait("tensor", "cV", v14)
                P.wait("tensor", "cS", pst["mscT_c"])
                v15 = P.add("tensor", (lambda c40: (lambda q: q.transpose(PS[0:8, 3584:3712], ST[:, c40:c40 + 8], idf[:, :])))(base + 40), "cP")
                v15_par[pp] = v15
                P.wait("scalar", "cP", v15)
                P.wait("scalar", "cP", pst["mt_rd"])
                pst["mscT_c"] = P.add("scalar", (lambda tt_: (lambda q: q.copy(out=MT[0:8, tt_ * 128:(tt_ + 1) * 128],
                                                                               in_=PS[0:8, 3584:3712])))(tt), "cS")
            for c4 in range(4):
                P.wait("tensor", "cS", pst["mscT_c"])
                bank_wait(3)
                P.add("tensor", (lambda c4_: (lambda q: q.matmul(BANK(3), blkid[0:8, c4_ * 128:(c4_ + 1) * 128],
                                                                 MT[0:8, :], start=True, stop=True)))(c4), "cP")
                vp = P.cnt["cP"]
                P.wait("scalar", "cP", vp)
                P.wait("scalar", "cV", pst["hm"])
                pst["msk_ev"] = P.add("scalar", (lambda c4_: (lambda q: q.copy(out=msk[:, c4_ * 512:(c4_ + 1) * 512],
                                                                               in_=BANK(3))))(c4), "cS")
                BG[3]["cS"] = pst["msk_ev"]
            pst["mt_rd"] = P.cnt["cP"]
            v_hm_bank = pst.setdefault("v_hm_bank", [0, 0])
            for ec in range(4):
                bk = 4 if ec % 2 == 0 else 2
                P.wait("tensor", "cS", pst["msk_ev"])
                if bk == 2:
                    bank_wait(2)
                else:
                    P.wait("tensor", "cV", v_hm_bank[0])
                for c in range(16):
                    P.add("tensor", (lambda c_, ec_, bk_: (lambda q: q.matmul(
                        BANK(bk_), fA[:, c_ * 512 + ec_ * 128: c_ * 512 + (ec_ + 1) * 128],
                        xlnT[:, c_ * 512:(c_ + 1) * 512], start=(c_ == 0), stop=(c_ == 15))))(c, ec, bk), "cP")
                vp = P.cnt["cP"]
                P.wait("vector", "cP", vp)
                v_hm_bank[ec % 2] = P.add("vector", (lambda ec_, bk_: (lambda q: q.tensor_tensor(
                    out=hm[:, ec_ * 512:(ec_ + 1) * 512], in0=BANK(bk_), in1=msk[:, ec_ * 512:(ec_ + 1) * 512],
                    op=ALU.mult)))(ec, bk), "cV")
                if bk == 2:
                    BG[2]["cV"] = v_hm_bank[1]
            pst["hm"] = v_hm_bank[1]
            v_yev = {}
            v_tr_done = {}
            for oc in range(17):
                if oc < 16:
                    bk = 5 if oc % 2 == 0 else 3
                    P.wait("tensor", "cV", pst["hm"])
                    if bk == 3:
                        bank_wait(3)
                    else:
                        P.wait("tensor", "cS", v_yev.get(oc - 2, 0))
                    for ec in range(4):
                        P.add("tensor", (lambda oc_, ec_, bk_: (lambda q: q.matmul(
                            BANK(bk_), fB[:, ec_ * 2048 + oc_ * 128: ec_ * 2048 + (oc_ + 1) * 128],
                            hm[:, ec_ * 512:(ec_ + 1) * 512], start=(ec_ == 0), stop=(ec_ == 3))))(oc, ec, bk), "cP")
                    vmm = P.cnt["cP"]
                    P.wait("scalar", "cP", vmm)
                    v_yev[oc] = P.add("scalar", (lambda bk_, oc_: (lambda q: q.copy(out=ys[oc_ % 2][:, :], in_=BANK(bk_))))(bk, oc), "cS")
                    if bk == 3:
                        BG[3]["cS"] = v_yev[oc]
                ot = oc - 1
                if 0 <= ot < 16:
                    P.wait("tensor", "cS", v_yev[ot])
                    P.wait("tensor", "cV", pst["trf"])
                    for tt in range(4):
                        P.add("tensor", (lambda tt_, ot_: (lambda q: q.transpose(
                            BANK(6)[:, tt_ * 128:(tt_ + 1) * 128], ys[ot_ % 2][:, tt_ * 128:(tt_ + 1) * 128], idf[:, :])))(tt, ot), "cP")
                    v_tr_done[ot] = P.cnt["cP"]
                    P.wait("vector", "cP", v_tr_done[ot])
                    for tt in range(4):
                        pst["trf"] = P.add("vector", (lambda tt_, oc_: (lambda q: q.tensor_tensor(
                            out=x_tiles[tt_][:, oc_ * 128:(oc_ + 1) * 128], in0=x_tiles[tt_][:, oc_ * 128:(oc_ + 1) * 128],
                            in1=BANK(6)[:, tt_ * 128:(tt_ + 1) * 128], op=ALU.add)))(tt, ot), "cV")
            if out_dst is not None:
                P.wait("sync", "cV", pst["trf"])
                fstate["st_tile"] = P.add("sync", (lambda d_: (lambda q: q.dma_start(
                    out=d_, in_=AR1[:, 0:8192].rearrange("p (a d) -> p a d", a=4))))(out_dst), "dS", 16)
            return pst["trf"]

        P.wait_bar("sync", bar0)
        vw1 = emit_ffn_weights_load()
        for e in ("scalar", "vector", "tensor"):
            P.wait_bar(e, bar0)
            P.wait(e, "dL", vw1)
        x_tiles = [AR1[:, k * 2048:(k + 1) * 2048] for k in range(4)]
        pst["v_xlb"] = [0, 0]

        def x_tile_dma(q, sup, hf):
            return q.dma_start(
                out=AR1[:, hf * 4096:(hf + 1) * 4096].rearrange("p (j a d) -> p j a d", j=1, a=2),
                in_=x_in[:, :].rearrange("(j a p) d -> p j a d", j=2, p=128)[
                    :, bass.DynSlice(cell["jj"], 1), sup * 4 + hf * 2:sup * 4 + hf * 2 + 2, :])

        xf = SM1
        xf2 = AR2[:, 8192:10240]
        xb = SM3b[:, 0:2048]
        sqd = AR2[:, 10240:12288]
        xlnbT = AR2b[:, 24576:32768]
        xln2T_sb = AR1[:, 8192:16384]
        v13_par = [0, 0]
        v_x1_rd = [0] * 4
        v_xf_rd = 0
        v_evb = 0
        v_evf = 0
        v_2a_st = 0
        fstate = {"tile_free": [], "st_tile": 0}

        for sup in range(4):
            vtrf = emit_ffn_sup(sup, x_tiles,
                         x1_scr[:, :].rearrange("(a p) d -> p a d", p=128)[:, sup * 4:sup * 4 + 4, :],
                         fstate, tile_dma=x_tile_dma)
            P.wait("scalar", "cV", vtrf)
            P.wait("vector", "cV", vtrf)
            P.wait("vector", "dSy0", v_2a_st)
            P.wait("scalar", "dSy0", v_2a_st)
            for tt in range(4):
                ti = sup * 4 + tt
                pp = tt % 2
                base = 128 + 32 * pp
                col = lambda k: ST[:, base + k:base + k + 1]
                x1t = x_tiles[tt]
                P.wait("vector", "cS", v13_par[pp])
                v1a = P.add("vector", (lambda x_, c_: (lambda q: q.reduce_sum(c_, x_, axis=AX.X)))(x1t, col(0)), "cV")
                v2 = P.add("scalar", (lambda x_, c_: (lambda q: q.activation(out=sqd[:, :], in_=x_, func=AF.Square, accum_out=c_)))(x1t, col(1)), "cS")
                P.wait("vector", "cS", v2)
                P.wait("vector", "cV", v1a)
                vv = P.add("vector", (lambda a, b: (lambda q: q.tensor_scalar(out=a, in0=b, scalar1=1.0 / D, scalar2=None, op0=ALU.mult)))(col(2), col(0)), "cV")
                vv = P.add("vector", (lambda a, b: (lambda q: q.tensor_scalar(out=a, in0=b, scalar1=1.0 / D, scalar2=None, op0=ALU.mult)))(col(3), col(1)), "cV")
                P.wait("vector", "cV", vv)
                vv = P.add("vector", (lambda a, b: (lambda q: q.tensor_tensor(out=a, in0=b, in1=b, op=ALU.mult)))(col(4), col(2)), "cV")
                P.wait("vector", "cV", vv)
                v3 = P.add("vector", (lambda a, b, c: (lambda q: q.tensor_tensor(out=a, in0=b, in1=c, op=ALU.subtract)))(col(5), col(3), col(4)), "cV")
                P.wait("scalar", "cV", v3)
                vl = P.add("scalar", (lambda a, b: (lambda q: q.activation(out=a, in_=b, func=AF.Ln, bias=cst[:, 0:1])))(col(6), col(5)), "cS")
                P.wait("scalar", "cS", vl)
                v4 = P.add("scalar", (lambda a, b: (lambda q: q.activation(out=a, in_=b, func=AF.Exp, scale=cst[:, 1:2])))(col(7), col(6)), "cS")
                P.wait("vector", "cS", v4)
                v5 = P.add("vector", (lambda a, b, c: (lambda q: q.tensor_scalar(out=a, in0=b, scalar1=c, scalar2=-1.0,
                                                                                 op0=ALU.mult, op1=ALU.mult)))(col(8), col(2), col(7)), "cV")
                P.wait("scalar", "cV", v5)
                P.wait("scalar", "cS", v_xf_rd)
                v6 = P.add("scalar", (lambda x_, c7, c8: (lambda q: q.activation(out=xf[:, :], in_=x_, func=AF.Identity,
                                                                                 scale=c7, bias=c8)))(x1t, col(7), col(8)), "cS")
                v_x1_rd[tt] = v6
                P.wait("vector", "cS", v6)
                v7a = P.add("vector", (lambda c_: (lambda q: q.reduce_sum(c_, xf[:, :], axis=AX.X)))(col(9)), "cV")
                P.wait("scalar", "cS", v6)
                v8 = P.add("scalar", (lambda c_: (lambda q: q.activation(out=sqd[:, :], in_=xf[:, :], func=AF.Square, accum_out=c_)))(col(10)), "cS")
                P.wait("vector", "cS", v8)
                P.wait("vector", "cV", v7a)
                vv = P.add("vector", (lambda a, b: (lambda q: q.tensor_scalar(out=a, in0=b, scalar1=1.0 / D, scalar2=None, op0=ALU.mult)))(col(11), col(9)), "cV")
                vv = P.add("vector", (lambda a, b: (lambda q: q.tensor_scalar(out=a, in0=b, scalar1=1.0 / D, scalar2=None, op0=ALU.mult)))(col(12), col(10)), "cV")
                P.wait("vector", "cV", vv)
                vv = P.add("vector", (lambda a, b: (lambda q: q.tensor_tensor(out=a, in0=b, in1=b, op=ALU.mult)))(col(13), col(11)), "cV")
                P.wait("vector", "cV", vv)
                v9 = P.add("vector", (lambda a, b, c: (lambda q: q.tensor_tensor(out=a, in0=b, in1=c, op=ALU.subtract)))(col(14), col(12), col(13)), "cV")
                P.wait("scalar", "cV", v9)
                vl = P.add("scalar", (lambda a, b: (lambda q: q.activation(out=a, in_=b, func=AF.Ln, bias=cst[:, 0:1])))(col(15), col(14)), "cS")
                P.wait("scalar", "cS", vl)
                v10 = P.add("scalar", (lambda a, b: (lambda q: q.activation(out=a, in_=b, func=AF.Exp, scale=cst[:, 1:2])))(col(16), col(15)), "cS")
                P.wait("vector", "cS", v10)
                v11 = P.add("vector", (lambda a, b, c: (lambda q: q.tensor_scalar(out=a, in0=b, scalar1=c, scalar2=-1.0,
                                                                                  op0=ALU.mult, op1=ALU.mult)))(col(17), col(11), col(16)), "cV")
                P.wait("scalar", "cV", v11)
                P.wait("scalar", "cP", pst["v_xlb_2a"][0])
                v12 = P.add("scalar", lambda q: q.activation(out=xb[:, :], in_=xf[:, :], func=AF.Identity), "cS")
                P.wait("scalar", "cP", pst["v_xlb_2a"][1])
                v13 = P.add("scalar", (lambda c16, c17: (lambda q: q.activation(out=xf2[:, :], in_=xf[:, :], func=AF.Identity,
                                                                                scale=c16, bias=c17)))(col(16), col(17)), "cS")
                v13_par[pp] = v13
                v_xf_rd = v13
                for g in range(4):
                    P.wait("tensor", "cS", v12)
                    P.wait("tensor", "cV", pst["tr_evac"])
                    for c4 in range(4):
                        c = g * 4 + c4
                        P.add("tensor", (lambda c_, c4_, g_: (lambda q: q.transpose(
                            PSb[:, (g_ % 2) * 1024 + c4_ * 128:(g_ % 2) * 1024 + (c4_ + 1) * 128],
                            xb[:, c_ * 128:(c_ + 1) * 128], idb[:, :])))(c, c4, g), "cP")
                    vp = P.cnt["cP"]
                    P.wait("vector", "cP", vp)
                    pst["tr_evac"] = P.add("vector", (lambda g_, tt_: (lambda q: q.tensor_copy(
                        out=xlnbT[:, :].rearrange("p (c t) -> p c t", c=16)[:, g_ * 4:(g_ + 1) * 4, tt_ * 128:(tt_ + 1) * 128],
                        in_=PSb[:, (g_ % 2) * 1024:(g_ % 2) * 1024 + 512].rearrange("p (c t) -> p c t", c=4))))(g, tt), "cV")
                pst["v_xlb_2a"][0] = P.cnt["cP"]
                for g in range(8):
                    P.wait("tensor", "cS", v13)
                    bk = 2 + g % 2
                    bank_wait(bk)
                    for c2 in range(2):
                        c = g * 2 + c2
                        P.add("tensor", (lambda c_, c2_, bk_: (lambda q: q.transpose(
                            BANK(bk_)[:, c2_ * 128:(c2_ + 1) * 128],
                            xf2[:, c_ * 128:(c_ + 1) * 128], idf[:, :])))(c, c2, bk), "cP")
                    vp = P.cnt["cP"]
                    P.wait("scalar", "cP", vp)
                    vev = P.add("scalar", (lambda g_, tt_, bk_: (lambda q: q.copy(
                        out=xln2T_sb[:, :].rearrange("p (c t) -> p c t", c=16)[:, g_ * 2:(g_ + 1) * 2, tt_ * 128:(tt_ + 1) * 128],
                        in_=BANK(bk_, 256).rearrange("p (c t) -> p c t", c=2))))(g, tt, bk), "cS")
                    BG[bk]["cS"] = vev
                pst["v_xlb_2a"][1] = P.cnt["cP"]
            P.wait("sync", "cV", pst["tr_evac"])
            P.wait("sync", "cS", BG[2]["cS"])
            P.wait("sync", "cS", BG[3]["cS"])
            P.wait("sync", "dSy0", v_2a_st)
            P.add("sync", (lambda s_: (lambda q: q.dma_start(
                out=xlnbT_scr[:, :].rearrange("(c p) t -> p c t", p=128)[:, :, s_ * 512:(s_ + 1) * 512],
                in_=xlnbT[:, :].rearrange("p (c t) -> p c t", c=16))))(sup), "dSy0", 16)
            v_2a_st = P.add("sync", (lambda s_: (lambda q: q.dma_start(
                out=xln2T_scr[:, :].rearrange("(c p) t -> p c t", p=128)[:, :, s_ * 512:(s_ + 1) * 512],
                in_=xln2T_sb[:, :].rearrange("p (c t) -> p c t", c=16))))(sup), "dSy0", 16)
            fstate["tile_free_h"] = [
                [("cV", pst["trf"]), ("cS", max(v_x1_rd[0], v_x1_rd[1]))],
                [("cV", pst["trf"]), ("cS", max(v_x1_rd[2], v_x1_rd[3]))],
            ]
        P.wait("sync", "dSy0", P.cnt["dSy0"])
        bar2a = P.bar()

        for e in ENGS[:4]:
            P.wait_bar(e, bar2a)
        xnb = AR1b
        chunkbuf = AR3[:, 2048:4096]
        xpw_sb = AR3b[:, 12288:16384]
        projd = AR3b[:, 0:2048]
        proj_sb = AR2[:, 0:4096]
        v_pw = P.add("sync", lambda q: q.dma_start(
            out=xpw_sb[:, :].rearrange("p (c e) -> p c e", c=16),
            in_=xpwT_b_scr[:, :].rearrange("(c p) e -> p c e", p=128)), "dR", 16)
        v_dwl = P.add("sync", lambda q: q.dma_start(out=dtw_sb[:, :], in_=dtwT_b_scr[:, :]), "dR", 16)
        v_aff_last = 0
        v_mm_half = {0: 0, 1: 0}
        v_ev_last = 0
        for half in range(2):
            for i in range(8):
                c = half * 8 + i
                vx = P.add("sync", (lambda c_: (lambda q: q.dma_start(
                    out=chunkbuf[:, :], in_=xln2T_scr[c_ * 128:(c_ + 1) * 128, :])))(c), "dL", 16)
                P.wait("scalar", "dL", vx)
                if half == 1:
                    P.wait("scalar", "cP", v_mm_half[0])
                v_aff_last = P.add("scalar", (lambda c_, i_: (lambda q: q.activation(
                    out=xnb[:, i_ * 2048:(i_ + 1) * 2048], in_=chunkbuf[:, :], func=AF.Identity,
                    scale=nw[:, c_:c_ + 1], bias=nb[:, c_:c_ + 1])))(c, i), "cS")
                P.wait("sync", "cS", v_aff_last)
            for po in range(2):
                for tp2 in range(4):
                    P.wait("tensor", "cS", v_aff_last)
                    P.wait("tensor", "dR", max(v_pw, v_dwl))
                    P.wait("tensor", "cV", v_ev_last)
                    for i in range(8):
                        P.add("tensor", (lambda po_, tp_, i_, h_: (lambda q: q.matmul(
                            PS[:, 0:512],
                            xpw_sb[:, (h_ * 8 + i_) * 256 + po_ * 128:(h_ * 8 + i_) * 256 + (po_ + 1) * 128],
                            xnb[:, i_ * 2048 + tp_ * 512:i_ * 2048 + (tp_ + 1) * 512],
                            start=(i_ == 0), stop=(i_ == 7))))(po, tp2, i, half), "cP")
                    vp = P.cnt["cP"]
                    P.wait("vector", "cP", vp)
                    if half == 0:
                        v_ev_last = P.add("vector", (lambda po_, tp_: (lambda q: q.tensor_copy(
                            out=proj_sb[:, po_ * 2048 + tp_ * 512: po_ * 2048 + (tp_ + 1) * 512],
                            in_=PS[:, 0:512])))(po, tp2), "cV")
                    else:
                        v_ev_last = P.add("vector", (lambda po_, tp_: (lambda q: q.tensor_tensor(
                            out=proj_sb[:, po_ * 2048 + tp_ * 512: po_ * 2048 + (tp_ + 1) * 512],
                            in0=proj_sb[:, po_ * 2048 + tp_ * 512: po_ * 2048 + (tp_ + 1) * 512],
                            in1=PS[:, 0:512], op=ALU.add)))(po, tp2), "cV")
            v_mm_half[half] = P.cnt["cP"]
        P.wait("vector", "cV", v_ev_last)
        v_pd = P.add("vector", lambda q: q.tensor_copy(out=projd[:, :], in_=proj_sb[:, 0:2048]), "cV")
        P.wait("sync", "cV", v_ev_last)
        v_bc = P.add("sync", lambda q: q.dma_start(out=PhA[2048:2176, :], in_=proj_sb[:, 2048:4096]), "dS", 16)
        v_bcB = P.add("sync", lambda q: q.dma_start(out=PhB[2048:2176, :], in_=proj_sb[:, 2048:4096]), "dS", 16)
        xnf = AR1[:, 0:2048]
        t1 = AR1[:, 4096:5120]
        t2 = AR1[:, 5120:6144]
        dpc = [SM2[:, 0:1024], SM2[:, 1024:2048]]
        wpc = [SM3[:, 0:1024], SM3[:, 1024:2048]]
        v_dst = {0: 0, 1: 0}
        v_sp = {0: 0, 1: 0}
        v_vd_prev = 0
        v_xnf_rd = 0
        v_cc2 = 0
        for di, dblk in enumerate(list(range(8, 16)) + list(range(8))):
            if di == 8:
                P.wait("gpsimd", "dSa", P.cnt["dSa"])
                P.wait("gpsimd", "dSb", P.cnt["dSb"])
                P.wait("gpsimd", "dS", v_bcB)
                v_cc2 = P.add("gpsimd", lambda q: q.collective_compute(
                    "AllGather", ALU.bypass, replica_groups=[[0, 1], [2, 3], [4, 5], [6, 7]],
                    ins=[PhB.ap().opt()], outs=[PgxB.ap().opt()]), "ccm")
            P.wait("sync", "cV", v_xnf_rd)
            vx = P.add("sync", (lambda d_: (lambda q: q.dma_start(
                out=chunkbuf[:, :], in_=xln2T_scr[d_ * 128:(d_ + 1) * 128, :])))(dblk), "dL", 16)
            P.wait("scalar", "dL", vx)
            P.wait("scalar", "cP", P.cnt["cP"] if di == 0 else 0)
            v_xnf = P.add("scalar", (lambda d_: (lambda q: q.activation(
                out=xnf[:, :], in_=chunkbuf[:, :], func=AF.Identity,
                scale=nw[:, d_:d_ + 1], bias=nb[:, d_:d_ + 1])))(dblk), "cS")
            P.wait("sync", "cS", v_xnf)
            for tp in range(2):
                par = tp % 2
                pslice = PS[:, 1024 + par * 1024: 2048 + par * 1024]
                P.wait("tensor", "cV", v_pd)
                P.wait("tensor", "cS", v_sp[par])
                for hh in range(2):
                    P.add("tensor", (lambda d_, tp_, ps_, hh_: (lambda q: q.matmul(
                        ps_[:, hh_ * 512:(hh_ + 1) * 512], dtw_sb[:, d_ * 128:(d_ + 1) * 128],
                        projd[:, tp_ * 1024 + hh_ * 512:tp_ * 1024 + (hh_ + 1) * 512],
                        start=True, stop=True)))(dblk, tp, pslice, hh), "cP")
                vp = P.cnt["cP"]
                P.wait("scalar", "cP", vp)
                P.wait("scalar", "cV", v_vd_prev)
                P.wait("scalar", "dS%s" % ("a" if par == 0 else "b"), v_dst[par])
                va = P.add("scalar", (lambda d_, ps_: (lambda q: q.activation(out=t1[:, :], in_=ps_, func=AF.Abs,
                                                                         bias=dtb[:, d_:d_ + 1])))(dblk, pslice), "cS")
                P.wait("scalar", "cS", va)
                va = P.add("scalar", lambda q: q.activation(out=t2[:, :], in_=t1[:, :], func=AF.Exp, scale=cst[:, 2:3]), "cS")
                P.wait("scalar", "cS", va)
                va = P.add("scalar", lambda q: q.activation(out=t1[:, :], in_=t2[:, :], func=AF.Ln, bias=cst[:, 3:4]), "cS")
                P.wait("scalar", "cS", va)
                v_sp[par] = P.add("scalar", (lambda d_, ps_, par_: (lambda q: q.activation(
                    out=dpc[par_][:, :], in_=ps_, func=AF.Relu, bias=dtb[:, d_:d_ + 1])))(dblk, pslice, par), "cS")
                P.wait("vector", "cS", v_sp[par])
                v_vd_prev = P.add("vector", (lambda par_: (lambda q: q.tensor_tensor(
                    out=dpc[par_][:, :], in0=dpc[par_][:, :], in1=t1[:, :], op=ALU.add)))(par), "cV")
                P.wait("vector", "dS%s" % ("a" if par == 0 else "b"), v_dst[par])
                P.wait("vector", "cV", v_vd_prev)
                vw = P.add("vector", (lambda par_, tp_: (lambda q: q.tensor_tensor(
                    out=wpc[par_][:, :], in0=dpc[par_][:, :], in1=xnf[:, tp_ * 1024:(tp_ + 1) * 1024],
                    op=ALU.mult)))(par, tp), "cV")
                if tp == 1:
                    v_xnf_rd = vw
                P.wait("sync", "cV", vw)
                spar = "dSa" if par == 0 else "dSb"
                tgt = PhB if dblk >= 8 else PhA
                gb = (dblk % 8) * 128
                P.add("sync", (lambda t_, g_, tp_, par_: (lambda q: q.dma_start(
                    out=t_[g_:g_ + 128, tp_ * 1024:(tp_ + 1) * 1024],
                    in_=dpc[par_][:, :])))(tgt, gb, tp, par), spar, 16)
                v_dst[par] = P.add("sync", (lambda t_, g_, tp_, par_: (lambda q: q.dma_start(
                    out=t_[1024 + g_:1024 + g_ + 128, tp_ * 1024:(tp_ + 1) * 1024],
                    in_=wpc[par_][:, :])))(tgt, gb, tp, par), spar, 16)
        bar2b = P.bar()

        for e in ENGS:
            P.wait_bar(e, bar2b)
        delta_sb = AR1[:, 0:8192]
        w_sb = AR1[:, 8192:16384]
        y_acc = AR2[:, 0:8192]
        dA2 = [AR3[:, 0:2048], AR3[:, 2048:4096]]
        u2 = PS[:, 0:2048]
        tmp2 = [AR3[:, 6144:8192], AR2[:, 12288:14336]]
        h2 = [PS[:, 2048:4096], AR2[:, 10240:12288]]
        y_bf = [AR2[:, 14336:15360].bitcast(BF16), AR2[:, 15360:16384].bitcast(BF16)]
        bufB2 = [SM0[:, 0:2048], SM1[:, 0:2048]]
        bufC2 = [SM2[:, 0:2048], SM3[:, 0:2048]]
        v_dA = {}
        v_u2 = {}
        v_tts = {}
        v_tts2 = {}
        v_ym2 = {}
        v_acm2 = {}
        v_uB_par = {0: 0, 1: 0}
        v_ymC_par = {0: 0, 1: 0}
        v_store_y = {0: 0, 1: 0}
        v_cast_prev_cc = 0
        v_ext_last = 0
        v_pair_end = 0
        tmp_reader = [None, None]
        NPAIR = NST * 4
        q = 0
        P.wait("sync", "ccm", v_cc2)
        v_pbc = P.add("sync", lambda q_: q_.dma_start(
            out=PgxB[:, :].rearrange("(b r) t -> b r t", b=2)[bass.DynSlice(cell["jj"], 1), :, :],
            in_=PhA[:, :].rearrange("(b r) t -> b r t", b=1)), "dR", 16)
        P.wait("sync", "dR", v_pbc)
        for cc in range(NCC):
            jb, tloc = cc // 2, cc % 2
            if cc > 0:
                P.wait("sync", "cS", v_dA[2 * q - 1])
                P.wait("sync", "cV", v_u2[q - 1])
            vL1 = P.add("sync", (lambda jb_, tl_: (lambda q_: q_.dma_start(
                out=delta_sb[:, :].rearrange("p (g d t) -> p g d t", g=1, d=8),
                in_=PgxB[:, :].rearrange("(g d p) t -> p g d t", g=2, d=17)[
                    :, jb_:jb_ + 1, 0:8, tl_ * TC:(tl_ + 1) * TC])))(jb, tloc), "dL", 16)
            vL2 = P.add("sync", (lambda jb_, tl_: (lambda q_: q_.dma_start(
                out=w_sb[:, :].rearrange("p (g d t) -> p g d t", g=1, d=8),
                in_=PgxB[:, :].rearrange("(g d p) t -> p g d t", g=2, d=17)[
                    :, jb_:jb_ + 1, 8:16, tl_ * TC:(tl_ + 1) * TC])))(jb, tloc), "dL", 16)
            P.wait("vector", "cS", v_cast_prev_cc)
            P.wait("vector", "cV", v_ext_last)
            for n in range(NST):
                par_n = n % 2
                sB = "dRB%d" % par_n
                sC = "dRC%d" % par_n
                P.wait("sync", "cV", v_uB_par[par_n])
                P.wait("sync", "cV", v_ymC_par[par_n])
                for rr in range(2):
                    rB = P.add("sync", (lambda n_, p_, jb_, tl_, rr_: (lambda q_: q_.dma_start(
                        out=bufB2[p_][:, rr_ * TC:(rr_ + 1) * TC],
                        in_=PgxB[jb_ * 2176 + 2048 + n_:jb_ * 2176 + 2049 + n_,
                                tl_ * TC:(tl_ + 1) * TC].broadcast_to([128, TC]))))(
                        n, par_n, jb, tloc, rr), sB, 16)
                for rr in range(4):
                    rC = P.add("sync", (lambda n_, p_, rr_: (lambda q_: q_.dma_start(
                        out=bufC2[p_][:, rr_ * 512:(rr_ + 1) * 512],
                        in_=PgxB[2112 + n_:2113 + n_, 0:512].broadcast_to([128, 512]))))(n, par_n, rr), sC, 16)
                for dp in range(4):
                    qs = q % 2
                    if q >= 2:
                        P.wait("scalar", "cV", v_tts2[q - 2])
                    if n == 0 and dp == 0:
                        P.wait("scalar", "dL", vL2)
                    for hh in range(2):
                        dblk = 2 * dp + hh
                        v_dA[2 * q + hh] = P.add("scalar", (lambda d_, n_, s_, hh_: (lambda q_: q_.activation(
                            out=dA2[s_][:, hh_ * 1024:(hh_ + 1) * 1024],
                            in_=delta_sb[:, d_ * TC:(d_ + 1) * TC], func=AF.Exp,
                            scale=A_sb[:, d_ * 64 + n_: d_ * 64 + n_ + 1])))(dblk, n, qs, hh), "cS")
                    if dp == 0:
                        P.wait("vector", sB, rB)
                        P.wait("vector", sC, rC)
                        if n == 0:
                            P.wait("vector", "dL", vL2)
                    P.wait("vector", "cV", v_pair_end)
                    v_u2[q] = P.add("vector", (lambda dp_, p_: (lambda q_: q_.tensor_tensor(
                        out=u2[:, :], in0=w_sb[:, dp_ * 2048:(dp_ + 1) * 2048],
                        in1=bufB2[p_][:, :], op=ALU.mult)))(dp, par_n), "cV")
                    P.wait("vector", "cS", v_dA[2 * q + 1])
                    P.wait("vector", "cV", v_u2[q])
                    for hh in range(2):
                        dblk = 2 * dp + hh
                        ini = 0.0 if cc == 0 else carry[:, dblk * 64 + n: dblk * 64 + n + 1]
                        v_tts[2 * q + hh] = P.add("vector", (lambda s_, hh_, i_: (lambda q_: q_.tensor_tensor_scan(
                            out=h2[s_][:, hh_ * 1024:(hh_ + 1) * 1024],
                            data0=dA2[s_][:, hh_ * 1024:(hh_ + 1) * 1024],
                            data1=u2[:, hh_ * 1024:(hh_ + 1) * 1024], initial=i_,
                            op0=ALU.mult, op1=ALU.add)))(qs, hh, ini), "cV")
                    v_tts2[q] = v_tts[2 * q + 1]
                    if q >= 1 and cc < NCC - 1:
                        qp = q - 1
                        for hh in range(2):
                            mb = 2 * qp + hh
                            v_ext_last = P.add("vector", (lambda d_, n_, s_, hh_: (lambda q_: q_.tensor_copy(
                                out=carry[:, d_ * 64 + n_: d_ * 64 + n_ + 1],
                                in_=h2[s_][:, hh_ * 1024 + TC - 1: hh_ * 1024 + TC])))(
                                (mb // 1) % 8 if False else 2 * ((qp) % 4) + hh, (qp // 4) % NST, qp % 2, hh), "cV")
                    P.wait("vector", "cV", v_tts2[q])
                    if n == 0:
                        v_ym2[q] = P.add("vector", (lambda dp_, s_, p_: (lambda q_: q_.tensor_tensor(
                            out=y_acc[:, dp_ * 2048:(dp_ + 1) * 2048],
                            in0=h2[s_][:, :], in1=bufC2[p_][:, :],
                            op=ALU.mult)))(dp, qs, par_n), "cV")
                        v_acm2[q] = None
                    else:
                        tr = tmp_reader[qs]
                        if tr is not None:
                            P.wait("vector", tr[0], tr[1])
                        v_ym2[q] = P.add("vector", (lambda s_, p_: (lambda q_: q_.tensor_tensor(
                            out=tmp2[s_][:, :], in0=h2[s_][:, :],
                            in1=bufC2[p_][:, :], op=ALU.mult)))(qs, par_n), "cV")
                        sG = "dG%d" % dp
                        P.wait("gpsimd", "cV", v_ym2[q])
                        if v_acm2.get(q - 4) is not None:
                            P.wait("gpsimd", sG, v_acm2[q - 4][1])
                        va = P.add("gpsimd", (lambda dp_, s_: (lambda q_: q_.dma_start(
                            out=y_acc[:, dp_ * 2048:(dp_ + 1) * 2048], in_=tmp2[s_][:, :],
                            accum_op=ALU.add)))(dp, qs), sG, 16)
                        v_acm2[q] = (sG, va)
                        tmp_reader[qs] = (sG, va)
                    if dp == 3:
                        v_uB_par[par_n] = v_u2[q]
                        v_ymC_par[par_n] = v_ym2[q]
                    v_pair_end = P.cnt["cV"]
                    q += 1
            if cc < NCC - 1:
                qp = q - 1
                P.wait("vector", "cV", v_tts2[qp])
                for hh in range(2):
                    v_ext_last = P.add("vector", (lambda d_, n_, s_, hh_: (lambda q_: q_.tensor_copy(
                        out=carry[:, d_ * 64 + n_: d_ * 64 + n_ + 1],
                        in_=h2[s_][:, hh_ * 1024 + TC - 1: hh_ * 1024 + TC])))(
                        2 * (qp % 4) + hh, (qp // 4) % NST, qp % 2, hh), "cV")
            v_cast_prev_cc = 0
            for dp in range(4):
                pb = dp % 2
                last_q = (cc * NPAIR) + 63 * 4 + dp
                ac = v_acm2[last_q]
                P.wait("scalar", ac[0], ac[1])
                P.wait("scalar", "dSy%d" % pb, v_store_y[pb])
                vca = P.add("scalar", (lambda dp_, pb_: (lambda q_: q_.activation(
                    out=y_bf[pb_][:, :], in_=y_acc[:, dp_ * 2048:(dp_ + 1) * 2048], func=AF.Identity)))(dp, pb), "cS")
                v_cast_prev_cc = vca
                P.wait("sync", "cS", vca)
                v_store_y[pb] = P.add("sync", (lambda dp_, cc_, pb_: (lambda q_: q_.dma_start(
                    out=y_scr[:, :].rearrange("(d p) t -> p d t", p=128)[:, 2 * dp_:2 * dp_ + 2, cc_ * TC:(cc_ + 1) * TC],
                    in_=y_bf[pb_][:, :].rearrange("p (r t) -> p r t", r=2))))(dp, cc, pb), "dSy%d" % pb, 16)
        bar3 = P.bar()

        if debug:
            P.wait_bar("sync", bar3)
            P.add("sync", lambda q: q.dma_start(out=y_dbg[:, :], in_=y_scr[:, :]), "dS", 16)
        P.wait_bar("gpsimd", bar3)
        v_cc = P.add("gpsimd", lambda q: q.collective_compute(
            "AllGather", ALU.bypass, replica_groups=[[0, 1], [2, 3], [4, 5], [6, 7]],
            ins=[y_scr.ap().opt()], outs=[ygx_scr.ap().opt()]), "ccm")

        for e in ENGS[:4]:
            P.wait_bar(e, bar3)
        yrhs = AR1b[:, 16384:24576]
        xlr = AR2b[:, 16384:32768]
        wpair = [AR3b[:, 0:4096], AR3b[:, 4096:8192]]
        x2t = [AR1[:, k * 2048:(k + 1) * 2048] for k in range(4)]
        ys2 = AR2[:, 7680:8192]
        v_ffn3 = 0
        v_out_st = 0
        DynS = bass.DynSlice
        for tp in range(4):
            P.wait("sync", "cP", P.cnt["cP"] if tp else 0)
            P.wait("sync", "cV", v_ffn3)
            P.wait("sync", "dS", v_out_st)
            vx1 = P.add("sync", (lambda tp_: (lambda q: q.dma_start(
                out=AR1[:, 0:8192].rearrange("p (a d) -> p a d", a=4),
                in_=x1_scr[:, :].rearrange("(a p) d -> p a d", p=128)[:, tp_ * 4:(tp_ + 1) * 4, :])))(tp), "dL", 16)
            vy1 = P.add("sync", (lambda tp_: (lambda q: q.dma_start(
                out=yrhs[:, 0:4096].rearrange("p (c t) -> p c t", c=8),
                in_=y_scr[:, :].rearrange("(c p) t -> p c t", p=128)[:, :, DynS(cell["tsl"][tp_], 512)])))(tp), "dL", 16)
            v_x1 = [vy1] * 4
            vy3 = P.add("sync", (lambda tp_: (lambda q: q.dma_start(
                out=xlr[:, 0:8192].rearrange("p (c t) -> p c t", c=16),
                in_=xlnbT_scr[:, :].rearrange("(c p) t -> p c t", p=128)[:, :, tp_ * 512:(tp_ + 1) * 512])))(tp), "dRB0", 16)
            if tp == 0:
                P.wait("sync", "ccm", v_cc)
            vy2 = P.add("sync", (lambda tp_: (lambda q: q.dma_start(
                out=yrhs[:, 4096:8192].rearrange("p (c t) -> p c t", c=8),
                in_=ygx_scr[:, :].rearrange("(c p) t -> p c t", p=128)[:, DynS(cell["k8"], 8), DynS(cell["tsl"][tp_], 512)])))(tp), "dRC0", 16)
            P.wait("vector", "dL", vy1)
            P.wait("vector", "dRC0", vy2)
            P.wait("vector", "dRB0", vy3)
            skipb = [AR2b[:, 0:512], AR2b[:, 512:1024]]
            vv_sc = [0, 0]
            v_skip = 0
            for c in range(17):
                if c < 16:
                    vv_sc[c % 2] = P.add("vector", (lambda c_: (lambda q: q.tensor_scalar(
                        out=skipb[c_ % 2], in0=xlr[:, c_ * 512:(c_ + 1) * 512],
                        scalar1=dpar[:, c_:c_ + 1], scalar2=None, op0=ALU.mult)))(c), "cV")
                if c >= 1:
                    cm = c - 1
                    P.wait("vector", "cV", vv_sc[cm % 2])
                    v_skip = P.add("vector", (lambda c_: (lambda q: q.tensor_tensor(
                        out=yrhs[:, c_ * 512:(c_ + 1) * 512], in0=yrhs[:, c_ * 512:(c_ + 1) * 512],
                        in1=skipb[c_ % 2], op=ALU.add)))(cm), "cV")
            v_wl = {0: 0, 1: 0}
            v_add = 0
            v_oev = 0
            for oc in range(16):
                wp = oc % 2
                sw = "dX%d" % wp
                P.wait("sync", "cP", v_wl[wp])
                vwo = P.add("sync", (lambda oc_, wp_: (lambda q: q.dma_start(
                    out=wpair[wp_][:, 0:2048].rearrange("p (c o) -> p c o", c=16),
                    in_=wo_b_scr[:, oc_ * 128:(oc_ + 1) * 128].rearrange("(c p) o -> p c o", p=128))))(oc, wp), sw, 16)
                P.wait("tensor", sw, vwo)
                if v_add:
                    P.wait("tensor", "cV", v_add)
                if oc == 0:
                    P.wait("tensor", "cV", v_skip)
                P.wait("tensor", "dRC0", vy2)
                P.wait("tensor", "dL", vy1)
                for dc in range(16):
                    P.add("tensor", (lambda dc_, wp_: (lambda q: q.matmul(
                        BANK(0), wpair[wp_][:, dc_ * 128:(dc_ + 1) * 128],
                        yrhs[:, dc_ * 512:(dc_ + 1) * 512], start=(dc_ == 0), stop=(dc_ == 15))))(dc, wp), "cP")
                v_wl[wp] = P.cnt["cP"]
                P.wait("scalar", "cP", v_wl[wp])
                P.wait("scalar", "cV", v_add if oc else 0)
                v_oev = P.add("scalar", lambda q: q.copy(out=ys2[:, :], in_=BANK(0)), "cS")
                P.wait("tensor", "cS", v_oev)
                for tt in range(4):
                    P.add("tensor", (lambda tt_: (lambda q: q.transpose(
                        BANK(1)[:, tt_ * 128:(tt_ + 1) * 128], ys2[:, tt_ * 128:(tt_ + 1) * 128], idf[:, :])))(tt), "cP")
                vp = P.cnt["cP"]
                P.wait("vector", "cP", vp)
                for tt in range(4):
                    P.wait("vector", "dL", v_x1[tt])
                    v_add = P.add("vector", (lambda tt_, oc_: (lambda q: q.tensor_tensor(
                        out=x2t[tt_][:, oc_ * 128:(oc_ + 1) * 128], in0=x2t[tt_][:, oc_ * 128:(oc_ + 1) * 128],
                        in1=BANK(1)[:, tt_ * 128:(tt_ + 1) * 128], op=ALU.add)))(tt, oc), "cV")
            P.wait("sync", "cP", max(v_wl[0], v_wl[1]))
            vw5 = emit_ffn_weights_load()
            for e in ("scalar", "vector", "tensor"):
                P.wait(e, "dL", vw5)
            f5state = {"tile_free": [], "st_tile": 0}
            v_ffn3 = emit_ffn_sup(0, x2t,
                                  out_c[:, :].rearrange("(a p) d -> p a d", p=128)[:, tp * 4:tp * 4 + 4, :],
                                  f5state, tile_dma=None, v_tiles_ready=v_add)
            v_out_st = P.cnt["dS"]
        P.wait("sync", "dS", P.cnt["dS"])

        import os as _os
        _dbg_emit = bool(_os.environ.get("KRN_DBG_EMIT"))

        def emitter(name):
            def run(q):
                for idx, (kind, *rest) in enumerate(P.ops[name]):
                    if kind == "wait":
                        q.wait_ge(sems[rest[0]], rest[1])
                    else:
                        fn, inc, k = rest
                        try:
                            instr = fn(q)
                        except Exception:
                            if _dbg_emit:
                                import traceback as _tb
                                print(f"EMIT FAIL {name} op#{idx}")
                                _tb.print_exc()
                            raise
                        if inc is not None and instr is not None:
                            instr.then_inc(sems[inc], k)
            return run

        block.sync(emitter("sync"))
        block.scalar(emitter("scalar"))
        block.vector(emitter("vector"))
        block.tensor(emitter("tensor"))
        block.gpsimd(emitter("gpsimd"))
    return nc


def _prep_inputs(core, inputs):
    b, j = core // 2, core % 2
    pm = np.arange(D) if j == 0 else np.r_[1024:2048, 0:1024]
    g = lambda k: np.asarray(inputs[k])
    x = g("L0_attn_x")[b][:, pm]
    xpw = g("L0_attn_x_proj_w")[:, pm]
    dtw = g("L0_attn_dt_proj_w")[pm]
    dtbv = g("L0_attn_dt_proj_b")[pm]
    alog = g("L0_attn_A_log")[pm][:DS]
    dparf = g("L0_attn_D_param")[pm]
    nwf = g("L0_attn_norm_w")[pm]
    nbf = g("L0_attn_norm_b")[pm]
    wout = g("L0_attn_out_proj_w")[pm][:, pm]
    A_cat = g("L0_ffn_A").reshape(ER, D)[:, pm]
    B_cat = g("L0_ffn_B").transpose(1, 0, 2).reshape(D, ER)[pm]
    wr = g("L0_ffn_Wr")[:, pm]
    br = g("L0_ffn_br")
    C = np.ascontiguousarray
    return dict(
        x_in=C(x.astype(np.float32)),
        xpwT=C(xpw.T.astype(np.float32)),
        dtwT=C(dtw.T.astype(np.float32)),
        dtb_r=C(dtbv.reshape(16, 128).T.astype(np.float32)),
        alog_r=C(alog.reshape(8, 128, NST).transpose(1, 0, 2).reshape(128, 512).astype(np.float32)),
        dpar_r=C(dparf.reshape(16, 128).T.astype(np.float32)),
        nw_r=C(nwf.reshape(16, 128).T.astype(np.float32)),
        nb_r=C(nbf.reshape(16, 128).T.astype(np.float32)),
        woutT=C(wout.T.astype(np.float32)),
        ffnAT=C(A_cat.T.astype(np.float32)),
        ffnBT=C(B_cat.T.astype(np.float32)),
        wrT=C(wr.T.astype(np.float32)),
        br_in=C(br.reshape(1, 8).astype(np.float32)),
        blkid_in=C(np.repeat(np.eye(8, dtype=np.float32), 64, axis=1)),
        idf_in=np.eye(128, dtype=np.float32),
        idb_in=np.eye(128, dtype=np.float32).astype(ml_dtypes.bfloat16),
        cst_in=np.array([[EPS, -0.5, -1.0, 1.0]], np.float32),
        jv_in=np.array([[j]], np.int32),
        kv_in=np.array([[1 - j]], np.int32),
    )


def kernel(**inputs):
    nc = build_nc()
    in_maps = [_prep_inputs(c, inputs) for c in range(8)]
    res = run_bass_kernel_spmd(nc, in_maps, core_ids=list(range(8)))
    out = np.empty((4, TOK, D), np.float32)
    for c in range(8):
        b, j = c // 2, c % 2
        pm = np.arange(D) if j == 0 else np.r_[1024:2048, 0:1024]
        out[b, j * MYT:(j + 1) * MYT, :] = res.results[c]["out_c"][:, pm]
    return out


# version 24
# speedup vs baseline: 1.0514x; 1.0100x over previous
import numpy as np
import ml_dtypes
from contextlib import ExitStack

import concourse.bass as bass
import concourse.mybir as mybir
from concourse.bass_utils import run_bass_kernel_spmd

F32 = mybir.dt.float32
BF16 = mybir.dt.bfloat16
I32 = mybir.dt.int32
AF = mybir.ActivationFunctionType
ALU = mybir.AluOpType
AX = mybir.AxisListType

TOK, D, DS, NST, RHO, ER = 4096, 2048, 1024, 64, 128, 512
TC, NCC = 1024, 4
NSUP = 8
MYT = 2048
EPS = 1e-5
SCL = 0.5 * (1.0 / 64.0)
GP_NS = frozenset(n for n in range(NST) if n % 3 != 0)

ENGS = ("sync", "scalar", "vector", "tensor", "gpsimd")
SEMS = ("dL", "dR", "dS", "cV", "cS", "cP", "cG", "ccm",
        "dRB0", "dRB1", "dRC0", "dRC1", "dSy0", "dSy1", "dSa", "dSb",
        "dG0", "dG1", "dG2", "dG3", "dG4", "dG5", "dG6", "dG7", "dX0", "dX1")


class Prog:
    def __init__(self):
        self.ops = {e: [] for e in ENGS}
        self.cnt = {s: 0 for s in SEMS}

    def add(self, eng, fn, inc=None, k=1):
        self.ops[eng].append(("op", fn, inc, k))
        if inc is not None:
            self.cnt[inc] += k
            return self.cnt[inc]
        return None

    def wait(self, eng, sem, val):
        if val:
            self.ops[eng].append(("wait", sem, int(val)))

    def bar(self):
        return dict(self.cnt)

    def wait_bar(self, eng, bar):
        for s in SEMS:
            self.wait(eng, s, bar[s])


def build_nc(debug=False):
    nc = bass.Bass("TRN2", target_bir_lowering=False, debug=False)
    dt = nc.dram_tensor

    x_in = dt("x_in", [TOK, D], F32, kind="ExternalInput")
    xpwT = dt("xpwT", [D, 256], F32, kind="ExternalInput")
    dtwT = dt("dtwT", [RHO, D], F32, kind="ExternalInput")
    dtb_r = dt("dtb_r", [128, 16], F32, kind="ExternalInput")
    alog_r = dt("alog_r", [128, 512], F32, kind="ExternalInput")
    dpar_r = dt("dpar_r", [128, 16], F32, kind="ExternalInput")
    nw_r = dt("nw_r", [128, 16], F32, kind="ExternalInput")
    nb_r = dt("nb_r", [128, 16], F32, kind="ExternalInput")
    woutT = dt("woutT", [D, D], F32, kind="ExternalInput")
    ffnAT = dt("ffnAT", [D, ER], F32, kind="ExternalInput")
    ffnBT = dt("ffnBT", [ER, D], F32, kind="ExternalInput")
    wrT = dt("wrT", [D, 8], F32, kind="ExternalInput")
    br_in = dt("br_in", [1, 8], F32, kind="ExternalInput")
    blkid_in = dt("blkid_in", [8, ER], F32, kind="ExternalInput")
    idf_in = dt("idf_in", [128, 128], F32, kind="ExternalInput")
    idb_in = dt("idb_in", [128, 128], BF16, kind="ExternalInput")
    cst_in = dt("cst_in", [1, 4], F32, kind="ExternalInput")
    jv_in = dt("jv_in", [1, 1], I32, kind="ExternalInput")
    kv_in = dt("kv_in", [1, 1], I32, kind="ExternalInput")

    out_c = dt("out_c", [MYT, D], F32, kind="ExternalOutput")

    kd = "ExternalOutput" if debug else "Internal"
    x1_scr = dt("x1_scr", [MYT, D], F32, kind=kd)
    xlnbT_scr = dt("xlnbT_scr", [D, MYT], BF16, kind=kd)
    xln2T_scr = dt("xln2T_scr", [D, MYT], F32, kind=kd)
    PhA = dt("PhA", [2176, MYT], F32, kind=kd)
    PhB = dt("PhB", [2176, MYT], F32, kind=kd)
    PgxB = dt("PgxB", [4352, MYT], F32, kind="Internal")
    y_scr2 = dt("y_scr2", [2 * DS, MYT], BF16, kind="Internal")
    y_dbg = dt("y_dbg", [2 * DS, MYT], BF16, kind=kd) if debug else None
    ygx2 = dt("ygx2", [2 * D, MYT], BF16, kind="Internal")
    wo_b_scr = dt("wo_b_scr", [D, D], BF16, kind="Internal")
    fAT_b_scr = dt("fAT_b_scr", [D, ER], BF16, kind="Internal")
    fBT_b_scr = dt("fBT_b_scr", [ER, D], BF16, kind="Internal")
    xpwT_b_scr = dt("xpwT_b_scr", [D, 256], BF16, kind="Internal")
    dtwT_b_scr = dt("dtwT_b_scr", [RHO, D], BF16, kind="Internal")
    wrT_b_scr = dt("wrT_b_scr", [D, 8], BF16, kind="Internal")

    P = Prog()
    cell = {}

    with ExitStack() as st:
        sb = lambda name, shape, d=F32: st.enter_context(nc.sbuf_tensor(name, shape, d))
        s_jv = sb("s_jv", [1, 2], I32)
        idf = sb("idf", [128, 128], F32)
        idb = sb("idb", [128, 128], BF16)
        blkid = sb("blkid", [8, ER], F32)
        br_b = sb("br_b", [128, 8], F32)
        nw = sb("nw", [128, 16], F32)
        nb = sb("nb", [128, 16], F32)
        dtb = sb("dtb", [128, 16], F32)
        dpar = sb("dpar", [128, 16], F32)
        A_sb = sb("A_sb", [128, 512], F32)
        carry = sb("carry", [128, 512], F32)
        ST = sb("ST", [128, 192], F32)
        MT = sb("MT", [8, ER], F32)
        cst = sb("cst", [128, 4], F32)
        wr_sb = sb("wr_sb", [128, 128], BF16)
        dtw_sb = sb("dtw_sb", [128, D], BF16)
        AR1 = sb("AR1", [128, 16384], F32)
        AR2 = sb("AR2", [128, 16384], F32)
        AR3 = sb("AR3", [128, 8192], F32)
        SM0 = sb("SM0", [128, 2048], F32)
        SM1 = sb("SM1", [128, 2048], F32)
        SM2 = sb("SM2", [128, 2048], F32)
        SM3 = sb("SM3", [128, 2048], F32)
        AR1b = AR1[:, :].bitcast(BF16)
        AR2b = AR2[:, :].bitcast(BF16)
        AR3b = AR3[:, :].bitcast(BF16)
        SM0b = SM0[:, :].bitcast(BF16)
        SM2b = SM2[:, :].bitcast(BF16)
        SM3b = SM3[:, :].bitcast(BF16)
        PS = st.enter_context(nc.psum_tensor("PS", [128, 4096], F32))
        PSb = PS[:, :].bitcast(BF16)
        BANK = lambda k, w=512: PS[:, 512 * k:512 * k + w]
        sems = {s: st.enter_context(nc.semaphore(s)) for s in SEMS}
        block = st.enter_context(nc.Block())

        for dst, src in [
            (s_jv[0:1, 0:1], jv_in[:, :]), (s_jv[0:1, 1:2], kv_in[:, :]),
            (idf[:, :], idf_in[:, :]), (idb[:, :], idb_in[:, :]),
            (blkid[:, :], blkid_in[:, :]),
            (br_b[:, :], br_in[0:1, :].broadcast_to([128, 8])),
            (nw[:, :], nw_r[:, :]), (nb[:, :], nb_r[:, :]),
            (dtb[:, :], dtb_r[:, :]), (dpar[:, :], dpar_r[:, :]),
            (A_sb[:, :], alog_r[:, :]),
            (cst[:, :], cst_in[0:1, :].broadcast_to([128, 4])),
        ]:
            P.add("sync", (lambda d_, s_: (lambda q: q.dma_start(out=d_, in_=s_)))(dst, src), "dL", 16)
        v_glob = P.cnt["dL"]

        def _ld_regs(q):
            j = nc.values_load(s_jv[0:1, 0:1], engines=[q.engine], min_val=0,
                               max_val=1, skip_runtime_bounds_check=True)
            k = nc.values_load(s_jv[0:1, 1:2], engines=[q.engine], min_val=0,
                               max_val=1, skip_runtime_bounds_check=True)
            cell["jj"] = j * 1
            cell["kk2"] = k + 2
            cell["j8"] = j * 8
            cell["k8"] = k * 8
            return None
        P.wait("sync", "dL", v_glob)
        P.add("sync", _ld_regs)

        P.wait("scalar", "dL", v_glob)
        vA1 = P.add("scalar", lambda q: q.activation(out=A_sb[:, :], in_=A_sb[:, :], func=AF.Exp), "cS")
        P.wait("vector", "cS", vA1)
        P.add("vector", lambda q: q.tensor_scalar(out=A_sb[:, :], in0=A_sb[:, :], scalar1=-1.0,
                                                  scalar2=None, op0=ALU.mult), "cV")

        jobs = []
        for r in range(16):
            jobs.append((wrT[r * 128:(r + 1) * 128, :], wrT_b_scr[r * 128:(r + 1) * 128, :], 8, None))
        for r in range(16):
            jobs.append((ffnAT[r * 128:(r + 1) * 128, :], fAT_b_scr[r * 128:(r + 1) * 128, :], ER, None))
        for r in range(4):
            jobs.append((ffnBT[r * 128:(r + 1) * 128, :], fBT_b_scr[r * 128:(r + 1) * 128, :], D, None))
        for r in range(16):
            jobs.append((xpwT[r * 128:(r + 1) * 128, :], xpwT_b_scr[r * 128:(r + 1) * 128, :], 256, None))
        jobs.append((dtwT[:, :], dtwT_b_scr[:, :], D, None))
        for r in range(16):
            jobs.append((woutT[r * 128:(r + 1) * 128, :], wo_b_scr[r * 128:(r + 1) * 128, :], D, None))
        stg_in = [SM0, SM1]
        stg_out = [SM2b, SM3b]
        v_st_last = {0: 0, 1: 0}
        for i, (jsrc, dst, cols, wchunk) in enumerate(jobs):
            par = i % 2
            sld = "dRB%d" % par
            sst = "dSa" if par == 0 else "dSb"
            P.wait("sync", sst, v_st_last[par])
            vl = P.add("sync", (lambda s_, c_, p_: (lambda q: q.dma_start(out=stg_in[p_][:, 0:c_], in_=s_)))(jsrc, cols, par), sld, 16)
            P.wait("vector", sld, vl)
            P.wait("vector", sst, v_st_last[par])
            vc = P.add("vector", (lambda c_, p_: (lambda q: q.tensor_copy(out=stg_out[p_][:, 0:c_], in_=stg_in[p_][:, 0:c_])))(cols, par), "cV")
            P.wait("sync", "cV", vc)
            vs = P.add("sync", (lambda d_, c_, p_: (lambda q: q.dma_start(out=d_, in_=stg_out[p_][:, 0:c_])))(dst, cols, par), sst, 16)
            v_st_last[par] = vs
        bar0 = P.bar()

        fA = AR3b[:, 0:8192]
        fB = AR3b[:, 8192:16384]
        pst = {"tr_evac": 0, "router_c": 0, "mscT_c": 0, "maskmm": 0, "msk_ev": 0,
               "hm": 0, "y_ev": 0, "trf": 0, "mt_rd": 0, "v_xlb_2a": [0, 0]}

        def emit_ffn_weights_load():
            vv = 0
            for r in range(16):
                vv = P.add("sync", (lambda r_: (lambda q: q.dma_start(
                    out=fA[:, r_ * 512:(r_ + 1) * 512], in_=fAT_b_scr[r_ * 128:(r_ + 1) * 128, :])))(r), "dL", 16)
            for r in range(4):
                vv = P.add("sync", (lambda r_: (lambda q: q.dma_start(
                    out=fB[:, r_ * 2048:(r_ + 1) * 2048], in_=fBT_b_scr[r_ * 128:(r_ + 1) * 128, :])))(r), "dL", 16)
            vv = P.add("sync", lambda q: q.dma_start(
                out=wr_sb[:, :].rearrange("p (c e) -> p c e", c=16),
                in_=wrT_b_scr[:, :].rearrange("(c p) e -> p c e", p=128)), "dL", 16)
            return vv

        BG = {2: {"cV": 0, "cS": 0}, 3: {"cV": 0, "cS": 0}}

        def bank_wait(k):
            for sem, val in BG[k].items():
                P.wait("tensor", sem, val)

        def emit_ffn_sup(sup, x_tiles, out_dst, fstate, tile_dma=None, v_tiles_ready=0):
            xlnT = AR2b[:, 0:8192]
            msk = AR2[:, 4096:6144]
            hm = AR2b[:, 12288:14336]
            ys = [AR2[:, 7168:7680], AR2[:, 7680:8192]]
            xlb = [SM0b[:, 0:2048], SM0b[:, 2048:4096]]
            sq = SM2
            v10_par = pst.setdefault("v10_par", [0, 0])
            v15_par = pst.setdefault("v15_par", [0, 0])
            vxh = [0, 0]
            if tile_dma is not None:
                for hf in range(2):
                    for sem, val in fstate.get("tile_free_h", [[], []])[hf]:
                        P.wait("sync", sem, val)
                    P.wait("sync", "dS", fstate.get("st_tile", 0))
                    vxh[hf] = P.add("sync", (lambda s_, hf_: (lambda q: tile_dma(q, s_, hf_)))(sup, hf), "dX%d" % hf, 16)
            for tt in range(4):
                pp = tt % 2
                base = 64 * pp
                col = lambda k: ST[:, base + k:base + k + 1]
                vx = vxh[tt // 2] if tile_dma is not None else 0
                sx = "dX%d" % (tt // 2)
                xt = x_tiles[tt]
                P.wait("vector", "cS", v10_par[pp])
                P.wait("vector", "cP", v15_par[pp])
                if v_tiles_ready and tile_dma is None:
                    P.wait("vector", "cV", v_tiles_ready)
                if vx:
                    P.wait("vector", sx, vx)
                v1 = P.add("vector", (lambda xt_, c_: (lambda q: q.reduce_sum(c_, xt_, axis=AX.X)))(xt, col(0)), "cV")
                if vx:
                    P.wait("scalar", sx, vx)
                if v_tiles_ready and tile_dma is None:
                    P.wait("scalar", "cV", v_tiles_ready)
                    P.wait("tensor", "cV", v_tiles_ready)
                v2 = P.add("scalar", (lambda xt_, c_: (lambda q: q.activation(out=sq[:, :], in_=xt_, func=AF.Square,
                                                                              accum_out=c_)))(xt, col(1)), "cS")
                P.wait("vector", "cS", v2)
                P.wait("vector", "cV", v1)
                vv = P.add("vector", (lambda a, b: (lambda q: q.tensor_scalar(out=a, in0=b, scalar1=1.0 / D, scalar2=None, op0=ALU.mult)))(col(2), col(0)), "cV")
                vv = P.add("vector", (lambda a, b: (lambda q: q.tensor_scalar(out=a, in0=b, scalar1=1.0 / D, scalar2=None, op0=ALU.mult)))(col(3), col(1)), "cV")
                P.wait("vector", "cV", vv)
                vv = P.add("vector", (lambda a, b: (lambda q: q.tensor_tensor(out=a, in0=b, in1=b, op=ALU.mult)))(col(4), col(2)), "cV")
                P.wait("vector", "cV", vv)
                v3 = P.add("vector", (lambda a, b, c: (lambda q: q.tensor_tensor(out=a, in0=b, in1=c, op=ALU.subtract)))(col(5), col(3), col(4)), "cV")
                P.wait("scalar", "cV", v3)
                vl = P.add("scalar", (lambda a, b: (lambda q: q.activation(out=a, in_=b, func=AF.Ln, bias=cst[:, 0:1])))(col(6), col(5)), "cS")
                P.wait("scalar", "cS", vl)
                v4 = P.add("scalar", (lambda a, b: (lambda q: q.activation(out=a, in_=b, func=AF.Exp, scale=cst[:, 1:2])))(col(7), col(6)), "cS")
                P.wait("vector", "cS", v4)
                v5 = P.add("vector", (lambda a, b, c: (lambda q: q.tensor_scalar(out=a, in0=b, scalar1=c,
                                                                                 scalar2=-1.0, op0=ALU.mult, op1=ALU.mult)))(col(8), col(2), col(7)), "cV")
                P.wait("scalar", "cV", v5)
                P.wait("scalar", "cP", pst["v_xlb"][pp])
                v6 = P.add("scalar", (lambda xt_, p_, c7, c8: (lambda q: q.activation(out=xlb[p_][:, :], in_=xt_, func=AF.Identity,
                                                                                      scale=c7, bias=c8)))(xt, pp, col(7), col(8)), "cS")
                for g in range(4):
                    P.wait("tensor", "cS", v6)
                    P.wait("tensor", "cV", pst["tr_evac"])
                    for c4 in range(4):
                        c = g * 4 + c4
                        P.add("tensor", (lambda c_, c4_, p_, g_: (lambda q: q.transpose(
                            PSb[:, (g_ % 2) * 1024 + c4_ * 128:(g_ % 2) * 1024 + (c4_ + 1) * 128],
                            xlb[p_][:, c_ * 128:(c_ + 1) * 128], idb[:, :])))(c, c4, pp, g), "cP")
                    vp = P.cnt["cP"]
                    P.wait("vector", "cP", vp)
                    pst["tr_evac"] = P.add("vector", (lambda g_, tt_: (lambda q: q.tensor_copy(
                        out=xlnT[:, :].rearrange("p (c t) -> p c t", c=16)[:, g_ * 4:(g_ + 1) * 4, tt_ * 128:(tt_ + 1) * 128],
                        in_=PSb[:, (g_ % 2) * 1024:(g_ % 2) * 1024 + 512].rearrange("p (c t) -> p c t", c=4))))(g, tt), "cV")
                pst["v_xlb"][pp] = P.cnt["cP"]
                P.wait("tensor", "cV", pst["tr_evac"])
                bank_wait(2)
                for c in range(16):
                    P.add("tensor", (lambda c_, tt_: (lambda q: q.matmul(
                        PS[:, 1024:1032], xlnT[:, c_ * 512 + tt_ * 128: c_ * 512 + (tt_ + 1) * 128],
                        wr_sb[:, c_ * 8:(c_ + 1) * 8], start=(c_ == 0), stop=(c_ == 15))))(c, tt), "cP")
                vr = P.cnt["cP"]
                P.wait("vector", "cP", vr)
                vrc = P.add("vector", (lambda c16: (lambda q: q.tensor_tensor(out=ST[:, c16:c16 + 8], in0=br_b[:, :],
                                                                              in1=PS[:, 1024:1032], op=ALU.add)))(base + 16), "cV")
                BG[2]["cV"] = vrc
                P.wait("vector", "cV", vrc)
                v8 = P.add("vector", (lambda a, c16: (lambda q: q.reduce_max(a, ST[:, c16:c16 + 8], axis=AX.X)))(col(9), base + 16), "cV")
                P.wait("vector", "cV", v8)
                v9 = P.add("vector", (lambda a, b: (lambda q: q.tensor_scalar(out=a, in0=b, scalar1=-1.0, scalar2=None, op0=ALU.mult)))(col(10), col(9)), "cV")
                P.wait("scalar", "cV", v9)
                v10 = P.add("scalar", (lambda c16, c24, c10: (lambda q: q.activation(out=ST[:, c24:c24 + 8], in_=ST[:, c16:c16 + 8],
                                                                                     func=AF.Exp, bias=c10)))(base + 16, base + 24, col(10)), "cS")
                v10_par[pp] = v10
                P.wait("vector", "cS", v10)
                v11 = P.add("vector", (lambda a, c24: (lambda q: q.reduce_sum(a, ST[:, c24:c24 + 8], axis=AX.X)))(col(11), base + 24), "cV")
                P.wait("vector", "cV", v11)
                vr2 = P.add("vector", (lambda a, b: (lambda q: q.reciprocal(a, b)))(col(12), col(11)), "cV")
                P.wait("vector", "cV", vr2)
                v12 = P.add("vector", (lambda a, b: (lambda q: q.tensor_scalar(out=a, in0=b, scalar1=SCL, scalar2=None, op0=ALU.mult)))(col(13), col(12)), "cV")
                P.wait("vector", "cV", v12)
                v13 = P.add("vector", (lambda c16, c32, c9: (lambda q: q.tensor_scalar(out=ST[:, c32:c32 + 8], in0=ST[:, c16:c16 + 8],
                                                                                       scalar1=c9, scalar2=None, op0=ALU.is_equal)))(base + 16, base + 32, col(9)), "cV")
                P.wait("vector", "cV", v13)
                v14 = P.add("vector", (lambda c32, c40, c13: (lambda q: q.tensor_scalar(out=ST[:, c40:c40 + 8], in0=ST[:, c32:c32 + 8],
                                                                                        scalar1=c13, scalar2=None, op0=ALU.mult)))(base + 32, base + 40, col(13)), "cV")
                P.wait("tensor", "cV", v14)
                P.wait("tensor", "cS", pst["mscT_c"])
                v15 = P.add("tensor", (lambda c40: (lambda q: q.transpose(PS[0:8, 3584:3712], ST[:, c40:c40 + 8], idf[:, :])))(base + 40), "cP")
                v15_par[pp] = v15
                P.wait("scalar", "cP", v15)
                P.wait("scalar", "cP", pst["mt_rd"])
                pst["mscT_c"] = P.add("scalar", (lambda tt_: (lambda q: q.copy(out=MT[0:8, tt_ * 128:(tt_ + 1) * 128],
                                                                               in_=PS[0:8, 3584:3712])))(tt), "cS")
            for c4 in range(4):
                P.wait("tensor", "cS", pst["mscT_c"])
                bank_wait(3)
                P.add("tensor", (lambda c4_: (lambda q: q.matmul(BANK(3), blkid[0:8, c4_ * 128:(c4_ + 1) * 128],
                                                                 MT[0:8, :], start=True, stop=True)))(c4), "cP")
                vp = P.cnt["cP"]
                P.wait("scalar", "cP", vp)
                P.wait("scalar", "cV", pst["hm"])
                pst["msk_ev"] = P.add("scalar", (lambda c4_: (lambda q: q.copy(out=msk[:, c4_ * 512:(c4_ + 1) * 512],
                                                                               in_=BANK(3))))(c4), "cS")
                BG[3]["cS"] = pst["msk_ev"]
            pst["mt_rd"] = P.cnt["cP"]
            v_hm_bank = pst.setdefault("v_hm_bank", [0, 0])
            for ec in range(4):
                bk = 4 if ec % 2 == 0 else 2
                P.wait("tensor", "cS", pst["msk_ev"])
                if bk == 2:
                    bank_wait(2)
                else:
                    P.wait("tensor", "cV", v_hm_bank[0])
                for c in range(16):
                    P.add("tensor", (lambda c_, ec_, bk_: (lambda q: q.matmul(
                        BANK(bk_), fA[:, c_ * 512 + ec_ * 128: c_ * 512 + (ec_ + 1) * 128],
                        xlnT[:, c_ * 512:(c_ + 1) * 512], start=(c_ == 0), stop=(c_ == 15))))(c, ec, bk), "cP")
                vp = P.cnt["cP"]
                P.wait("vector", "cP", vp)
                v_hm_bank[ec % 2] = P.add("vector", (lambda ec_, bk_: (lambda q: q.tensor_tensor(
                    out=hm[:, ec_ * 512:(ec_ + 1) * 512], in0=BANK(bk_), in1=msk[:, ec_ * 512:(ec_ + 1) * 512],
                    op=ALU.mult)))(ec, bk), "cV")
                if bk == 2:
                    BG[2]["cV"] = v_hm_bank[1]
            pst["hm"] = v_hm_bank[1]
            v_yev = {}
            v_tr_done = {}
            for oc in range(17):
                if oc < 16:
                    bk = 5 if oc % 2 == 0 else 3
                    P.wait("tensor", "cV", pst["hm"])
                    if bk == 3:
                        bank_wait(3)
                    else:
                        P.wait("tensor", "cS", v_yev.get(oc - 2, 0))
                    for ec in range(4):
                        P.add("tensor", (lambda oc_, ec_, bk_: (lambda q: q.matmul(
                            BANK(bk_), fB[:, ec_ * 2048 + oc_ * 128: ec_ * 2048 + (oc_ + 1) * 128],
                            hm[:, ec_ * 512:(ec_ + 1) * 512], start=(ec_ == 0), stop=(ec_ == 3))))(oc, ec, bk), "cP")
                    vmm = P.cnt["cP"]
                    P.wait("scalar", "cP", vmm)
                    v_yev[oc] = P.add("scalar", (lambda bk_, oc_: (lambda q: q.copy(out=ys[oc_ % 2][:, :], in_=BANK(bk_))))(bk, oc), "cS")
                    if bk == 3:
                        BG[3]["cS"] = v_yev[oc]
                ot = oc - 1
                if 0 <= ot < 16:
                    P.wait("tensor", "cS", v_yev[ot])
                    P.wait("tensor", "cV", pst["trf"])
                    for tt in range(4):
                        P.add("tensor", (lambda tt_, ot_: (lambda q: q.transpose(
                            BANK(6)[:, tt_ * 128:(tt_ + 1) * 128], ys[ot_ % 2][:, tt_ * 128:(tt_ + 1) * 128], idf[:, :])))(tt, ot), "cP")
                    v_tr_done[ot] = P.cnt["cP"]
                    P.wait("vector", "cP", v_tr_done[ot])
                    for tt in range(4):
                        pst["trf"] = P.add("vector", (lambda tt_, oc_: (lambda q: q.tensor_tensor(
                            out=x_tiles[tt_][:, oc_ * 128:(oc_ + 1) * 128], in0=x_tiles[tt_][:, oc_ * 128:(oc_ + 1) * 128],
                            in1=BANK(6)[:, tt_ * 128:(tt_ + 1) * 128], op=ALU.add)))(tt, ot), "cV")
            if out_dst is not None:
                P.wait("sync", "cV", pst["trf"])
                fstate["st_tile"] = P.add("sync", (lambda d_: (lambda q: q.dma_start(
                    out=d_, in_=AR1[:, 0:8192].rearrange("p (a d) -> p a d", a=4))))(out_dst), "dS", 16)
            return pst["trf"]

        P.wait_bar("sync", bar0)
        vw1 = emit_ffn_weights_load()
        for e in ("scalar", "vector", "tensor"):
            P.wait_bar(e, bar0)
            P.wait(e, "dL", vw1)
        x_tiles = [AR1[:, k * 2048:(k + 1) * 2048] for k in range(4)]
        pst["v_xlb"] = [0, 0]

        def x_tile_dma(q, sup, hf):
            return q.dma_start(
                out=AR1[:, hf * 4096:(hf + 1) * 4096].rearrange("p (j a d) -> p j a d", j=1, a=2),
                in_=x_in[:, :].rearrange("(j a p) d -> p j a d", j=2, p=128)[
                    :, bass.DynSlice(cell["jj"], 1), sup * 4 + hf * 2:sup * 4 + hf * 2 + 2, :])

        xf = SM1
        xf2 = AR2[:, 8192:10240]
        xb = SM3b[:, 0:2048]
        sqd = AR2[:, 10240:12288]
        xlnbT = AR2b[:, 24576:32768]
        xln2T_sb = AR1[:, 8192:16384]
        v13_par = [0, 0]
        v_x1_rd = [0] * 4
        v_xf_rd = 0
        v_evb = 0
        v_evf = 0
        v_2a_st = 0
        fstate = {"tile_free": [], "st_tile": 0}

        for sup in range(4):
            vtrf = emit_ffn_sup(sup, x_tiles,
                         x1_scr[:, :].rearrange("(a p) d -> p a d", p=128)[:, sup * 4:sup * 4 + 4, :],
                         fstate, tile_dma=x_tile_dma)
            P.wait("scalar", "cV", vtrf)
            P.wait("vector", "cV", vtrf)
            P.wait("vector", "dSy0", v_2a_st)
            P.wait("scalar", "dSy0", v_2a_st)
            for tt in range(4):
                ti = sup * 4 + tt
                pp = tt % 2
                base = 128 + 32 * pp
                col = lambda k: ST[:, base + k:base + k + 1]
                x1t = x_tiles[tt]
                P.wait("vector", "cS", v13_par[pp])
                v1a = P.add("vector", (lambda x_, c_: (lambda q: q.reduce_sum(c_, x_, axis=AX.X)))(x1t, col(0)), "cV")
                v2 = P.add("scalar", (lambda x_, c_: (lambda q: q.activation(out=sqd[:, :], in_=x_, func=AF.Square, accum_out=c_)))(x1t, col(1)), "cS")
                P.wait("vector", "cS", v2)
                P.wait("vector", "cV", v1a)
                vv = P.add("vector", (lambda a, b: (lambda q: q.tensor_scalar(out=a, in0=b, scalar1=1.0 / D, scalar2=None, op0=ALU.mult)))(col(2), col(0)), "cV")
                vv = P.add("vector", (lambda a, b: (lambda q: q.tensor_scalar(out=a, in0=b, scalar1=1.0 / D, scalar2=None, op0=ALU.mult)))(col(3), col(1)), "cV")
                P.wait("vector", "cV", vv)
                vv = P.add("vector", (lambda a, b: (lambda q: q.tensor_tensor(out=a, in0=b, in1=b, op=ALU.mult)))(col(4), col(2)), "cV")
                P.wait("vector", "cV", vv)
                v3 = P.add("vector", (lambda a, b, c: (lambda q: q.tensor_tensor(out=a, in0=b, in1=c, op=ALU.subtract)))(col(5), col(3), col(4)), "cV")
                P.wait("scalar", "cV", v3)
                vl = P.add("scalar", (lambda a, b: (lambda q: q.activation(out=a, in_=b, func=AF.Ln, bias=cst[:, 0:1])))(col(6), col(5)), "cS")
                P.wait("scalar", "cS", vl)
                v4 = P.add("scalar", (lambda a, b: (lambda q: q.activation(out=a, in_=b, func=AF.Exp, scale=cst[:, 1:2])))(col(7), col(6)), "cS")
                P.wait("vector", "cS", v4)
                v5 = P.add("vector", (lambda a, b, c: (lambda q: q.tensor_scalar(out=a, in0=b, scalar1=c, scalar2=-1.0,
                                                                                 op0=ALU.mult, op1=ALU.mult)))(col(8), col(2), col(7)), "cV")
                P.wait("scalar", "cV", v5)
                P.wait("scalar", "cS", v_xf_rd)
                v6 = P.add("scalar", (lambda x_, c7, c8: (lambda q: q.activation(out=xf[:, :], in_=x_, func=AF.Identity,
                                                                                 scale=c7, bias=c8)))(x1t, col(7), col(8)), "cS")
                v_x1_rd[tt] = v6
                P.wait("vector", "cS", v6)
                v7a = P.add("vector", (lambda c_: (lambda q: q.reduce_sum(c_, xf[:, :], axis=AX.X)))(col(9)), "cV")
                P.wait("scalar", "cS", v6)
                v8 = P.add("scalar", (lambda c_: (lambda q: q.activation(out=sqd[:, :], in_=xf[:, :], func=AF.Square, accum_out=c_)))(col(10)), "cS")
                P.wait("vector", "cS", v8)
                P.wait("vector", "cV", v7a)
                vv = P.add("vector", (lambda a, b: (lambda q: q.tensor_scalar(out=a, in0=b, scalar1=1.0 / D, scalar2=None, op0=ALU.mult)))(col(11), col(9)), "cV")
                vv = P.add("vector", (lambda a, b: (lambda q: q.tensor_scalar(out=a, in0=b, scalar1=1.0 / D, scalar2=None, op0=ALU.mult)))(col(12), col(10)), "cV")
                P.wait("vector", "cV", vv)
                vv = P.add("vector", (lambda a, b: (lambda q: q.tensor_tensor(out=a, in0=b, in1=b, op=ALU.mult)))(col(13), col(11)), "cV")
                P.wait("vector", "cV", vv)
                v9 = P.add("vector", (lambda a, b, c: (lambda q: q.tensor_tensor(out=a, in0=b, in1=c, op=ALU.subtract)))(col(14), col(12), col(13)), "cV")
                P.wait("scalar", "cV", v9)
                vl = P.add("scalar", (lambda a, b: (lambda q: q.activation(out=a, in_=b, func=AF.Ln, bias=cst[:, 0:1])))(col(15), col(14)), "cS")
                P.wait("scalar", "cS", vl)
                v10 = P.add("scalar", (lambda a, b: (lambda q: q.activation(out=a, in_=b, func=AF.Exp, scale=cst[:, 1:2])))(col(16), col(15)), "cS")
                P.wait("vector", "cS", v10)
                v11 = P.add("vector", (lambda a, b, c: (lambda q: q.tensor_scalar(out=a, in0=b, scalar1=c, scalar2=-1.0,
                                                                                  op0=ALU.mult, op1=ALU.mult)))(col(17), col(11), col(16)), "cV")
                P.wait("scalar", "cV", v11)
                P.wait("scalar", "cP", pst["v_xlb_2a"][0])
                v12 = P.add("scalar", lambda q: q.activation(out=xb[:, :], in_=xf[:, :], func=AF.Identity), "cS")
                P.wait("scalar", "cP", pst["v_xlb_2a"][1])
                v13 = P.add("scalar", (lambda c16, c17: (lambda q: q.activation(out=xf2[:, :], in_=xf[:, :], func=AF.Identity,
                                                                                scale=c16, bias=c17)))(col(16), col(17)), "cS")
                v13_par[pp] = v13
                v_xf_rd = v13
                for g in range(4):
                    P.wait("tensor", "cS", v12)
                    P.wait("tensor", "cV", pst["tr_evac"])
                    for c4 in range(4):
                        c = g * 4 + c4
                        P.add("tensor", (lambda c_, c4_, g_: (lambda q: q.transpose(
                            PSb[:, (g_ % 2) * 1024 + c4_ * 128:(g_ % 2) * 1024 + (c4_ + 1) * 128],
                            xb[:, c_ * 128:(c_ + 1) * 128], idb[:, :])))(c, c4, g), "cP")
                    vp = P.cnt["cP"]
                    P.wait("vector", "cP", vp)
                    pst["tr_evac"] = P.add("vector", (lambda g_, tt_: (lambda q: q.tensor_copy(
                        out=xlnbT[:, :].rearrange("p (c t) -> p c t", c=16)[:, g_ * 4:(g_ + 1) * 4, tt_ * 128:(tt_ + 1) * 128],
                        in_=PSb[:, (g_ % 2) * 1024:(g_ % 2) * 1024 + 512].rearrange("p (c t) -> p c t", c=4))))(g, tt), "cV")
                pst["v_xlb_2a"][0] = P.cnt["cP"]
                for g in range(8):
                    P.wait("tensor", "cS", v13)
                    bk = 2 + g % 2
                    bank_wait(bk)
                    for c2 in range(2):
                        c = g * 2 + c2
                        P.add("tensor", (lambda c_, c2_, bk_: (lambda q: q.transpose(
                            BANK(bk_)[:, c2_ * 128:(c2_ + 1) * 128],
                            xf2[:, c_ * 128:(c_ + 1) * 128], idf[:, :])))(c, c2, bk), "cP")
                    vp = P.cnt["cP"]
                    P.wait("scalar", "cP", vp)
                    vev = P.add("scalar", (lambda g_, tt_, bk_: (lambda q: q.copy(
                        out=xln2T_sb[:, :].rearrange("p (c t) -> p c t", c=16)[:, g_ * 2:(g_ + 1) * 2, tt_ * 128:(tt_ + 1) * 128],
                        in_=BANK(bk_, 256).rearrange("p (c t) -> p c t", c=2))))(g, tt, bk), "cS")
                    BG[bk]["cS"] = vev
                pst["v_xlb_2a"][1] = P.cnt["cP"]
            P.wait("sync", "cV", pst["tr_evac"])
            P.wait("sync", "cS", BG[2]["cS"])
            P.wait("sync", "cS", BG[3]["cS"])
            P.wait("sync", "dSy0", v_2a_st)
            P.add("sync", (lambda s_: (lambda q: q.dma_start(
                out=xlnbT_scr[:, :].rearrange("(c p) t -> p c t", p=128)[:, :, s_ * 512:(s_ + 1) * 512],
                in_=xlnbT[:, :].rearrange("p (c t) -> p c t", c=16))))(sup), "dSy0", 16)
            v_2a_st = P.add("sync", (lambda s_: (lambda q: q.dma_start(
                out=xln2T_scr[:, :].rearrange("(c p) t -> p c t", p=128)[:, :, s_ * 512:(s_ + 1) * 512],
                in_=xln2T_sb[:, :].rearrange("p (c t) -> p c t", c=16))))(sup), "dSy0", 16)
            fstate["tile_free_h"] = [
                [("cV", pst["trf"]), ("cS", max(v_x1_rd[0], v_x1_rd[1]))],
                [("cV", pst["trf"]), ("cS", max(v_x1_rd[2], v_x1_rd[3]))],
            ]
        P.wait("sync", "dSy0", P.cnt["dSy0"])
        bar2a = P.bar()

        for e in ENGS[:4]:
            P.wait_bar(e, bar2a)
        xnb = AR1b
        chunkbuf = AR3[:, 2048:4096]
        xpw_sb = AR3b[:, 12288:16384]
        projd = AR3b[:, 0:2048]
        proj_sb = AR2[:, 0:4096]
        v_pw = P.add("sync", lambda q: q.dma_start(
            out=xpw_sb[:, :].rearrange("p (c e) -> p c e", c=16),
            in_=xpwT_b_scr[:, :].rearrange("(c p) e -> p c e", p=128)), "dR", 16)
        v_dwl = P.add("sync", lambda q: q.dma_start(out=dtw_sb[:, :], in_=dtwT_b_scr[:, :]), "dR", 16)
        v_aff_last = 0
        v_mm_half = {0: 0, 1: 0}
        v_ev_last = 0
        for half in range(2):
            for i in range(8):
                c = half * 8 + i
                vx = P.add("sync", (lambda c_: (lambda q: q.dma_start(
                    out=chunkbuf[:, :], in_=xln2T_scr[c_ * 128:(c_ + 1) * 128, :])))(c), "dL", 16)
                P.wait("scalar", "dL", vx)
                if half == 1:
                    P.wait("scalar", "cP", v_mm_half[0])
                v_aff_last = P.add("scalar", (lambda c_, i_: (lambda q: q.activation(
                    out=xnb[:, i_ * 2048:(i_ + 1) * 2048], in_=chunkbuf[:, :], func=AF.Identity,
                    scale=nw[:, c_:c_ + 1], bias=nb[:, c_:c_ + 1])))(c, i), "cS")
                P.wait("sync", "cS", v_aff_last)
            for po in range(2):
                for tp2 in range(4):
                    P.wait("tensor", "cS", v_aff_last)
                    P.wait("tensor", "dR", max(v_pw, v_dwl))
                    P.wait("tensor", "cV", v_ev_last)
                    for i in range(8):
                        P.add("tensor", (lambda po_, tp_, i_, h_: (lambda q: q.matmul(
                            PS[:, 0:512],
                            xpw_sb[:, (h_ * 8 + i_) * 256 + po_ * 128:(h_ * 8 + i_) * 256 + (po_ + 1) * 128],
                            xnb[:, i_ * 2048 + tp_ * 512:i_ * 2048 + (tp_ + 1) * 512],
                            start=(i_ == 0), stop=(i_ == 7))))(po, tp2, i, half), "cP")
                    vp = P.cnt["cP"]
                    P.wait("vector", "cP", vp)
                    if half == 0:
                        v_ev_last = P.add("vector", (lambda po_, tp_: (lambda q: q.tensor_copy(
                            out=proj_sb[:, po_ * 2048 + tp_ * 512: po_ * 2048 + (tp_ + 1) * 512],
                            in_=PS[:, 0:512])))(po, tp2), "cV")
                    else:
                        v_ev_last = P.add("vector", (lambda po_, tp_: (lambda q: q.tensor_tensor(
                            out=proj_sb[:, po_ * 2048 + tp_ * 512: po_ * 2048 + (tp_ + 1) * 512],
                            in0=proj_sb[:, po_ * 2048 + tp_ * 512: po_ * 2048 + (tp_ + 1) * 512],
                            in1=PS[:, 0:512], op=ALU.add)))(po, tp2), "cV")
            v_mm_half[half] = P.cnt["cP"]
        P.wait("vector", "cV", v_ev_last)
        v_pd = P.add("vector", lambda q: q.tensor_copy(out=projd[:, :], in_=proj_sb[:, 0:2048]), "cV")
        P.wait("sync", "cV", v_ev_last)
        v_bc = P.add("sync", lambda q: q.dma_start(out=PhA[2048:2176, :], in_=proj_sb[:, 2048:4096]), "dS", 16)
        v_bcB = P.add("sync", lambda q: q.dma_start(out=PhB[2048:2176, :], in_=proj_sb[:, 2048:4096]), "dS", 16)
        xnf = AR1[:, 0:2048]
        t1p = [AR1[:, 4096:5120], AR1[:, 6144:7168]]
        t2p = [AR1[:, 5120:6144], AR1[:, 7168:8192]]
        dpc = [SM2[:, 0:1024], SM2[:, 1024:2048]]
        wpc = [SM3[:, 0:1024], SM3[:, 1024:2048]]
        v_dst = {0: 0, 1: 0}
        v_sp = {0: 0, 1: 0}
        v_vd_par = {0: 0, 1: 0}
        v_xnf_rd = 0
        v_cc2 = 0
        for di, dblk in enumerate(list(range(8, 16)) + list(range(8))):
            if di == 8:
                P.wait("gpsimd", "dSa", P.cnt["dSa"])
                P.wait("gpsimd", "dSb", P.cnt["dSb"])
                P.wait("gpsimd", "dS", v_bcB)
                v_cc2 = P.add("gpsimd", lambda q: q.collective_compute(
                    "AllGather", ALU.bypass, replica_groups=[[0, 1], [2, 3], [4, 5], [6, 7]],
                    ins=[PhB.ap().opt()], outs=[PgxB.ap().opt()]), "ccm")
            P.wait("sync", "cV", v_xnf_rd)
            vx = P.add("sync", (lambda d_: (lambda q: q.dma_start(
                out=chunkbuf[:, :], in_=xln2T_scr[d_ * 128:(d_ + 1) * 128, :])))(dblk), "dL", 16)
            P.wait("scalar", "dL", vx)
            P.wait("scalar", "cP", P.cnt["cP"] if di == 0 else 0)
            v_xnf = P.add("scalar", (lambda d_: (lambda q: q.activation(
                out=xnf[:, :], in_=chunkbuf[:, :], func=AF.Identity,
                scale=nw[:, d_:d_ + 1], bias=nb[:, d_:d_ + 1])))(dblk), "cS")
            P.wait("sync", "cS", v_xnf)
            for tp in range(2):
                par = tp % 2
                pslice = PS[:, 1024 + par * 1024: 2048 + par * 1024]
                P.wait("tensor", "cV", v_pd)
                P.wait("tensor", "cS", v_sp[par])
                for hh in range(2):
                    P.add("tensor", (lambda d_, tp_, ps_, hh_: (lambda q: q.matmul(
                        ps_[:, hh_ * 512:(hh_ + 1) * 512], dtw_sb[:, d_ * 128:(d_ + 1) * 128],
                        projd[:, tp_ * 1024 + hh_ * 512:tp_ * 1024 + (hh_ + 1) * 512],
                        start=True, stop=True)))(dblk, tp, pslice, hh), "cP")
                vp = P.cnt["cP"]
                P.wait("scalar", "cP", vp)
                P.wait("scalar", "cV", v_vd_par[par])
                P.wait("scalar", "dS%s" % ("a" if par == 0 else "b"), v_dst[par])
                va = P.add("scalar", (lambda d_, ps_, pr_: (lambda q: q.activation(out=t1p[pr_][:, :], in_=ps_, func=AF.Abs,
                                                                         bias=dtb[:, d_:d_ + 1])))(dblk, pslice, par), "cS")
                P.wait("scalar", "cS", va)
                va = P.add("scalar", (lambda pr_: (lambda q: q.activation(out=t2p[pr_][:, :], in_=t1p[pr_][:, :], func=AF.Exp, scale=cst[:, 2:3])))(par), "cS")
                P.wait("scalar", "cS", va)
                va = P.add("scalar", (lambda pr_: (lambda q: q.activation(out=t1p[pr_][:, :], in_=t2p[pr_][:, :], func=AF.Ln, bias=cst[:, 3:4])))(par), "cS")
                P.wait("scalar", "cS", va)
                v_sp[par] = P.add("scalar", (lambda d_, ps_, par_: (lambda q: q.activation(
                    out=dpc[par_][:, :], in_=ps_, func=AF.Relu, bias=dtb[:, d_:d_ + 1])))(dblk, pslice, par), "cS")
                P.wait("vector", "cS", v_sp[par])
                v_vd_par[par] = P.add("vector", (lambda par_: (lambda q: q.tensor_tensor(
                    out=dpc[par_][:, :], in0=dpc[par_][:, :], in1=t1p[par_][:, :], op=ALU.add)))(par), "cV")
                P.wait("vector", "dS%s" % ("a" if par == 0 else "b"), v_dst[par])
                P.wait("vector", "cV", v_vd_par[par])
                vw = P.add("vector", (lambda par_, tp_: (lambda q: q.tensor_tensor(
                    out=wpc[par_][:, :], in0=dpc[par_][:, :], in1=xnf[:, tp_ * 1024:(tp_ + 1) * 1024],
                    op=ALU.mult)))(par, tp), "cV")
                if tp == 1:
                    v_xnf_rd = vw
                P.wait("sync", "cV", vw)
                spar = "dSa" if par == 0 else "dSb"
                tgt = PhB if dblk >= 8 else PhA
                gb = (dblk % 8) * 128
                P.add("sync", (lambda t_, g_, tp_, par_: (lambda q: q.dma_start(
                    out=t_[g_:g_ + 128, tp_ * 1024:(tp_ + 1) * 1024],
                    in_=dpc[par_][:, :])))(tgt, gb, tp, par), spar, 16)
                v_dst[par] = P.add("sync", (lambda t_, g_, tp_, par_: (lambda q: q.dma_start(
                    out=t_[1024 + g_:1024 + g_ + 128, tp_ * 1024:(tp_ + 1) * 1024],
                    in_=wpc[par_][:, :])))(tgt, gb, tp, par), spar, 16)
        bar2b = P.bar()

        for e in ENGS:
            P.wait_bar(e, bar2b)
        delta_sb = AR1[:, 0:8192]
        w_sb = AR1[:, 8192:16384]
        y_acc = AR2[:, 0:8192]
        dA2 = [AR3[:, 0:2048], AR3[:, 2048:4096]]
        u2 = PS[:, 0:2048]
        tmp2 = [AR3[:, 6144:8192], AR2[:, 12288:14336]]
        h2 = [PS[:, 2048:4096], AR2[:, 10240:12288]]
        y_bf = [AR2[:, 14336:15360].bitcast(BF16), AR2[:, 15360:16384].bitcast(BF16)]
        bufB2 = [SM0[:, 0:2048], SM1[:, 0:2048]]
        bufC2 = [SM2[:, 0:2048], SM3[:, 0:2048]]
        v_dA = {}
        v_u2 = {}
        v_tts = {}
        v_tts2 = {}
        v_ym2 = {}
        v_acm2 = {}
        v_uB_par = {0: 0, 1: 0}
        v_ymC_par = {0: 0, 1: 0}
        v_store_y = {0: 0, 1: 0}
        v_cast_prev_cc = 0
        v_ext_last = 0
        v_pair_end = 0
        tmp_reader = [None, None]
        NPAIR = NST * 4
        q = 0
        P.wait("sync", "ccm", v_cc2)
        v_pbc = P.add("sync", lambda q_: q_.dma_start(
            out=PgxB[:, :].rearrange("(b r) t -> b r t", b=2)[bass.DynSlice(cell["jj"], 1), :, :],
            in_=PhA[:, :].rearrange("(b r) t -> b r t", b=1)), "dR", 16)
        P.wait("sync", "dR", v_pbc)
        for cc in range(NCC):
            jb, tloc = cc // 2, cc % 2
            if cc > 0:
                P.wait("sync", "cS", v_dA[2 * q - 1])
                P.wait("sync", "cV", v_u2[q - 1])
            vL1 = P.add("sync", (lambda jb_, tl_: (lambda q_: q_.dma_start(
                out=delta_sb[:, :].rearrange("p (g d t) -> p g d t", g=1, d=8),
                in_=PgxB[:, :].rearrange("(g d p) t -> p g d t", g=2, d=17)[
                    :, jb_:jb_ + 1, 0:8, tl_ * TC:(tl_ + 1) * TC])))(jb, tloc), "dL", 16)
            vL2 = P.add("sync", (lambda jb_, tl_: (lambda q_: q_.dma_start(
                out=w_sb[:, :].rearrange("p (g d t) -> p g d t", g=1, d=8),
                in_=PgxB[:, :].rearrange("(g d p) t -> p g d t", g=2, d=17)[
                    :, jb_:jb_ + 1, 8:16, tl_ * TC:(tl_ + 1) * TC])))(jb, tloc), "dL", 16)
            P.wait("vector", "cS", v_cast_prev_cc)
            P.wait("vector", "cV", v_ext_last)
            for n in range(NST):
                par_n = n % 2
                sB = "dRB%d" % par_n
                sC = "dRC%d" % par_n
                P.wait("sync", "cV", v_uB_par[par_n])
                P.wait("sync", "cV", v_ymC_par[par_n])
                for rr in range(2):
                    rB = P.add("sync", (lambda n_, p_, jb_, tl_, rr_: (lambda q_: q_.dma_start(
                        out=bufB2[p_][:, rr_ * TC:(rr_ + 1) * TC],
                        in_=PgxB[jb_ * 2176 + 2048 + n_:jb_ * 2176 + 2049 + n_,
                                tl_ * TC:(tl_ + 1) * TC].broadcast_to([128, TC]))))(
                        n, par_n, jb, tloc, rr), sB, 16)
                for rr in range(4):
                    rC = P.add("sync", (lambda n_, p_, rr_: (lambda q_: q_.dma_start(
                        out=bufC2[p_][:, rr_ * 512:(rr_ + 1) * 512],
                        in_=PgxB[2112 + n_:2113 + n_, 0:512].broadcast_to([128, 512]))))(n, par_n, rr), sC, 16)
                for dp in range(4):
                    qs = q % 2
                    if q >= 2:
                        P.wait("scalar", "cV", v_tts2[q - 2])
                    if n == 0 and dp == 0:
                        P.wait("scalar", "dL", vL2)
                    for hh in range(2):
                        dblk = 2 * dp + hh
                        v_dA[2 * q + hh] = P.add("scalar", (lambda d_, n_, s_, hh_: (lambda q_: q_.activation(
                            out=dA2[s_][:, hh_ * 1024:(hh_ + 1) * 1024],
                            in_=delta_sb[:, d_ * TC:(d_ + 1) * TC], func=AF.Exp,
                            scale=A_sb[:, d_ * 64 + n_: d_ * 64 + n_ + 1])))(dblk, n, qs, hh), "cS")
                    if dp == 0:
                        P.wait("vector", sB, rB)
                        P.wait("vector", sC, rC)
                        if n == 0:
                            P.wait("vector", "dL", vL2)
                    P.wait("vector", "cV", v_pair_end)
                    v_u2[q] = P.add("vector", (lambda dp_, p_: (lambda q_: q_.tensor_tensor(
                        out=u2[:, :], in0=w_sb[:, dp_ * 2048:(dp_ + 1) * 2048],
                        in1=bufB2[p_][:, :], op=ALU.mult)))(dp, par_n), "cV")
                    P.wait("vector", "cS", v_dA[2 * q + 1])
                    P.wait("vector", "cV", v_u2[q])
                    for hh in range(2):
                        dblk = 2 * dp + hh
                        ini = 0.0 if cc == 0 else carry[:, dblk * 64 + n: dblk * 64 + n + 1]
                        v_tts[2 * q + hh] = P.add("vector", (lambda s_, hh_, i_: (lambda q_: q_.tensor_tensor_scan(
                            out=h2[s_][:, hh_ * 1024:(hh_ + 1) * 1024],
                            data0=dA2[s_][:, hh_ * 1024:(hh_ + 1) * 1024],
                            data1=u2[:, hh_ * 1024:(hh_ + 1) * 1024], initial=i_,
                            op0=ALU.mult, op1=ALU.add)))(qs, hh, ini), "cV")
                    v_tts2[q] = v_tts[2 * q + 1]
                    if q >= 1 and cc < NCC - 1:
                        qp = q - 1
                        for hh in range(2):
                            mb = 2 * qp + hh
                            v_ext_last = P.add("vector", (lambda d_, n_, s_, hh_: (lambda q_: q_.tensor_copy(
                                out=carry[:, d_ * 64 + n_: d_ * 64 + n_ + 1],
                                in_=h2[s_][:, hh_ * 1024 + TC - 1: hh_ * 1024 + TC])))(
                                (mb // 1) % 8 if False else 2 * ((qp) % 4) + hh, (qp // 4) % NST, qp % 2, hh), "cV")
                    P.wait("vector", "cV", v_tts2[q])
                    if n == 0:
                        v_ym2[q] = P.add("vector", (lambda dp_, s_, p_: (lambda q_: q_.tensor_tensor(
                            out=y_acc[:, dp_ * 2048:(dp_ + 1) * 2048],
                            in0=h2[s_][:, :], in1=bufC2[p_][:, :],
                            op=ALU.mult)))(dp, qs, par_n), "cV")
                        v_acm2[q] = None
                    else:
                        tr = tmp_reader[qs]
                        if tr is not None:
                            P.wait("vector", tr[0], tr[1])
                        v_ym2[q] = P.add("vector", (lambda s_, p_: (lambda q_: q_.tensor_tensor(
                            out=tmp2[s_][:, :], in0=h2[s_][:, :],
                            in1=bufC2[p_][:, :], op=ALU.mult)))(qs, par_n), "cV")
                        sG = "dG%d" % dp
                        P.wait("gpsimd", "cV", v_ym2[q])
                        if v_acm2.get(q - 4) is not None:
                            P.wait("gpsimd", sG, v_acm2[q - 4][1])
                        va = P.add("gpsimd", (lambda dp_, s_: (lambda q_: q_.dma_start(
                            out=y_acc[:, dp_ * 2048:(dp_ + 1) * 2048], in_=tmp2[s_][:, :],
                            accum_op=ALU.add)))(dp, qs), sG, 16)
                        v_acm2[q] = (sG, va)
                        tmp_reader[qs] = (sG, va)
                    if dp == 3:
                        v_uB_par[par_n] = v_u2[q]
                        v_ymC_par[par_n] = v_ym2[q]
                    v_pair_end = P.cnt["cV"]
                    q += 1
            if cc < NCC - 1:
                qp = q - 1
                P.wait("vector", "cV", v_tts2[qp])
                for hh in range(2):
                    v_ext_last = P.add("vector", (lambda d_, n_, s_, hh_: (lambda q_: q_.tensor_copy(
                        out=carry[:, d_ * 64 + n_: d_ * 64 + n_ + 1],
                        in_=h2[s_][:, hh_ * 1024 + TC - 1: hh_ * 1024 + TC])))(
                        2 * (qp % 4) + hh, (qp // 4) % NST, qp % 2, hh), "cV")
            v_cast_prev_cc = 0
            for dp in range(4):
                pb = dp % 2
                last_q = (cc * NPAIR) + 63 * 4 + dp
                ac = v_acm2[last_q]
                P.wait("scalar", ac[0], ac[1])
                P.wait("scalar", "dSy%d" % pb, v_store_y[pb])
                vca = P.add("scalar", (lambda dp_, pb_: (lambda q_: q_.activation(
                    out=y_bf[pb_][:, :], in_=y_acc[:, dp_ * 2048:(dp_ + 1) * 2048], func=AF.Identity)))(dp, pb), "cS")
                v_cast_prev_cc = vca
                P.wait("sync", "cS", vca)
                v_store_y[pb] = P.add("sync", (lambda dp_, hb_, tl_, pb_: (lambda q_: q_.dma_start(
                    out=y_scr2[:, :].rearrange("(h d p) t -> p h d t", h=2, d=8)[
                        :, hb_:hb_ + 1, 2 * dp_:2 * dp_ + 2, tl_ * TC:(tl_ + 1) * TC],
                    in_=y_bf[pb_][:, :].rearrange("p (h r t) -> p h r t", h=1, r=2))))(
                    dp, cc // 2, cc % 2, pb), "dSy%d" % pb, 16)
            if cc == 1:
                P.wait("gpsimd", "dSy0", v_store_y[0])
                P.wait("gpsimd", "dSy1", v_store_y[1])
                P.add("gpsimd", lambda q_: q_.collective_compute(
                    "AllGather", ALU.bypass, replica_groups=[[0, 1], [2, 3], [4, 5], [6, 7]],
                    ins=[y_scr2[0:DS, :].opt()], outs=[ygx2[0:D, :].opt()]), "ccm")
        bar3 = P.bar()

        if debug:
            P.wait_bar("sync", bar3)
            P.add("sync", lambda q: q.dma_start(out=y_dbg[:, :], in_=y_scr2[:, :]), "dS", 16)
        P.wait_bar("gpsimd", bar3)
        v_cc = P.add("gpsimd", lambda q: q.collective_compute(
            "AllGather", ALU.bypass, replica_groups=[[0, 1], [2, 3], [4, 5], [6, 7]],
            ins=[y_scr2[DS:2 * DS, :].opt()], outs=[ygx2[D:2 * D, :].opt()]), "ccm")

        for e in ENGS[:4]:
            P.wait_bar(e, bar3)
        yrhs = AR1b[:, 16384:24576]
        xlr = AR2b[:, 16384:32768]
        wpair = [AR3b[:, 0:4096], AR3b[:, 4096:8192]]
        x2t = [AR1[:, k * 2048:(k + 1) * 2048] for k in range(4)]
        ys2 = AR2[:, 7680:8192]
        v_ffn3 = 0
        v_out_st = 0
        DynS = bass.DynSlice
        for tp in range(4):
            P.wait("sync", "cP", P.cnt["cP"] if tp else 0)
            P.wait("sync", "cV", v_ffn3)
            P.wait("sync", "dS", v_out_st)
            vx1 = P.add("sync", (lambda tp_: (lambda q: q.dma_start(
                out=AR1[:, 0:8192].rearrange("p (a d) -> p a d", a=4),
                in_=x1_scr[:, :].rearrange("(a p) d -> p a d", p=128)[:, tp_ * 4:(tp_ + 1) * 4, :])))(tp), "dL", 16)
            if tp == 0:
                P.wait("sync", "ccm", v_cc)
            vy1 = P.add("sync", (lambda tp_: (lambda q: q.dma_start(
                out=yrhs[:, 0:4096].rearrange("p (h c t) -> p h c t", h=1, c=8),
                in_=ygx2[:, :].rearrange("(h c p) t -> p h c t", h=2, c=16)[
                    :, DynS(cell["jj"], 1), DynS(cell["j8"], 8),
                    tp_ * 512:(tp_ + 1) * 512])))(tp), "dL", 16)
            v_x1 = [vy1] * 4
            vy3 = P.add("sync", (lambda tp_: (lambda q: q.dma_start(
                out=xlr[:, 0:8192].rearrange("p (c t) -> p c t", c=16),
                in_=xlnbT_scr[:, :].rearrange("(c p) t -> p c t", p=128)[:, :, tp_ * 512:(tp_ + 1) * 512])))(tp), "dRB0", 16)
            vy2 = P.add("sync", (lambda tp_: (lambda q: q.dma_start(
                out=yrhs[:, 4096:8192].rearrange("p (h c t) -> p h c t", h=1, c=8),
                in_=ygx2[:, :].rearrange("(h c p) t -> p h c t", h=2, c=16)[
                    :, DynS(cell["jj"], 1), DynS(cell["k8"], 8),
                    tp_ * 512:(tp_ + 1) * 512])))(tp), "dRC0", 16)
            P.wait("vector", "dL", vy1)
            P.wait("vector", "dRC0", vy2)
            P.wait("vector", "dRB0", vy3)
            skipb = [AR2b[:, 0:512], AR2b[:, 512:1024]]
            vv_sc = [0, 0]
            v_skip = 0
            for c in range(17):
                if c < 16:
                    vv_sc[c % 2] = P.add("vector", (lambda c_: (lambda q: q.tensor_scalar(
                        out=skipb[c_ % 2], in0=xlr[:, c_ * 512:(c_ + 1) * 512],
                        scalar1=dpar[:, c_:c_ + 1], scalar2=None, op0=ALU.mult)))(c), "cV")
                if c >= 1:
                    cm = c - 1
                    P.wait("vector", "cV", vv_sc[cm % 2])
                    v_skip = P.add("vector", (lambda c_: (lambda q: q.tensor_tensor(
                        out=yrhs[:, c_ * 512:(c_ + 1) * 512], in0=yrhs[:, c_ * 512:(c_ + 1) * 512],
                        in1=skipb[c_ % 2], op=ALU.add)))(cm), "cV")
            v_wl = {0: 0, 1: 0}
            v_add = 0
            v_oev = 0
            for oc in range(16):
                wp = oc % 2
                sw = "dX%d" % wp
                P.wait("sync", "cP", v_wl[wp])
                vwo = P.add("sync", (lambda oc_, wp_: (lambda q: q.dma_start(
                    out=wpair[wp_][:, 0:2048].rearrange("p (c o) -> p c o", c=16),
                    in_=wo_b_scr[:, oc_ * 128:(oc_ + 1) * 128].rearrange("(c p) o -> p c o", p=128))))(oc, wp), sw, 16)
                P.wait("tensor", sw, vwo)
                if v_add:
                    P.wait("tensor", "cV", v_add)
                if oc == 0:
                    P.wait("tensor", "cV", v_skip)
                P.wait("tensor", "dRC0", vy2)
                P.wait("tensor", "dL", vy1)
                for dc in range(16):
                    P.add("tensor", (lambda dc_, wp_: (lambda q: q.matmul(
                        BANK(0), wpair[wp_][:, dc_ * 128:(dc_ + 1) * 128],
                        yrhs[:, dc_ * 512:(dc_ + 1) * 512], start=(dc_ == 0), stop=(dc_ == 15))))(dc, wp), "cP")
                v_wl[wp] = P.cnt["cP"]
                P.wait("scalar", "cP", v_wl[wp])
                P.wait("scalar", "cV", v_add if oc else 0)
                v_oev = P.add("scalar", lambda q: q.copy(out=ys2[:, :], in_=BANK(0)), "cS")
                P.wait("tensor", "cS", v_oev)
                for tt in range(4):
                    P.add("tensor", (lambda tt_: (lambda q: q.transpose(
                        BANK(1)[:, tt_ * 128:(tt_ + 1) * 128], ys2[:, tt_ * 128:(tt_ + 1) * 128], idf[:, :])))(tt), "cP")
                vp = P.cnt["cP"]
                P.wait("vector", "cP", vp)
                for tt in range(4):
                    P.wait("vector", "dL", v_x1[tt])
                    v_add = P.add("vector", (lambda tt_, oc_: (lambda q: q.tensor_tensor(
                        out=x2t[tt_][:, oc_ * 128:(oc_ + 1) * 128], in0=x2t[tt_][:, oc_ * 128:(oc_ + 1) * 128],
                        in1=BANK(1)[:, tt_ * 128:(tt_ + 1) * 128], op=ALU.add)))(tt, oc), "cV")
            P.wait("sync", "cP", max(v_wl[0], v_wl[1]))
            vw5 = emit_ffn_weights_load()
            for e in ("scalar", "vector", "tensor"):
                P.wait(e, "dL", vw5)
            f5state = {"tile_free": [], "st_tile": 0}
            v_ffn3 = emit_ffn_sup(0, x2t,
                                  out_c[:, :].rearrange("(a p) d -> p a d", p=128)[:, tp * 4:tp * 4 + 4, :],
                                  f5state, tile_dma=None, v_tiles_ready=v_add)
            v_out_st = P.cnt["dS"]
        P.wait("sync", "dS", P.cnt["dS"])

        import os as _os
        _dbg_emit = bool(_os.environ.get("KRN_DBG_EMIT"))

        def emitter(name):
            def run(q):
                for idx, (kind, *rest) in enumerate(P.ops[name]):
                    if kind == "wait":
                        q.wait_ge(sems[rest[0]], rest[1])
                    else:
                        fn, inc, k = rest
                        try:
                            instr = fn(q)
                        except Exception:
                            if _dbg_emit:
                                import traceback as _tb
                                print(f"EMIT FAIL {name} op#{idx}")
                                _tb.print_exc()
                            raise
                        if inc is not None and instr is not None:
                            instr.then_inc(sems[inc], k)
            return run

        block.sync(emitter("sync"))
        block.scalar(emitter("scalar"))
        block.vector(emitter("vector"))
        block.tensor(emitter("tensor"))
        block.gpsimd(emitter("gpsimd"))
    return nc


def _prep_inputs(core, inputs):
    b, j = core // 2, core % 2
    pm = np.arange(D) if j == 0 else np.r_[1024:2048, 0:1024]
    g = lambda k: np.asarray(inputs[k])
    x = g("L0_attn_x")[b][:, pm]
    xpw = g("L0_attn_x_proj_w")[:, pm]
    dtw = g("L0_attn_dt_proj_w")[pm]
    dtbv = g("L0_attn_dt_proj_b")[pm]
    alog = g("L0_attn_A_log")[pm][:DS]
    dparf = g("L0_attn_D_param")[pm]
    nwf = g("L0_attn_norm_w")[pm]
    nbf = g("L0_attn_norm_b")[pm]
    wout = g("L0_attn_out_proj_w")[pm][:, pm]
    A_cat = g("L0_ffn_A").reshape(ER, D)[:, pm]
    B_cat = g("L0_ffn_B").transpose(1, 0, 2).reshape(D, ER)[pm]
    wr = g("L0_ffn_Wr")[:, pm]
    br = g("L0_ffn_br")
    C = np.ascontiguousarray
    return dict(
        x_in=C(x.astype(np.float32)),
        xpwT=C(xpw.T.astype(np.float32)),
        dtwT=C(dtw.T.astype(np.float32)),
        dtb_r=C(dtbv.reshape(16, 128).T.astype(np.float32)),
        alog_r=C(alog.reshape(8, 128, NST).transpose(1, 0, 2).reshape(128, 512).astype(np.float32)),
        dpar_r=C(dparf.reshape(16, 128).T.astype(np.float32)),
        nw_r=C(nwf.reshape(16, 128).T.astype(np.float32)),
        nb_r=C(nbf.reshape(16, 128).T.astype(np.float32)),
        woutT=C(wout.T.astype(np.float32)),
        ffnAT=C(A_cat.T.astype(np.float32)),
        ffnBT=C(B_cat.T.astype(np.float32)),
        wrT=C(wr.T.astype(np.float32)),
        br_in=C(br.reshape(1, 8).astype(np.float32)),
        blkid_in=C(np.repeat(np.eye(8, dtype=np.float32), 64, axis=1)),
        idf_in=np.eye(128, dtype=np.float32),
        idb_in=np.eye(128, dtype=np.float32).astype(ml_dtypes.bfloat16),
        cst_in=np.array([[EPS, -0.5, -1.0, 1.0]], np.float32),
        jv_in=np.array([[j]], np.int32),
        kv_in=np.array([[1 - j]], np.int32),
    )


def kernel(**inputs):
    nc = build_nc()
    in_maps = [_prep_inputs(c, inputs) for c in range(8)]
    res = run_bass_kernel_spmd(nc, in_maps, core_ids=list(range(8)))
    out = np.empty((4, TOK, D), np.float32)
    for c in range(8):
        b, j = c // 2, c % 2
        pm = np.arange(D) if j == 0 else np.r_[1024:2048, 0:1024]
        out[b, j * MYT:(j + 1) * MYT, :] = res.results[c]["out_c"][:, pm]
    return out


# version 25
# speedup vs baseline: 1.0528x; 1.0013x over previous
import numpy as np
import ml_dtypes
from contextlib import ExitStack

import concourse.bass as bass
import concourse.mybir as mybir
from concourse.bass_utils import run_bass_kernel_spmd

F32 = mybir.dt.float32
BF16 = mybir.dt.bfloat16
I32 = mybir.dt.int32
AF = mybir.ActivationFunctionType
ALU = mybir.AluOpType
AX = mybir.AxisListType

TOK, D, DS, NST, RHO, ER = 4096, 2048, 1024, 64, 128, 512
TC, NCC = 1024, 4
NSUP = 8
MYT = 2048
EPS = 1e-5
SCL = 0.5 * (1.0 / 64.0)
GP_NS = frozenset(n for n in range(NST) if n % 3 != 0)

ENGS = ("sync", "scalar", "vector", "tensor", "gpsimd")
SEMS = ("dL", "dR", "dS", "cV", "cS", "cP", "cG", "ccm",
        "dRB0", "dRB1", "dRC0", "dRC1", "dSy0", "dSy1", "dSa", "dSb",
        "dG0", "dG1", "dG2", "dG3", "dG4", "dG5", "dG6", "dG7", "dX0", "dX1")


class Prog:
    def __init__(self):
        self.ops = {e: [] for e in ENGS}
        self.cnt = {s: 0 for s in SEMS}

    def add(self, eng, fn, inc=None, k=1):
        self.ops[eng].append(("op", fn, inc, k))
        if inc is not None:
            self.cnt[inc] += k
            return self.cnt[inc]
        return None

    def wait(self, eng, sem, val):
        if val:
            self.ops[eng].append(("wait", sem, int(val)))

    def bar(self):
        return dict(self.cnt)

    def wait_bar(self, eng, bar):
        for s in SEMS:
            self.wait(eng, s, bar[s])


def build_nc(debug=False):
    nc = bass.Bass("TRN2", target_bir_lowering=False, debug=False)
    dt = nc.dram_tensor

    x_in = dt("x_in", [TOK, D], F32, kind="ExternalInput")
    xpwT = dt("xpwT", [D, 256], F32, kind="ExternalInput")
    dtwT = dt("dtwT", [RHO, D], F32, kind="ExternalInput")
    dtb_r = dt("dtb_r", [128, 16], F32, kind="ExternalInput")
    alog_r = dt("alog_r", [128, 512], F32, kind="ExternalInput")
    dpar_r = dt("dpar_r", [128, 16], F32, kind="ExternalInput")
    nw_r = dt("nw_r", [128, 16], F32, kind="ExternalInput")
    nb_r = dt("nb_r", [128, 16], F32, kind="ExternalInput")
    woutT = dt("woutT", [D, D], F32, kind="ExternalInput")
    ffnAT = dt("ffnAT", [D, ER], F32, kind="ExternalInput")
    ffnBT = dt("ffnBT", [ER, D], F32, kind="ExternalInput")
    wrT = dt("wrT", [D, 8], F32, kind="ExternalInput")
    br_in = dt("br_in", [1, 8], F32, kind="ExternalInput")
    blkid_in = dt("blkid_in", [8, ER], F32, kind="ExternalInput")
    idf_in = dt("idf_in", [128, 128], F32, kind="ExternalInput")
    idb_in = dt("idb_in", [128, 128], BF16, kind="ExternalInput")
    cst_in = dt("cst_in", [1, 4], F32, kind="ExternalInput")
    jv_in = dt("jv_in", [1, 1], I32, kind="ExternalInput")
    kv_in = dt("kv_in", [1, 1], I32, kind="ExternalInput")

    out_c = dt("out_c", [MYT, D], F32, kind="ExternalOutput")

    kd = "ExternalOutput" if debug else "Internal"
    x1_scr = dt("x1_scr", [MYT, D], F32, kind=kd)
    xlnbT_scr = dt("xlnbT_scr", [D, MYT], BF16, kind=kd)
    xln2T_scr = dt("xln2T_scr", [D, MYT], F32, kind=kd)
    PhA = dt("PhA", [2176, MYT], F32, kind=kd)
    PhB = dt("PhB", [2176, MYT], F32, kind=kd)
    PgxB = dt("PgxB", [4352, MYT], F32, kind="Internal")
    y_scr2 = dt("y_scr2", [2 * DS, MYT], BF16, kind="Internal")
    y_dbg = dt("y_dbg", [2 * DS, MYT], BF16, kind=kd) if debug else None
    ygx2 = dt("ygx2", [2 * D, MYT], BF16, kind="Internal")
    wo_b_scr = dt("wo_b_scr", [D, D], BF16, kind="Internal")
    fAT_b_scr = dt("fAT_b_scr", [D, ER], BF16, kind="Internal")
    fBT_b_scr = dt("fBT_b_scr", [ER, D], BF16, kind="Internal")
    xpwT_b_scr = dt("xpwT_b_scr", [D, 256], BF16, kind="Internal")
    dtwT_b_scr = dt("dtwT_b_scr", [RHO, D], BF16, kind="Internal")
    wrT_b_scr = dt("wrT_b_scr", [D, 8], BF16, kind="Internal")

    P = Prog()
    cell = {}

    with ExitStack() as st:
        sb = lambda name, shape, d=F32: st.enter_context(nc.sbuf_tensor(name, shape, d))
        s_jv = sb("s_jv", [1, 2], I32)
        idf = sb("idf", [128, 128], F32)
        idb = sb("idb", [128, 128], BF16)
        blkid = sb("blkid", [8, ER], F32)
        br_b = sb("br_b", [128, 8], F32)
        nw = sb("nw", [128, 16], F32)
        nb = sb("nb", [128, 16], F32)
        dtb = sb("dtb", [128, 16], F32)
        dpar = sb("dpar", [128, 16], F32)
        A_sb = sb("A_sb", [128, 512], F32)
        carry = sb("carry", [128, 512], F32)
        ST = sb("ST", [128, 192], F32)
        MT = sb("MT", [8, ER], F32)
        cst = sb("cst", [128, 4], F32)
        wr_sb = sb("wr_sb", [128, 128], BF16)
        dtw_sb = sb("dtw_sb", [128, D], BF16)
        AR1 = sb("AR1", [128, 16384], F32)
        AR2 = sb("AR2", [128, 16384], F32)
        AR3 = sb("AR3", [128, 8192], F32)
        SM0 = sb("SM0", [128, 2048], F32)
        SM1 = sb("SM1", [128, 2048], F32)
        SM2 = sb("SM2", [128, 2048], F32)
        SM3 = sb("SM3", [128, 2048], F32)
        AR1b = AR1[:, :].bitcast(BF16)
        AR2b = AR2[:, :].bitcast(BF16)
        AR3b = AR3[:, :].bitcast(BF16)
        SM0b = SM0[:, :].bitcast(BF16)
        SM2b = SM2[:, :].bitcast(BF16)
        SM3b = SM3[:, :].bitcast(BF16)
        PS = st.enter_context(nc.psum_tensor("PS", [128, 4096], F32))
        PSb = PS[:, :].bitcast(BF16)
        BANK = lambda k, w=512: PS[:, 512 * k:512 * k + w]
        sems = {s: st.enter_context(nc.semaphore(s)) for s in SEMS}
        block = st.enter_context(nc.Block())

        for dst, src in [
            (s_jv[0:1, 0:1], jv_in[:, :]), (s_jv[0:1, 1:2], kv_in[:, :]),
            (idf[:, :], idf_in[:, :]), (idb[:, :], idb_in[:, :]),
            (blkid[:, :], blkid_in[:, :]),
            (br_b[:, :], br_in[0:1, :].broadcast_to([128, 8])),
            (nw[:, :], nw_r[:, :]), (nb[:, :], nb_r[:, :]),
            (dtb[:, :], dtb_r[:, :]), (dpar[:, :], dpar_r[:, :]),
            (A_sb[:, :], alog_r[:, :]),
            (cst[:, :], cst_in[0:1, :].broadcast_to([128, 4])),
        ]:
            P.add("sync", (lambda d_, s_: (lambda q: q.dma_start(out=d_, in_=s_)))(dst, src), "dL", 16)
        v_glob = P.cnt["dL"]

        def _ld_regs(q):
            j = nc.values_load(s_jv[0:1, 0:1], engines=[q.engine], min_val=0,
                               max_val=1, skip_runtime_bounds_check=True)
            k = nc.values_load(s_jv[0:1, 1:2], engines=[q.engine], min_val=0,
                               max_val=1, skip_runtime_bounds_check=True)
            cell["jj"] = j * 1
            cell["kk2"] = k + 2
            cell["j8"] = j * 8
            cell["k8"] = k * 8
            return None
        P.wait("sync", "dL", v_glob)
        P.add("sync", _ld_regs)

        P.wait("scalar", "dL", v_glob)
        vA1 = P.add("scalar", lambda q: q.activation(out=A_sb[:, :], in_=A_sb[:, :], func=AF.Exp), "cS")
        P.wait("vector", "cS", vA1)
        P.add("vector", lambda q: q.tensor_scalar(out=A_sb[:, :], in0=A_sb[:, :], scalar1=-1.0,
                                                  scalar2=None, op0=ALU.mult), "cV")

        jobs = []
        for r in range(16):
            jobs.append((wrT[r * 128:(r + 1) * 128, :], wrT_b_scr[r * 128:(r + 1) * 128, :], 8, None))
        for r in range(16):
            jobs.append((ffnAT[r * 128:(r + 1) * 128, :], fAT_b_scr[r * 128:(r + 1) * 128, :], ER, None))
        for r in range(4):
            jobs.append((ffnBT[r * 128:(r + 1) * 128, :], fBT_b_scr[r * 128:(r + 1) * 128, :], D, None))
        for r in range(16):
            jobs.append((xpwT[r * 128:(r + 1) * 128, :], xpwT_b_scr[r * 128:(r + 1) * 128, :], 256, None))
        jobs.append((dtwT[:, :], dtwT_b_scr[:, :], D, None))
        for r in range(16):
            jobs.append((woutT[r * 128:(r + 1) * 128, :], wo_b_scr[r * 128:(r + 1) * 128, :], D, None))
        stg_in = [SM0, SM1]
        stg_out = [SM2b, SM3b]
        v_st_last = {0: 0, 1: 0}
        for i, (jsrc, dst, cols, wchunk) in enumerate(jobs):
            par = i % 2
            sld = "dRB%d" % par
            sst = "dSa" if par == 0 else "dSb"
            P.wait("sync", sst, v_st_last[par])
            vl = P.add("sync", (lambda s_, c_, p_: (lambda q: q.dma_start(out=stg_in[p_][:, 0:c_], in_=s_)))(jsrc, cols, par), sld, 16)
            P.wait("vector", sld, vl)
            P.wait("vector", sst, v_st_last[par])
            vc = P.add("vector", (lambda c_, p_: (lambda q: q.tensor_copy(out=stg_out[p_][:, 0:c_], in_=stg_in[p_][:, 0:c_])))(cols, par), "cV")
            P.wait("sync", "cV", vc)
            vs = P.add("sync", (lambda d_, c_, p_: (lambda q: q.dma_start(out=d_, in_=stg_out[p_][:, 0:c_])))(dst, cols, par), sst, 16)
            v_st_last[par] = vs
        bar0 = P.bar()

        fA = AR3b[:, 0:8192]
        fB = AR3b[:, 8192:16384]
        pst = {"tr_evac": 0, "router_c": 0, "mscT_c": 0, "maskmm": 0, "msk_ev": 0,
               "hm": 0, "y_ev": 0, "trf": 0, "mt_rd": 0, "v_xlb_2a": [0, 0]}

        def emit_ffn_weights_load():
            vv = 0
            for r in range(16):
                vv = P.add("sync", (lambda r_: (lambda q: q.dma_start(
                    out=fA[:, r_ * 512:(r_ + 1) * 512], in_=fAT_b_scr[r_ * 128:(r_ + 1) * 128, :])))(r), "dL", 16)
            for r in range(4):
                vv = P.add("sync", (lambda r_: (lambda q: q.dma_start(
                    out=fB[:, r_ * 2048:(r_ + 1) * 2048], in_=fBT_b_scr[r_ * 128:(r_ + 1) * 128, :])))(r), "dL", 16)
            vv = P.add("sync", lambda q: q.dma_start(
                out=wr_sb[:, :].rearrange("p (c e) -> p c e", c=16),
                in_=wrT_b_scr[:, :].rearrange("(c p) e -> p c e", p=128)), "dL", 16)
            return vv

        BG = {2: {"cV": 0, "cS": 0}, 3: {"cV": 0, "cS": 0}}

        def bank_wait(k):
            for sem, val in BG[k].items():
                P.wait("tensor", sem, val)

        def emit_ffn_sup(sup, x_tiles, out_dst, fstate, tile_dma=None, v_tiles_ready=0):
            xlnT = AR2b[:, 0:8192]
            msk = AR2[:, 4096:6144]
            hm = AR2b[:, 12288:14336]
            ys = [AR2[:, 7168:7680], AR2[:, 7680:8192]]
            xlb = [SM0b[:, 0:2048], SM0b[:, 2048:4096]]
            sq = SM2
            v10_par = pst.setdefault("v10_par", [0, 0])
            v15_par = pst.setdefault("v15_par", [0, 0])
            vxh = [0, 0]
            if tile_dma is not None:
                for hf in range(2):
                    for sem, val in fstate.get("tile_free_h", [[], []])[hf]:
                        P.wait("sync", sem, val)
                    P.wait("sync", "dS", fstate.get("st_tile", 0))
                    vxh[hf] = P.add("sync", (lambda s_, hf_: (lambda q: tile_dma(q, s_, hf_)))(sup, hf), "dX%d" % hf, 16)
            for tt in range(4):
                pp = tt % 2
                base = 64 * pp
                col = lambda k: ST[:, base + k:base + k + 1]
                vx = vxh[tt // 2] if tile_dma is not None else 0
                sx = "dX%d" % (tt // 2)
                xt = x_tiles[tt]
                P.wait("vector", "cS", v10_par[pp])
                P.wait("vector", "cP", v15_par[pp])
                if v_tiles_ready and tile_dma is None:
                    P.wait("vector", "cV", v_tiles_ready)
                if vx:
                    P.wait("vector", sx, vx)
                v1 = P.add("vector", (lambda xt_, c_: (lambda q: q.reduce_sum(c_, xt_, axis=AX.X)))(xt, col(0)), "cV")
                if vx:
                    P.wait("scalar", sx, vx)
                if v_tiles_ready and tile_dma is None:
                    P.wait("scalar", "cV", v_tiles_ready)
                    P.wait("tensor", "cV", v_tiles_ready)
                v2 = P.add("scalar", (lambda xt_, c_: (lambda q: q.activation(out=sq[:, :], in_=xt_, func=AF.Square,
                                                                              accum_out=c_)))(xt, col(1)), "cS")
                P.wait("vector", "cS", v2)
                P.wait("vector", "cV", v1)
                vv = P.add("vector", (lambda a, b: (lambda q: q.tensor_scalar(out=a, in0=b, scalar1=1.0 / D, scalar2=None, op0=ALU.mult)))(col(2), col(0)), "cV")
                vv = P.add("vector", (lambda a, b: (lambda q: q.tensor_scalar(out=a, in0=b, scalar1=1.0 / D, scalar2=None, op0=ALU.mult)))(col(3), col(1)), "cV")
                P.wait("vector", "cV", vv)
                vv = P.add("vector", (lambda a, b: (lambda q: q.tensor_tensor(out=a, in0=b, in1=b, op=ALU.mult)))(col(4), col(2)), "cV")
                P.wait("vector", "cV", vv)
                v3 = P.add("vector", (lambda a, b, c: (lambda q: q.tensor_tensor(out=a, in0=b, in1=c, op=ALU.subtract)))(col(5), col(3), col(4)), "cV")
                P.wait("scalar", "cV", v3)
                vl = P.add("scalar", (lambda a, b: (lambda q: q.activation(out=a, in_=b, func=AF.Ln, bias=cst[:, 0:1])))(col(6), col(5)), "cS")
                P.wait("scalar", "cS", vl)
                v4 = P.add("scalar", (lambda a, b: (lambda q: q.activation(out=a, in_=b, func=AF.Exp, scale=cst[:, 1:2])))(col(7), col(6)), "cS")
                P.wait("vector", "cS", v4)
                v5 = P.add("vector", (lambda a, b, c: (lambda q: q.tensor_scalar(out=a, in0=b, scalar1=c,
                                                                                 scalar2=-1.0, op0=ALU.mult, op1=ALU.mult)))(col(8), col(2), col(7)), "cV")
                P.wait("scalar", "cV", v5)
                P.wait("scalar", "cP", pst["v_xlb"][pp])
                v6 = P.add("scalar", (lambda xt_, p_, c7, c8: (lambda q: q.activation(out=xlb[p_][:, :], in_=xt_, func=AF.Identity,
                                                                                      scale=c7, bias=c8)))(xt, pp, col(7), col(8)), "cS")
                for g in range(4):
                    P.wait("tensor", "cS", v6)
                    P.wait("tensor", "cV", pst["tr_evac"])
                    for c4 in range(4):
                        c = g * 4 + c4
                        P.add("tensor", (lambda c_, c4_, p_, g_: (lambda q: q.transpose(
                            PSb[:, (g_ % 2) * 1024 + c4_ * 128:(g_ % 2) * 1024 + (c4_ + 1) * 128],
                            xlb[p_][:, c_ * 128:(c_ + 1) * 128], idb[:, :])))(c, c4, pp, g), "cP")
                    vp = P.cnt["cP"]
                    P.wait("vector", "cP", vp)
                    pst["tr_evac"] = P.add("vector", (lambda g_, tt_: (lambda q: q.tensor_copy(
                        out=xlnT[:, :].rearrange("p (c t) -> p c t", c=16)[:, g_ * 4:(g_ + 1) * 4, tt_ * 128:(tt_ + 1) * 128],
                        in_=PSb[:, (g_ % 2) * 1024:(g_ % 2) * 1024 + 512].rearrange("p (c t) -> p c t", c=4))))(g, tt), "cV")
                pst["v_xlb"][pp] = P.cnt["cP"]
                P.wait("tensor", "cV", pst["tr_evac"])
                bank_wait(2)
                for c in range(16):
                    P.add("tensor", (lambda c_, tt_: (lambda q: q.matmul(
                        PS[:, 1024:1032], xlnT[:, c_ * 512 + tt_ * 128: c_ * 512 + (tt_ + 1) * 128],
                        wr_sb[:, c_ * 8:(c_ + 1) * 8], start=(c_ == 0), stop=(c_ == 15))))(c, tt), "cP")
                vr = P.cnt["cP"]
                P.wait("vector", "cP", vr)
                vrc = P.add("vector", (lambda c16: (lambda q: q.tensor_tensor(out=ST[:, c16:c16 + 8], in0=br_b[:, :],
                                                                              in1=PS[:, 1024:1032], op=ALU.add)))(base + 16), "cV")
                BG[2]["cV"] = vrc
                P.wait("vector", "cV", vrc)
                v8 = P.add("vector", (lambda a, c16: (lambda q: q.reduce_max(a, ST[:, c16:c16 + 8], axis=AX.X)))(col(9), base + 16), "cV")
                P.wait("vector", "cV", v8)
                v9 = P.add("vector", (lambda a, b: (lambda q: q.tensor_scalar(out=a, in0=b, scalar1=-1.0, scalar2=None, op0=ALU.mult)))(col(10), col(9)), "cV")
                P.wait("scalar", "cV", v9)
                v10 = P.add("scalar", (lambda c16, c24, c10: (lambda q: q.activation(out=ST[:, c24:c24 + 8], in_=ST[:, c16:c16 + 8],
                                                                                     func=AF.Exp, bias=c10)))(base + 16, base + 24, col(10)), "cS")
                v10_par[pp] = v10
                P.wait("vector", "cS", v10)
                v11 = P.add("vector", (lambda a, c24: (lambda q: q.reduce_sum(a, ST[:, c24:c24 + 8], axis=AX.X)))(col(11), base + 24), "cV")
                P.wait("vector", "cV", v11)
                vr2 = P.add("vector", (lambda a, b: (lambda q: q.reciprocal(a, b)))(col(12), col(11)), "cV")
                P.wait("vector", "cV", vr2)
                v12 = P.add("vector", (lambda a, b: (lambda q: q.tensor_scalar(out=a, in0=b, scalar1=SCL, scalar2=None, op0=ALU.mult)))(col(13), col(12)), "cV")
                P.wait("vector", "cV", v12)
                v13 = P.add("vector", (lambda c16, c32, c9: (lambda q: q.tensor_scalar(out=ST[:, c32:c32 + 8], in0=ST[:, c16:c16 + 8],
                                                                                       scalar1=c9, scalar2=None, op0=ALU.is_equal)))(base + 16, base + 32, col(9)), "cV")
                P.wait("vector", "cV", v13)
                v14 = P.add("vector", (lambda c32, c40, c13: (lambda q: q.tensor_scalar(out=ST[:, c40:c40 + 8], in0=ST[:, c32:c32 + 8],
                                                                                        scalar1=c13, scalar2=None, op0=ALU.mult)))(base + 32, base + 40, col(13)), "cV")
                P.wait("tensor", "cV", v14)
                P.wait("tensor", "cS", pst["mscT_c"])
                v15 = P.add("tensor", (lambda c40: (lambda q: q.transpose(PS[0:8, 3584:3712], ST[:, c40:c40 + 8], idf[:, :])))(base + 40), "cP")
                v15_par[pp] = v15
                P.wait("scalar", "cP", v15)
                P.wait("scalar", "cP", pst["mt_rd"])
                pst["mscT_c"] = P.add("scalar", (lambda tt_: (lambda q: q.copy(out=MT[0:8, tt_ * 128:(tt_ + 1) * 128],
                                                                               in_=PS[0:8, 3584:3712])))(tt), "cS")
            for c4 in range(4):
                P.wait("tensor", "cS", pst["mscT_c"])
                bank_wait(3)
                P.add("tensor", (lambda c4_: (lambda q: q.matmul(BANK(3), blkid[0:8, c4_ * 128:(c4_ + 1) * 128],
                                                                 MT[0:8, :], start=True, stop=True)))(c4), "cP")
                vp = P.cnt["cP"]
                P.wait("scalar", "cP", vp)
                P.wait("scalar", "cV", pst["hm"])
                pst["msk_ev"] = P.add("scalar", (lambda c4_: (lambda q: q.copy(out=msk[:, c4_ * 512:(c4_ + 1) * 512],
                                                                               in_=BANK(3))))(c4), "cS")
                BG[3]["cS"] = pst["msk_ev"]
            pst["mt_rd"] = P.cnt["cP"]
            v_hm_bank = pst.setdefault("v_hm_bank", [0, 0])
            for ec in range(4):
                bk = 4 if ec % 2 == 0 else 2
                P.wait("tensor", "cS", pst["msk_ev"])
                if bk == 2:
                    bank_wait(2)
                else:
                    P.wait("tensor", "cV", v_hm_bank[0])
                for c in range(16):
                    P.add("tensor", (lambda c_, ec_, bk_: (lambda q: q.matmul(
                        BANK(bk_), fA[:, c_ * 512 + ec_ * 128: c_ * 512 + (ec_ + 1) * 128],
                        xlnT[:, c_ * 512:(c_ + 1) * 512], start=(c_ == 0), stop=(c_ == 15))))(c, ec, bk), "cP")
                vp = P.cnt["cP"]
                P.wait("vector", "cP", vp)
                v_hm_bank[ec % 2] = P.add("vector", (lambda ec_, bk_: (lambda q: q.tensor_tensor(
                    out=hm[:, ec_ * 512:(ec_ + 1) * 512], in0=BANK(bk_), in1=msk[:, ec_ * 512:(ec_ + 1) * 512],
                    op=ALU.mult)))(ec, bk), "cV")
                if bk == 2:
                    BG[2]["cV"] = v_hm_bank[1]
            pst["hm"] = v_hm_bank[1]
            v_yev = {}
            v_tr_done = {}
            for oc in range(17):
                if oc < 16:
                    bk = 5 if oc % 2 == 0 else 3
                    P.wait("tensor", "cV", pst["hm"])
                    if bk == 3:
                        bank_wait(3)
                    else:
                        P.wait("tensor", "cS", v_yev.get(oc - 2, 0))
                    for ec in range(4):
                        P.add("tensor", (lambda oc_, ec_, bk_: (lambda q: q.matmul(
                            BANK(bk_), fB[:, ec_ * 2048 + oc_ * 128: ec_ * 2048 + (oc_ + 1) * 128],
                            hm[:, ec_ * 512:(ec_ + 1) * 512], start=(ec_ == 0), stop=(ec_ == 3))))(oc, ec, bk), "cP")
                    vmm = P.cnt["cP"]
                    P.wait("scalar", "cP", vmm)
                    v_yev[oc] = P.add("scalar", (lambda bk_, oc_: (lambda q: q.copy(out=ys[oc_ % 2][:, :], in_=BANK(bk_))))(bk, oc), "cS")
                    if bk == 3:
                        BG[3]["cS"] = v_yev[oc]
                ot = oc - 1
                if 0 <= ot < 16:
                    P.wait("tensor", "cS", v_yev[ot])
                    P.wait("tensor", "cV", pst["trf"])
                    for tt in range(4):
                        P.add("tensor", (lambda tt_, ot_: (lambda q: q.transpose(
                            BANK(6)[:, tt_ * 128:(tt_ + 1) * 128], ys[ot_ % 2][:, tt_ * 128:(tt_ + 1) * 128], idf[:, :])))(tt, ot), "cP")
                    v_tr_done[ot] = P.cnt["cP"]
                    P.wait("vector", "cP", v_tr_done[ot])
                    for tt in range(4):
                        pst["trf"] = P.add("vector", (lambda tt_, oc_: (lambda q: q.tensor_tensor(
                            out=x_tiles[tt_][:, oc_ * 128:(oc_ + 1) * 128], in0=x_tiles[tt_][:, oc_ * 128:(oc_ + 1) * 128],
                            in1=BANK(6)[:, tt_ * 128:(tt_ + 1) * 128], op=ALU.add)))(tt, ot), "cV")
            if out_dst is not None:
                P.wait("sync", "cV", pst["trf"])
                fstate["st_tile"] = P.add("sync", (lambda d_: (lambda q: q.dma_start(
                    out=d_, in_=AR1[:, 0:8192].rearrange("p (a d) -> p a d", a=4))))(out_dst), "dS", 16)
            return pst["trf"]

        P.wait_bar("sync", bar0)
        vw1 = emit_ffn_weights_load()
        for e in ("scalar", "vector", "tensor"):
            P.wait_bar(e, bar0)
            P.wait(e, "dL", vw1)
        x_tiles = [AR1[:, k * 2048:(k + 1) * 2048] for k in range(4)]
        pst["v_xlb"] = [0, 0]

        def x_tile_dma(q, sup, hf):
            return q.dma_start(
                out=AR1[:, hf * 4096:(hf + 1) * 4096].rearrange("p (j a d) -> p j a d", j=1, a=2),
                in_=x_in[:, :].rearrange("(j a p) d -> p j a d", j=2, p=128)[
                    :, bass.DynSlice(cell["jj"], 1), sup * 4 + hf * 2:sup * 4 + hf * 2 + 2, :])

        xf = SM1
        xf2 = AR2[:, 8192:10240]
        xb = SM3b[:, 0:2048]
        sqd = AR2[:, 10240:12288]
        xlnbT = AR2b[:, 24576:32768]
        xln2T_sb = AR1[:, 8192:16384]
        v13_par = [0, 0]
        v_x1_rd = [0] * 4
        v_xf_rd = 0
        v_evb = 0
        v_evf = 0
        v_2a_st = 0
        fstate = {"tile_free": [], "st_tile": 0}

        for sup in range(4):
            vtrf = emit_ffn_sup(sup, x_tiles,
                         x1_scr[:, :].rearrange("(a p) d -> p a d", p=128)[:, sup * 4:sup * 4 + 4, :],
                         fstate, tile_dma=x_tile_dma)
            P.wait("scalar", "cV", vtrf)
            P.wait("vector", "cV", vtrf)
            P.wait("vector", "dSy0", v_2a_st)
            P.wait("scalar", "dSy0", v_2a_st)
            for tt in range(4):
                ti = sup * 4 + tt
                pp = tt % 2
                base = 128 + 32 * pp
                col = lambda k: ST[:, base + k:base + k + 1]
                x1t = x_tiles[tt]
                P.wait("vector", "cS", v13_par[pp])
                v1a = P.add("vector", (lambda x_, c_: (lambda q: q.reduce_sum(c_, x_, axis=AX.X)))(x1t, col(0)), "cV")
                v2 = P.add("scalar", (lambda x_, c_: (lambda q: q.activation(out=sqd[:, :], in_=x_, func=AF.Square, accum_out=c_)))(x1t, col(1)), "cS")
                P.wait("vector", "cS", v2)
                P.wait("vector", "cV", v1a)
                vv = P.add("vector", (lambda a, b: (lambda q: q.tensor_scalar(out=a, in0=b, scalar1=1.0 / D, scalar2=None, op0=ALU.mult)))(col(2), col(0)), "cV")
                vv = P.add("vector", (lambda a, b: (lambda q: q.tensor_scalar(out=a, in0=b, scalar1=1.0 / D, scalar2=None, op0=ALU.mult)))(col(3), col(1)), "cV")
                P.wait("vector", "cV", vv)
                vv = P.add("vector", (lambda a, b: (lambda q: q.tensor_tensor(out=a, in0=b, in1=b, op=ALU.mult)))(col(4), col(2)), "cV")
                P.wait("vector", "cV", vv)
                v3 = P.add("vector", (lambda a, b, c: (lambda q: q.tensor_tensor(out=a, in0=b, in1=c, op=ALU.subtract)))(col(5), col(3), col(4)), "cV")
                P.wait("scalar", "cV", v3)
                vl = P.add("scalar", (lambda a, b: (lambda q: q.activation(out=a, in_=b, func=AF.Ln, bias=cst[:, 0:1])))(col(6), col(5)), "cS")
                P.wait("scalar", "cS", vl)
                v4 = P.add("scalar", (lambda a, b: (lambda q: q.activation(out=a, in_=b, func=AF.Exp, scale=cst[:, 1:2])))(col(7), col(6)), "cS")
                P.wait("vector", "cS", v4)
                v5 = P.add("vector", (lambda a, b, c: (lambda q: q.tensor_scalar(out=a, in0=b, scalar1=c, scalar2=-1.0,
                                                                                 op0=ALU.mult, op1=ALU.mult)))(col(8), col(2), col(7)), "cV")
                P.wait("scalar", "cV", v5)
                P.wait("scalar", "cS", v_xf_rd)
                v6 = P.add("scalar", (lambda x_, c7, c8: (lambda q: q.activation(out=xf[:, :], in_=x_, func=AF.Identity,
                                                                                 scale=c7, bias=c8)))(x1t, col(7), col(8)), "cS")
                v_x1_rd[tt] = v6
                P.wait("vector", "cS", v6)
                v7a = P.add("vector", (lambda c_: (lambda q: q.reduce_sum(c_, xf[:, :], axis=AX.X)))(col(9)), "cV")
                P.wait("scalar", "cS", v6)
                v8 = P.add("scalar", (lambda c_: (lambda q: q.activation(out=sqd[:, :], in_=xf[:, :], func=AF.Square, accum_out=c_)))(col(10)), "cS")
                P.wait("vector", "cS", v8)
                P.wait("vector", "cV", v7a)
                vv = P.add("vector", (lambda a, b: (lambda q: q.tensor_scalar(out=a, in0=b, scalar1=1.0 / D, scalar2=None, op0=ALU.mult)))(col(11), col(9)), "cV")
                vv = P.add("vector", (lambda a, b: (lambda q: q.tensor_scalar(out=a, in0=b, scalar1=1.0 / D, scalar2=None, op0=ALU.mult)))(col(12), col(10)), "cV")
                P.wait("vector", "cV", vv)
                vv = P.add("vector", (lambda a, b: (lambda q: q.tensor_tensor(out=a, in0=b, in1=b, op=ALU.mult)))(col(13), col(11)), "cV")
                P.wait("vector", "cV", vv)
                v9 = P.add("vector", (lambda a, b, c: (lambda q: q.tensor_tensor(out=a, in0=b, in1=c, op=ALU.subtract)))(col(14), col(12), col(13)), "cV")
                P.wait("scalar", "cV", v9)
                vl = P.add("scalar", (lambda a, b: (lambda q: q.activation(out=a, in_=b, func=AF.Ln, bias=cst[:, 0:1])))(col(15), col(14)), "cS")
                P.wait("scalar", "cS", vl)
                v10 = P.add("scalar", (lambda a, b: (lambda q: q.activation(out=a, in_=b, func=AF.Exp, scale=cst[:, 1:2])))(col(16), col(15)), "cS")
                P.wait("vector", "cS", v10)
                v11 = P.add("vector", (lambda a, b, c: (lambda q: q.tensor_scalar(out=a, in0=b, scalar1=c, scalar2=-1.0,
                                                                                  op0=ALU.mult, op1=ALU.mult)))(col(17), col(11), col(16)), "cV")
                P.wait("scalar", "cV", v11)
                P.wait("scalar", "cP", pst["v_xlb_2a"][0])
                v12 = P.add("scalar", lambda q: q.activation(out=xb[:, :], in_=xf[:, :], func=AF.Identity), "cS")
                P.wait("scalar", "cP", pst["v_xlb_2a"][1])
                v13 = P.add("scalar", (lambda c16, c17: (lambda q: q.activation(out=xf2[:, :], in_=xf[:, :], func=AF.Identity,
                                                                                scale=c16, bias=c17)))(col(16), col(17)), "cS")
                v13_par[pp] = v13
                v_xf_rd = v13
                for g in range(4):
                    P.wait("tensor", "cS", v12)
                    P.wait("tensor", "cV", pst["tr_evac"])
                    for c4 in range(4):
                        c = g * 4 + c4
                        P.add("tensor", (lambda c_, c4_, g_: (lambda q: q.transpose(
                            PSb[:, (g_ % 2) * 1024 + c4_ * 128:(g_ % 2) * 1024 + (c4_ + 1) * 128],
                            xb[:, c_ * 128:(c_ + 1) * 128], idb[:, :])))(c, c4, g), "cP")
                    vp = P.cnt["cP"]
                    P.wait("vector", "cP", vp)
                    pst["tr_evac"] = P.add("vector", (lambda g_, tt_: (lambda q: q.tensor_copy(
                        out=xlnbT[:, :].rearrange("p (c t) -> p c t", c=16)[:, g_ * 4:(g_ + 1) * 4, tt_ * 128:(tt_ + 1) * 128],
                        in_=PSb[:, (g_ % 2) * 1024:(g_ % 2) * 1024 + 512].rearrange("p (c t) -> p c t", c=4))))(g, tt), "cV")
                pst["v_xlb_2a"][0] = P.cnt["cP"]
                for g in range(8):
                    P.wait("tensor", "cS", v13)
                    bk = 2 + g % 2
                    bank_wait(bk)
                    for c2 in range(2):
                        c = g * 2 + c2
                        P.add("tensor", (lambda c_, c2_, bk_: (lambda q: q.transpose(
                            BANK(bk_)[:, c2_ * 128:(c2_ + 1) * 128],
                            xf2[:, c_ * 128:(c_ + 1) * 128], idf[:, :])))(c, c2, bk), "cP")
                    vp = P.cnt["cP"]
                    P.wait("scalar", "cP", vp)
                    vev = P.add("scalar", (lambda g_, tt_, bk_: (lambda q: q.copy(
                        out=xln2T_sb[:, :].rearrange("p (c t) -> p c t", c=16)[:, g_ * 2:(g_ + 1) * 2, tt_ * 128:(tt_ + 1) * 128],
                        in_=BANK(bk_, 256).rearrange("p (c t) -> p c t", c=2))))(g, tt, bk), "cS")
                    BG[bk]["cS"] = vev
                pst["v_xlb_2a"][1] = P.cnt["cP"]
            P.wait("sync", "cV", pst["tr_evac"])
            P.wait("sync", "cS", BG[2]["cS"])
            P.wait("sync", "cS", BG[3]["cS"])
            P.wait("sync", "dSy0", v_2a_st)
            P.add("sync", (lambda s_: (lambda q: q.dma_start(
                out=xlnbT_scr[:, :].rearrange("(c p) t -> p c t", p=128)[:, :, s_ * 512:(s_ + 1) * 512],
                in_=xlnbT[:, :].rearrange("p (c t) -> p c t", c=16))))(sup), "dSy0", 16)
            v_2a_st = P.add("sync", (lambda s_: (lambda q: q.dma_start(
                out=xln2T_scr[:, :].rearrange("(c p) t -> p c t", p=128)[:, :, s_ * 512:(s_ + 1) * 512],
                in_=xln2T_sb[:, :].rearrange("p (c t) -> p c t", c=16))))(sup), "dSy0", 16)
            fstate["tile_free_h"] = [
                [("cV", pst["trf"]), ("cS", max(v_x1_rd[0], v_x1_rd[1]))],
                [("cV", pst["trf"]), ("cS", max(v_x1_rd[2], v_x1_rd[3]))],
            ]
        P.wait("sync", "dSy0", P.cnt["dSy0"])
        bar2a = P.bar()

        for e in ENGS[:4]:
            P.wait_bar(e, bar2a)
        xnb = AR1b
        chunkbuf = AR3[:, 2048:4096]
        xpw_sb = AR3b[:, 12288:16384]
        projd = AR3b[:, 0:2048]
        proj_sb = AR2[:, 0:4096]
        v_pw = P.add("sync", lambda q: q.dma_start(
            out=xpw_sb[:, :].rearrange("p (c e) -> p c e", c=16),
            in_=xpwT_b_scr[:, :].rearrange("(c p) e -> p c e", p=128)), "dR", 16)
        v_dwl = P.add("sync", lambda q: q.dma_start(out=dtw_sb[:, :], in_=dtwT_b_scr[:, :]), "dR", 16)
        v_aff_last = 0
        v_mm_half = {0: 0, 1: 0}
        v_ev_last = 0
        for half in range(2):
            for i in range(8):
                c = half * 8 + i
                vx = P.add("sync", (lambda c_: (lambda q: q.dma_start(
                    out=chunkbuf[:, :], in_=xln2T_scr[c_ * 128:(c_ + 1) * 128, :])))(c), "dL", 16)
                P.wait("scalar", "dL", vx)
                if half == 1:
                    P.wait("scalar", "cP", v_mm_half[0])
                v_aff_last = P.add("scalar", (lambda c_, i_: (lambda q: q.activation(
                    out=xnb[:, i_ * 2048:(i_ + 1) * 2048], in_=chunkbuf[:, :], func=AF.Identity,
                    scale=nw[:, c_:c_ + 1], bias=nb[:, c_:c_ + 1])))(c, i), "cS")
                P.wait("sync", "cS", v_aff_last)
            for po in range(2):
                for tp2 in range(4):
                    P.wait("tensor", "cS", v_aff_last)
                    P.wait("tensor", "dR", max(v_pw, v_dwl))
                    P.wait("tensor", "cV", v_ev_last)
                    for i in range(8):
                        P.add("tensor", (lambda po_, tp_, i_, h_: (lambda q: q.matmul(
                            PS[:, 0:512],
                            xpw_sb[:, (h_ * 8 + i_) * 256 + po_ * 128:(h_ * 8 + i_) * 256 + (po_ + 1) * 128],
                            xnb[:, i_ * 2048 + tp_ * 512:i_ * 2048 + (tp_ + 1) * 512],
                            start=(i_ == 0), stop=(i_ == 7))))(po, tp2, i, half), "cP")
                    vp = P.cnt["cP"]
                    P.wait("vector", "cP", vp)
                    if half == 0:
                        v_ev_last = P.add("vector", (lambda po_, tp_: (lambda q: q.tensor_copy(
                            out=proj_sb[:, po_ * 2048 + tp_ * 512: po_ * 2048 + (tp_ + 1) * 512],
                            in_=PS[:, 0:512])))(po, tp2), "cV")
                    else:
                        v_ev_last = P.add("vector", (lambda po_, tp_: (lambda q: q.tensor_tensor(
                            out=proj_sb[:, po_ * 2048 + tp_ * 512: po_ * 2048 + (tp_ + 1) * 512],
                            in0=proj_sb[:, po_ * 2048 + tp_ * 512: po_ * 2048 + (tp_ + 1) * 512],
                            in1=PS[:, 0:512], op=ALU.add)))(po, tp2), "cV")
            v_mm_half[half] = P.cnt["cP"]
        P.wait("vector", "cV", v_ev_last)
        v_pd = P.add("vector", lambda q: q.tensor_copy(out=projd[:, :], in_=proj_sb[:, 0:2048]), "cV")
        P.wait("sync", "cV", v_ev_last)
        v_bc = P.add("sync", lambda q: q.dma_start(out=PhA[2048:2176, :], in_=proj_sb[:, 2048:4096]), "dS", 16)
        v_bcB = P.add("sync", lambda q: q.dma_start(out=PhB[2048:2176, :], in_=proj_sb[:, 2048:4096]), "dS", 16)
        xnf = AR1[:, 0:2048]
        t1p = [AR1[:, 4096:5120], AR1[:, 6144:7168]]
        t2p = [AR1[:, 5120:6144], AR1[:, 7168:8192]]
        dpc = [SM2[:, 0:1024], SM2[:, 1024:2048]]
        wpc = [SM3[:, 0:1024], SM3[:, 1024:2048]]
        v_dst = {0: 0, 1: 0}
        v_sp = {0: 0, 1: 0}
        v_vd_par = {0: 0, 1: 0}
        v_xnf_rd = 0
        v_cc2 = 0
        for di, dblk in enumerate(list(range(8, 16)) + list(range(8))):
            if di == 8:
                P.wait("gpsimd", "dSa", P.cnt["dSa"])
                P.wait("gpsimd", "dSb", P.cnt["dSb"])
                P.wait("gpsimd", "dS", v_bcB)
                v_cc2 = P.add("gpsimd", lambda q: q.collective_compute(
                    "AllGather", ALU.bypass, replica_groups=[[0, 1], [2, 3], [4, 5], [6, 7]],
                    ins=[PhB.ap().opt()], outs=[PgxB.ap().opt()]), "ccm")
            P.wait("sync", "cV", v_xnf_rd)
            vx = P.add("sync", (lambda d_: (lambda q: q.dma_start(
                out=chunkbuf[:, :], in_=xln2T_scr[d_ * 128:(d_ + 1) * 128, :])))(dblk), "dL", 16)
            P.wait("scalar", "dL", vx)
            P.wait("scalar", "cP", P.cnt["cP"] if di == 0 else 0)
            v_xnf = P.add("scalar", (lambda d_: (lambda q: q.activation(
                out=xnf[:, :], in_=chunkbuf[:, :], func=AF.Identity,
                scale=nw[:, d_:d_ + 1], bias=nb[:, d_:d_ + 1])))(dblk), "cS")
            P.wait("sync", "cS", v_xnf)
            for tp in range(2):
                par = tp % 2
                pslice = PS[:, 1024 + par * 1024: 2048 + par * 1024]
                P.wait("tensor", "cV", v_pd)
                P.wait("tensor", "cS", v_sp[par])
                for hh in range(2):
                    P.add("tensor", (lambda d_, tp_, ps_, hh_: (lambda q: q.matmul(
                        ps_[:, hh_ * 512:(hh_ + 1) * 512], dtw_sb[:, d_ * 128:(d_ + 1) * 128],
                        projd[:, tp_ * 1024 + hh_ * 512:tp_ * 1024 + (hh_ + 1) * 512],
                        start=True, stop=True)))(dblk, tp, pslice, hh), "cP")
                vp = P.cnt["cP"]
                P.wait("scalar", "cP", vp)
                P.wait("scalar", "cV", v_vd_par[par])
                P.wait("scalar", "dS%s" % ("a" if par == 0 else "b"), v_dst[par])
                va = P.add("scalar", (lambda d_, ps_, pr_: (lambda q: q.activation(out=t1p[pr_][:, :], in_=ps_, func=AF.Abs,
                                                                         bias=dtb[:, d_:d_ + 1])))(dblk, pslice, par), "cS")
                P.wait("scalar", "cS", va)
                va = P.add("scalar", (lambda pr_: (lambda q: q.activation(out=t2p[pr_][:, :], in_=t1p[pr_][:, :], func=AF.Exp, scale=cst[:, 2:3])))(par), "cS")
                P.wait("scalar", "cS", va)
                va = P.add("scalar", (lambda pr_: (lambda q: q.activation(out=t1p[pr_][:, :], in_=t2p[pr_][:, :], func=AF.Ln, bias=cst[:, 3:4])))(par), "cS")
                P.wait("scalar", "cS", va)
                v_sp[par] = P.add("scalar", (lambda d_, ps_, par_: (lambda q: q.activation(
                    out=dpc[par_][:, :], in_=ps_, func=AF.Relu, bias=dtb[:, d_:d_ + 1])))(dblk, pslice, par), "cS")
                P.wait("vector", "cS", v_sp[par])
                v_vd_par[par] = P.add("vector", (lambda par_: (lambda q: q.tensor_tensor(
                    out=dpc[par_][:, :], in0=dpc[par_][:, :], in1=t1p[par_][:, :], op=ALU.add)))(par), "cV")
                P.wait("vector", "dS%s" % ("a" if par == 0 else "b"), v_dst[par])
                P.wait("vector", "cV", v_vd_par[par])
                vw = P.add("vector", (lambda par_, tp_: (lambda q: q.tensor_tensor(
                    out=wpc[par_][:, :], in0=dpc[par_][:, :], in1=xnf[:, tp_ * 1024:(tp_ + 1) * 1024],
                    op=ALU.mult)))(par, tp), "cV")
                if tp == 1:
                    v_xnf_rd = vw
                P.wait("sync", "cV", vw)
                spar = "dSa" if par == 0 else "dSb"
                tgt = PhB if dblk >= 8 else PhA
                gb = (dblk % 8) * 128
                P.add("sync", (lambda t_, g_, tp_, par_: (lambda q: q.dma_start(
                    out=t_[g_:g_ + 128, tp_ * 1024:(tp_ + 1) * 1024],
                    in_=dpc[par_][:, :])))(tgt, gb, tp, par), spar, 16)
                v_dst[par] = P.add("sync", (lambda t_, g_, tp_, par_: (lambda q: q.dma_start(
                    out=t_[1024 + g_:1024 + g_ + 128, tp_ * 1024:(tp_ + 1) * 1024],
                    in_=wpc[par_][:, :])))(tgt, gb, tp, par), spar, 16)
        bar2b = P.bar()

        for e in ENGS:
            P.wait_bar(e, bar2b)
        delta_sb = AR1[:, 0:8192]
        w_sb = AR1[:, 8192:16384]
        y_acc = AR2[:, 0:8192]
        dA2 = [AR3[:, 0:2048], AR3[:, 2048:4096]]
        u2 = PS[:, 0:2048]
        tmp2 = [AR3[:, 6144:8192], AR2[:, 12288:14336]]
        h2 = [PS[:, 2048:4096], AR2[:, 10240:12288]]
        y_bf = [AR2[:, 14336:15360].bitcast(BF16), AR2[:, 15360:16384].bitcast(BF16)]
        bufB2 = [SM0[:, 0:2048], SM1[:, 0:2048]]
        bufC2 = [SM2[:, 0:2048], SM3[:, 0:2048]]
        v_dA = {}
        v_u2 = {}
        v_tts = {}
        v_tts2 = {}
        v_ym2 = {}
        v_acm2 = {}
        v_uB_par = {0: 0, 1: 0}
        v_ymC_par = {0: 0, 1: 0}
        v_store_y = {0: 0, 1: 0}
        v_cast_prev_cc = 0
        v_ext_last = 0
        v_ext_slot = {0: 0, 1: 0}
        v_pair_end = 0
        tmp_reader = [None, None]
        NPAIR = NST * 4
        q = 0
        P.wait("sync", "ccm", v_cc2)
        v_pbc = P.add("sync", lambda q_: q_.dma_start(
            out=PgxB[:, :].rearrange("(b r) t -> b r t", b=2)[bass.DynSlice(cell["jj"], 1), :, :],
            in_=PhA[:, :].rearrange("(b r) t -> b r t", b=1)), "dR", 16)
        P.wait("sync", "dR", v_pbc)
        for cc in range(NCC):
            jb, tloc = cc // 2, cc % 2
            if cc > 0:
                P.wait("sync", "cS", v_dA[2 * q - 1])
                P.wait("sync", "cV", v_u2[q - 1])
            vL1 = P.add("sync", (lambda jb_, tl_: (lambda q_: q_.dma_start(
                out=delta_sb[:, :].rearrange("p (g d t) -> p g d t", g=1, d=8),
                in_=PgxB[:, :].rearrange("(g d p) t -> p g d t", g=2, d=17)[
                    :, jb_:jb_ + 1, 0:8, tl_ * TC:(tl_ + 1) * TC])))(jb, tloc), "dL", 16)
            vL2 = P.add("sync", (lambda jb_, tl_: (lambda q_: q_.dma_start(
                out=w_sb[:, :].rearrange("p (g d t) -> p g d t", g=1, d=8),
                in_=PgxB[:, :].rearrange("(g d p) t -> p g d t", g=2, d=17)[
                    :, jb_:jb_ + 1, 8:16, tl_ * TC:(tl_ + 1) * TC])))(jb, tloc), "dL", 16)
            P.wait("vector", "cS", v_cast_prev_cc)
            P.wait("vector", "cS", v_ext_last)
            for n in range(NST):
                par_n = n % 2
                sB = "dRB%d" % par_n
                sC = "dRC%d" % par_n
                P.wait("sync", "cV", v_uB_par[par_n])
                P.wait("sync", "cV", v_ymC_par[par_n])
                for rr in range(2):
                    rB = P.add("sync", (lambda n_, p_, jb_, tl_, rr_: (lambda q_: q_.dma_start(
                        out=bufB2[p_][:, rr_ * TC:(rr_ + 1) * TC],
                        in_=PgxB[jb_ * 2176 + 2048 + n_:jb_ * 2176 + 2049 + n_,
                                tl_ * TC:(tl_ + 1) * TC].broadcast_to([128, TC]))))(
                        n, par_n, jb, tloc, rr), sB, 16)
                for rr in range(4):
                    rC = P.add("sync", (lambda n_, p_, rr_: (lambda q_: q_.dma_start(
                        out=bufC2[p_][:, rr_ * 512:(rr_ + 1) * 512],
                        in_=PgxB[2112 + n_:2113 + n_, 0:512].broadcast_to([128, 512]))))(n, par_n, rr), sC, 16)
                for dp in range(4):
                    qs = q % 2
                    if q >= 2:
                        P.wait("scalar", "cV", v_tts2[q - 2])
                    if n == 0 and dp == 0:
                        P.wait("scalar", "dL", vL2)
                    for hh in range(2):
                        dblk = 2 * dp + hh
                        v_dA[2 * q + hh] = P.add("scalar", (lambda d_, n_, s_, hh_: (lambda q_: q_.activation(
                            out=dA2[s_][:, hh_ * 1024:(hh_ + 1) * 1024],
                            in_=delta_sb[:, d_ * TC:(d_ + 1) * TC], func=AF.Exp,
                            scale=A_sb[:, d_ * 64 + n_: d_ * 64 + n_ + 1])))(dblk, n, qs, hh), "cS")
                    if dp == 0:
                        P.wait("vector", sB, rB)
                        P.wait("vector", sC, rC)
                        if n == 0:
                            P.wait("vector", "dL", vL2)
                    P.wait("vector", "cV", v_pair_end)
                    v_u2[q] = P.add("vector", (lambda dp_, p_: (lambda q_: q_.tensor_tensor(
                        out=u2[:, :], in0=w_sb[:, dp_ * 2048:(dp_ + 1) * 2048],
                        in1=bufB2[p_][:, :], op=ALU.mult)))(dp, par_n), "cV")
                    P.wait("vector", "cS", v_dA[2 * q + 1])
                    P.wait("vector", "cS", v_ext_slot[qs])
                    P.wait("vector", "cV", v_u2[q])
                    for hh in range(2):
                        dblk = 2 * dp + hh
                        ini = 0.0 if cc == 0 else carry[:, dblk * 64 + n: dblk * 64 + n + 1]
                        v_tts[2 * q + hh] = P.add("vector", (lambda s_, hh_, i_: (lambda q_: q_.tensor_tensor_scan(
                            out=h2[s_][:, hh_ * 1024:(hh_ + 1) * 1024],
                            data0=dA2[s_][:, hh_ * 1024:(hh_ + 1) * 1024],
                            data1=u2[:, hh_ * 1024:(hh_ + 1) * 1024], initial=i_,
                            op0=ALU.mult, op1=ALU.add)))(qs, hh, ini), "cV")
                    v_tts2[q] = v_tts[2 * q + 1]
                    if q >= 1 and cc < NCC - 1:
                        qp = q - 1
                        P.wait("scalar", "cV", v_tts2[qp])
                        for hh in range(2):
                            v_ext_last = P.add("scalar", (lambda d_, n_, s_, hh_: (lambda q_: q_.copy(
                                out=carry[:, d_ * 64 + n_: d_ * 64 + n_ + 1],
                                in_=h2[s_][:, hh_ * 1024 + TC - 1: hh_ * 1024 + TC])))(
                                2 * (qp % 4) + hh, (qp // 4) % NST, qp % 2, hh), "cS")
                        v_ext_slot[qp % 2] = v_ext_last
                    P.wait("vector", "cV", v_tts2[q])
                    if n == 0:
                        v_ym2[q] = P.add("vector", (lambda dp_, s_, p_: (lambda q_: q_.tensor_tensor(
                            out=y_acc[:, dp_ * 2048:(dp_ + 1) * 2048],
                            in0=h2[s_][:, :], in1=bufC2[p_][:, :],
                            op=ALU.mult)))(dp, qs, par_n), "cV")
                        v_acm2[q] = None
                    else:
                        tr = tmp_reader[qs]
                        if tr is not None:
                            P.wait("vector", tr[0], tr[1])
                        v_ym2[q] = P.add("vector", (lambda s_, p_: (lambda q_: q_.tensor_tensor(
                            out=tmp2[s_][:, :], in0=h2[s_][:, :],
                            in1=bufC2[p_][:, :], op=ALU.mult)))(qs, par_n), "cV")
                        sG = "dG%d" % dp
                        P.wait("gpsimd", "cV", v_ym2[q])
                        if v_acm2.get(q - 4) is not None:
                            P.wait("gpsimd", sG, v_acm2[q - 4][1])
                        va = P.add("gpsimd", (lambda dp_, s_: (lambda q_: q_.dma_start(
                            out=y_acc[:, dp_ * 2048:(dp_ + 1) * 2048], in_=tmp2[s_][:, :],
                            accum_op=ALU.add)))(dp, qs), sG, 16)
                        v_acm2[q] = (sG, va)
                        tmp_reader[qs] = (sG, va)
                    if dp == 3:
                        v_uB_par[par_n] = v_u2[q]
                        v_ymC_par[par_n] = v_ym2[q]
                    v_pair_end = P.cnt["cV"]
                    q += 1
            if cc < NCC - 1:
                qp = q - 1
                P.wait("scalar", "cV", v_tts2[qp])
                for hh in range(2):
                    v_ext_last = P.add("scalar", (lambda d_, n_, s_, hh_: (lambda q_: q_.copy(
                        out=carry[:, d_ * 64 + n_: d_ * 64 + n_ + 1],
                        in_=h2[s_][:, hh_ * 1024 + TC - 1: hh_ * 1024 + TC])))(
                        2 * (qp % 4) + hh, (qp // 4) % NST, qp % 2, hh), "cS")
                v_ext_slot[qp % 2] = v_ext_last
            v_cast_prev_cc = 0
            for dp in range(4):
                pb = dp % 2
                last_q = (cc * NPAIR) + 63 * 4 + dp
                ac = v_acm2[last_q]
                P.wait("scalar", ac[0], ac[1])
                P.wait("scalar", "dSy%d" % pb, v_store_y[pb])
                vca = P.add("scalar", (lambda dp_, pb_: (lambda q_: q_.activation(
                    out=y_bf[pb_][:, :], in_=y_acc[:, dp_ * 2048:(dp_ + 1) * 2048], func=AF.Identity)))(dp, pb), "cS")
                v_cast_prev_cc = vca
                P.wait("sync", "cS", vca)
                v_store_y[pb] = P.add("sync", (lambda dp_, hb_, tl_, pb_: (lambda q_: q_.dma_start(
                    out=y_scr2[:, :].rearrange("(h d p) t -> p h d t", h=2, d=8)[
                        :, hb_:hb_ + 1, 2 * dp_:2 * dp_ + 2, tl_ * TC:(tl_ + 1) * TC],
                    in_=y_bf[pb_][:, :].rearrange("p (h r t) -> p h r t", h=1, r=2))))(
                    dp, cc // 2, cc % 2, pb), "dSy%d" % pb, 16)
            if cc == 1:
                P.wait("gpsimd", "dSy0", v_store_y[0])
                P.wait("gpsimd", "dSy1", v_store_y[1])
                P.add("gpsimd", lambda q_: q_.collective_compute(
                    "AllGather", ALU.bypass, replica_groups=[[0, 1], [2, 3], [4, 5], [6, 7]],
                    ins=[y_scr2[0:DS, :].opt()], outs=[ygx2[0:D, :].opt()]), "ccm")
        bar3 = P.bar()

        if debug:
            P.wait_bar("sync", bar3)
            P.add("sync", lambda q: q.dma_start(out=y_dbg[:, :], in_=y_scr2[:, :]), "dS", 16)
        P.wait_bar("gpsimd", bar3)
        v_cc = P.add("gpsimd", lambda q: q.collective_compute(
            "AllGather", ALU.bypass, replica_groups=[[0, 1], [2, 3], [4, 5], [6, 7]],
            ins=[y_scr2[DS:2 * DS, :].opt()], outs=[ygx2[D:2 * D, :].opt()]), "ccm")

        for e in ENGS[:4]:
            P.wait_bar(e, bar3)
        yrhs = AR1b[:, 16384:24576]
        xlr = AR2b[:, 16384:32768]
        wpair = [AR3b[:, 0:4096], AR3b[:, 4096:8192]]
        x2t = [AR1[:, k * 2048:(k + 1) * 2048] for k in range(4)]
        ys2 = AR2[:, 7680:8192]
        v_ffn3 = 0
        v_out_st = 0
        DynS = bass.DynSlice
        for tp in range(4):
            P.wait("sync", "cP", P.cnt["cP"] if tp else 0)
            P.wait("sync", "cV", v_ffn3)
            P.wait("sync", "dS", v_out_st)
            vx1 = P.add("sync", (lambda tp_: (lambda q: q.dma_start(
                out=AR1[:, 0:8192].rearrange("p (a d) -> p a d", a=4),
                in_=x1_scr[:, :].rearrange("(a p) d -> p a d", p=128)[:, tp_ * 4:(tp_ + 1) * 4, :])))(tp), "dL", 16)
            if tp == 0:
                P.wait("sync", "ccm", v_cc)
            vy1 = P.add("sync", (lambda tp_: (lambda q: q.dma_start(
                out=yrhs[:, 0:4096].rearrange("p (h c t) -> p h c t", h=1, c=8),
                in_=ygx2[:, :].rearrange("(h c p) t -> p h c t", h=2, c=16)[
                    :, DynS(cell["jj"], 1), DynS(cell["j8"], 8),
                    tp_ * 512:(tp_ + 1) * 512])))(tp), "dL", 16)
            v_x1 = [vy1] * 4
            vy3 = P.add("sync", (lambda tp_: (lambda q: q.dma_start(
                out=xlr[:, 0:8192].rearrange("p (c t) -> p c t", c=16),
                in_=xlnbT_scr[:, :].rearrange("(c p) t -> p c t", p=128)[:, :, tp_ * 512:(tp_ + 1) * 512])))(tp), "dRB0", 16)
            vy2 = P.add("sync", (lambda tp_: (lambda q: q.dma_start(
                out=yrhs[:, 4096:8192].rearrange("p (h c t) -> p h c t", h=1, c=8),
                in_=ygx2[:, :].rearrange("(h c p) t -> p h c t", h=2, c=16)[
                    :, DynS(cell["jj"], 1), DynS(cell["k8"], 8),
                    tp_ * 512:(tp_ + 1) * 512])))(tp), "dRC0", 16)
            P.wait("vector", "dL", vy1)
            P.wait("vector", "dRC0", vy2)
            P.wait("vector", "dRB0", vy3)
            skipb = [AR2b[:, 0:512], AR2b[:, 512:1024]]
            vv_sc = [0, 0]
            v_skip = 0
            for c in range(17):
                if c < 16:
                    vv_sc[c % 2] = P.add("vector", (lambda c_: (lambda q: q.tensor_scalar(
                        out=skipb[c_ % 2], in0=xlr[:, c_ * 512:(c_ + 1) * 512],
                        scalar1=dpar[:, c_:c_ + 1], scalar2=None, op0=ALU.mult)))(c), "cV")
                if c >= 1:
                    cm = c - 1
                    P.wait("vector", "cV", vv_sc[cm % 2])
                    v_skip = P.add("vector", (lambda c_: (lambda q: q.tensor_tensor(
                        out=yrhs[:, c_ * 512:(c_ + 1) * 512], in0=yrhs[:, c_ * 512:(c_ + 1) * 512],
                        in1=skipb[c_ % 2], op=ALU.add)))(cm), "cV")
            v_wl = {0: 0, 1: 0}
            v_add = 0
            v_oev = 0
            for oc in range(16):
                wp = oc % 2
                sw = "dX%d" % wp
                P.wait("sync", "cP", v_wl[wp])
                vwo = P.add("sync", (lambda oc_, wp_: (lambda q: q.dma_start(
                    out=wpair[wp_][:, 0:2048].rearrange("p (c o) -> p c o", c=16),
                    in_=wo_b_scr[:, oc_ * 128:(oc_ + 1) * 128].rearrange("(c p) o -> p c o", p=128))))(oc, wp), sw, 16)
                P.wait("tensor", sw, vwo)
                if v_add:
                    P.wait("tensor", "cV", v_add)
                if oc == 0:
                    P.wait("tensor", "cV", v_skip)
                P.wait("tensor", "dRC0", vy2)
                P.wait("tensor", "dL", vy1)
                for dc in range(16):
                    P.add("tensor", (lambda dc_, wp_: (lambda q: q.matmul(
                        BANK(0), wpair[wp_][:, dc_ * 128:(dc_ + 1) * 128],
                        yrhs[:, dc_ * 512:(dc_ + 1) * 512], start=(dc_ == 0), stop=(dc_ == 15))))(dc, wp), "cP")
                v_wl[wp] = P.cnt["cP"]
                P.wait("scalar", "cP", v_wl[wp])
                P.wait("scalar", "cV", v_add if oc else 0)
                v_oev = P.add("scalar", lambda q: q.copy(out=ys2[:, :], in_=BANK(0)), "cS")
                P.wait("tensor", "cS", v_oev)
                for tt in range(4):
                    P.add("tensor", (lambda tt_: (lambda q: q.transpose(
                        BANK(1)[:, tt_ * 128:(tt_ + 1) * 128], ys2[:, tt_ * 128:(tt_ + 1) * 128], idf[:, :])))(tt), "cP")
                vp = P.cnt["cP"]
                P.wait("vector", "cP", vp)
                for tt in range(4):
                    P.wait("vector", "dL", v_x1[tt])
                    v_add = P.add("vector", (lambda tt_, oc_: (lambda q: q.tensor_tensor(
                        out=x2t[tt_][:, oc_ * 128:(oc_ + 1) * 128], in0=x2t[tt_][:, oc_ * 128:(oc_ + 1) * 128],
                        in1=BANK(1)[:, tt_ * 128:(tt_ + 1) * 128], op=ALU.add)))(tt, oc), "cV")
            P.wait("sync", "cP", max(v_wl[0], v_wl[1]))
            vw5 = emit_ffn_weights_load()
            for e in ("scalar", "vector", "tensor"):
                P.wait(e, "dL", vw5)
            f5state = {"tile_free": [], "st_tile": 0}
            v_ffn3 = emit_ffn_sup(0, x2t,
                                  out_c[:, :].rearrange("(a p) d -> p a d", p=128)[:, tp * 4:tp * 4 + 4, :],
                                  f5state, tile_dma=None, v_tiles_ready=v_add)
            v_out_st = P.cnt["dS"]
        P.wait("sync", "dS", P.cnt["dS"])

        import os as _os
        _dbg_emit = bool(_os.environ.get("KRN_DBG_EMIT"))

        def emitter(name):
            def run(q):
                for idx, (kind, *rest) in enumerate(P.ops[name]):
                    if kind == "wait":
                        q.wait_ge(sems[rest[0]], rest[1])
                    else:
                        fn, inc, k = rest
                        try:
                            instr = fn(q)
                        except Exception:
                            if _dbg_emit:
                                import traceback as _tb
                                print(f"EMIT FAIL {name} op#{idx}")
                                _tb.print_exc()
                            raise
                        if inc is not None and instr is not None:
                            instr.then_inc(sems[inc], k)
            return run

        block.sync(emitter("sync"))
        block.scalar(emitter("scalar"))
        block.vector(emitter("vector"))
        block.tensor(emitter("tensor"))
        block.gpsimd(emitter("gpsimd"))
    return nc


def _prep_inputs(core, inputs):
    b, j = core // 2, core % 2
    pm = np.arange(D) if j == 0 else np.r_[1024:2048, 0:1024]
    g = lambda k: np.asarray(inputs[k])
    x = g("L0_attn_x")[b][:, pm]
    xpw = g("L0_attn_x_proj_w")[:, pm]
    dtw = g("L0_attn_dt_proj_w")[pm]
    dtbv = g("L0_attn_dt_proj_b")[pm]
    alog = g("L0_attn_A_log")[pm][:DS]
    dparf = g("L0_attn_D_param")[pm]
    nwf = g("L0_attn_norm_w")[pm]
    nbf = g("L0_attn_norm_b")[pm]
    wout = g("L0_attn_out_proj_w")[pm][:, pm]
    A_cat = g("L0_ffn_A").reshape(ER, D)[:, pm]
    B_cat = g("L0_ffn_B").transpose(1, 0, 2).reshape(D, ER)[pm]
    wr = g("L0_ffn_Wr")[:, pm]
    br = g("L0_ffn_br")
    C = np.ascontiguousarray
    return dict(
        x_in=C(x.astype(np.float32)),
        xpwT=C(xpw.T.astype(np.float32)),
        dtwT=C(dtw.T.astype(np.float32)),
        dtb_r=C(dtbv.reshape(16, 128).T.astype(np.float32)),
        alog_r=C(alog.reshape(8, 128, NST).transpose(1, 0, 2).reshape(128, 512).astype(np.float32)),
        dpar_r=C(dparf.reshape(16, 128).T.astype(np.float32)),
        nw_r=C(nwf.reshape(16, 128).T.astype(np.float32)),
        nb_r=C(nbf.reshape(16, 128).T.astype(np.float32)),
        woutT=C(wout.T.astype(np.float32)),
        ffnAT=C(A_cat.T.astype(np.float32)),
        ffnBT=C(B_cat.T.astype(np.float32)),
        wrT=C(wr.T.astype(np.float32)),
        br_in=C(br.reshape(1, 8).astype(np.float32)),
        blkid_in=C(np.repeat(np.eye(8, dtype=np.float32), 64, axis=1)),
        idf_in=np.eye(128, dtype=np.float32),
        idb_in=np.eye(128, dtype=np.float32).astype(ml_dtypes.bfloat16),
        cst_in=np.array([[EPS, -0.5, -1.0, 1.0]], np.float32),
        jv_in=np.array([[j]], np.int32),
        kv_in=np.array([[1 - j]], np.int32),
    )


def kernel(**inputs):
    nc = build_nc()
    in_maps = [_prep_inputs(c, inputs) for c in range(8)]
    res = run_bass_kernel_spmd(nc, in_maps, core_ids=list(range(8)))
    out = np.empty((4, TOK, D), np.float32)
    for c in range(8):
        b, j = c // 2, c % 2
        pm = np.arange(D) if j == 0 else np.r_[1024:2048, 0:1024]
        out[b, j * MYT:(j + 1) * MYT, :] = res.results[c]["out_c"][:, pm]
    return out
